# version 05 (final)
# speedup vs baseline: 1.2192x; 1.0979x over previous
import numpy as np
import ml_dtypes
import concourse.bass as bass
import concourse.mybir as mybir
from concourse.bass_utils import run_bass_kernel_spmd
from contextlib import ExitStack

F32 = mybir.dt.float32
BF16 = mybir.dt.bfloat16
AF = mybir.ActivationFunctionType
ALU = mybir.AluOpType
AX = mybir.AxisListType
BF16NP = ml_dtypes.bfloat16

NCORES = 8
FULL_SYNC = True
B, T, C = 4, 4096, 1024
TLOC = T // 2
NTILE = 512
ALL = None


class Buf:
    __slots__ = ("h", "name", "writes", "readers")

    def __init__(self, h, name):
        self.h = h
        self.name = name
        self.writes = {}
        self.readers = []

    def ap(self):
        return self.h.ap() if hasattr(self.h, "ap") else self.h

    def __getitem__(self, sl):
        return self.h[sl]


def _conf(a, b):
    return a is ALL or b is ALL or a == b


class Op:
    __slots__ = ("eng", "fn", "reads", "writes", "is_dma", "seq", "deps",
                 "need_inc", "cum", "tag")

    def __init__(self, eng, fn, reads, writes, is_dma, seq, tag):
        self.eng, self.fn = eng, fn
        self.reads, self.writes = reads, writes
        self.is_dma, self.seq, self.tag = is_dma, seq, tag
        self.deps = []
        self.need_inc = False
        self.cum = None


class Em:
    ENGS = ("pe", "act", "dve", "pool", "sp")

    def __init__(self, nc, ctx):
        self.nc = nc
        self.ctx = ctx
        self.ops = []
        self._name_i = 0

    def _nm(self, p):
        self._name_i += 1
        return f"{p}_{self._name_i}"

    def sb(self, name, shape, dtype=F32):
        h = self.ctx.enter_context(self.nc.sbuf_tensor(self._nm(name), shape, dtype))
        return Buf(h, name)

    def psum(self, name, shape=(128, 512), dtype=F32):
        h = self.ctx.enter_context(self.nc.psum_tensor(self._nm(name), list(shape), dtype))
        return Buf(h, name)

    def dram(self, name, shape, dtype=F32, kind="Internal"):
        h = self.nc.dram_tensor(name, list(shape), dtype, kind=kind)
        return Buf(h, name)

    def op(self, eng, fn, reads=(), writes=(), is_dma=False, tag=""):
        def norm(lst):
            return [(it, ALL) if isinstance(it, Buf) else it for it in lst]
        o = Op(eng, fn, norm(reads), norm(writes), is_dma, len(self.ops), tag)
        self.ops.append(o)
        deps = {}
        for buf, key in o.reads:
            for wkey, wop in buf.writes.items():
                if _conf(key, wkey):
                    deps[wop.seq] = wop
        for buf, key in o.writes:
            for wkey, wop in buf.writes.items():
                if _conf(key, wkey):
                    deps[wop.seq] = wop
            for rkey, rop in buf.readers:
                if rop is not o and _conf(key, rkey):
                    deps[rop.seq] = rop
        o.deps = list(deps.values())
        for buf, key in o.reads:
            buf.readers.append((key, o))
        for buf, key in o.writes:
            if key is ALL:
                buf.writes = {ALL: o}
                buf.readers = []
            else:
                buf.writes[key] = o
                buf.readers = [(k, r) for (k, r) in buf.readers
                               if r is o or k != key]
        return o

    def dma(self, eng, out_ap, in_ap, reads, writes, tag="dma", **kw):
        def fn(e, out_ap=out_ap, in_ap=in_ap, kw=kw):
            return e.dma_start(out=out_ap, in_=in_ap, **kw)
        return self.op(eng, fn, reads, writes, is_dma=True, tag=tag)

    def barrier(self, bufs, eng="sp"):
        def fn(e):
            return None
        self.op(eng, fn, reads=[(b, ALL) for b in bufs], writes=(), tag="barrier")

    def emit(self, full_sync=False):
        for o in self.ops:
            real = []
            for d in o.deps:
                if (not full_sync) and d.eng == o.eng and not d.is_dma and not o.is_dma:
                    continue
                if d.eng == o.eng and not d.is_dma and o.is_dma:
                    pass
                real.append(d)
                d.need_inc = True
            o.deps = real
        NLANES = 4
        counts = {}
        for o in self.ops:
            if o.is_dma:
                o.need_inc = True
                qk = (o.eng, True)
                n = counts.get(qk, 0)
                counts[qk] = n + 1
                o.cum = (n % NLANES, n // NLANES + 1)
            elif o.need_inc:
                k = (o.eng, False)
                counts[k] = counts.get(k, 0) + 1
                o.cum = counts[k]
        sems = {}
        for k, n in counts.items():
            if k[1]:
                for ln in range(min(NLANES, n)):
                    sems[(k[0], True, ln)] = self.ctx.enter_context(
                        self.nc.semaphore(f"s_{k[0]}_d{ln}"))
            else:
                sems[k] = self.ctx.enter_context(self.nc.semaphore(f"s_{k[0]}_c"))
        block = self.ctx.enter_context(self.nc.Block())
        by_eng = {e: [o for o in self.ops if o.eng == e] for e in self.ENGS}

        def mk_body(eng_name):
            def body(e):
                water = {}
                for o in by_eng[eng_name]:
                    seen = {}
                    for d in o.deps:
                        if d.is_dma:
                            k = (d.eng, True, d.cum[0])
                            v = d.cum[1] * 16
                        else:
                            k = (d.eng, False)
                            v = d.cum
                        if seen.get(k, 0) < v:
                            seen[k] = v
                    for k, v in seen.items():
                        e.wait_ge(sems[k], v)
                        water[k] = max(water.get(k, 0), v)
                    if o.is_dma:
                        k = (o.eng, True, o.cum[0])
                        v = (o.cum[1] - 1) * 16
                        if v > 0 and water.get(k, 0) < v:
                            e.wait_ge(sems[k], v)
                            water[k] = v
                    inst = o.fn(e)
                    if o.need_inc:
                        assert inst is not None, f"op {o.tag} needs inc, no inst"
                        if o.is_dma:
                            inst.then_inc(sems[(o.eng, True, o.cum[0])], 16)
                        else:
                            inst.then_inc(sems[(o.eng, False)], 1)
            return body

        secname = {"pe": "tensor", "act": "scalar", "dve": "vector",
                   "pool": "gpsimd", "sp": "sync"}
        for e in self.ENGS:
            if by_eng[e]:
                getattr(block, secname[e])(mk_body(e))


def cdiv(a, b):
    return -(-a // b)



class StageCtx:

    def __init__(self, em):
        self.em = em
        self.static = {}
        self._consts = {}

    def cin(self, name, arr, dtype=None):
        arr = np.asarray(arr)
        if name in self._consts:
            return self._consts[name]
        dt = dtype or (BF16 if arr.dtype == BF16NP else F32)
        d = self.em.dram(name, list(arr.shape), dt, kind="ExternalInput")
        self.static[name] = np.ascontiguousarray(arr)
        self._consts[name] = d
        return d

    def const_tile(self, name, arr, dtype=None, eng="sp"):
        key = "t_" + name
        if key in self._consts:
            return self._consts[key]
        arr = np.asarray(arr)
        if arr.ndim == 1:
            arr = arr[None, :]
        d = self.cin(name, arr, dtype)
        dt = dtype or (BF16 if arr.dtype == BF16NP else F32)
        t = self.em.sb("t_" + name, list(arr.shape), dt)
        if dt == BF16 and arr.dtype != BF16NP:
            eng = "pool"
        self.em.dma(eng, t.ap(), d.ap(), reads=[d], writes=[t], tag=f"ld_{name}")
        self._consts[key] = t
        return t

    def weight_bf16(self, name, w_np, eng="pool"):
        w_np = np.asarray(w_np, np.float32)
        k, m = w_np.shape
        nk = cdiv(k, 128)
        d = self.cin(name, w_np, F32)
        t = self.em.sb("wt_" + name, [128, nk * m], BF16)
        for kc in range(nk):
            k0, k1 = kc * 128, min(k, (kc + 1) * 128)
            self.em.dma(eng, t[0:k1 - k0, kc * m:(kc + 1) * m], d[k0:k1, :],
                        reads=[d], writes=[(t, kc)], tag=f"ldw_{name}_{kc}")
        return WTile(t, k, m)

    def ones1(self):
        return self.const_tile("ones1_f32", np.ones((1, 128), np.float32))

    def ones_row(self, n=NTILE):
        return self.const_tile("onesrow_f32", np.ones((1, n), np.float32))

    def ones_col_bf(self):
        return self.const_tile("onescol_bf", np.ones((128, 1), BF16NP), BF16)


class WTile:

    def __init__(self, buf, k, m):
        self.buf = buf
        self.k = k
        self.m = m

    def sl(self, kc, m0, m1):
        k0, k1 = kc * 128, min(self.k, (kc + 1) * 128)
        return self.buf[0:k1 - k0, kc * self.m + m0:kc * self.m + m1]


def emit_proj(em, wt, in_tiles, n_in, n_out, ntok, psums, evac,
              tag="proj", in_key=None):
    nk = cdiv(n_in, 128)
    nm = cdiv(n_out, 128)
    nt = cdiv(ntok, NTILE)
    for mc in range(nm):
        m0, m1 = mc * 128, min(n_out, (mc + 1) * 128)
        for tc in range(nt):
            t0, t1 = tc * NTILE, min(ntok, (tc + 1) * NTILE)
            ps = psums[(mc * nt + tc) % len(psums)]

            def fn(e, m0=m0, m1=m1, t0=t0, t1=t1, ps=ps):
                last = None
                for kc in range(nk):
                    k0, k1 = kc * 128, min(n_in, (kc + 1) * 128)
                    last = e.matmul(out=ps[0:m1 - m0, 0:t1 - t0],
                                    lhsT=wt.sl(kc, m0, m1),
                                    rhs=in_tiles[kc][0:k1 - k0, t0:t1],
                                    start=(kc == 0), stop=(kc == nk - 1))
                return last

            reads = [(wt.buf, ALL)] + [(in_tiles[kc], in_key) for kc in range(nk)]
            em.op("pe", fn, reads=reads, writes=[(ps, ALL)],
                  tag=f"{tag}_mm_{mc}_{tc}")
            evac(mc, m0, m1, t0, t1, ps)


def evac_copy(em, out_tiles, dtype_note="", eng="dve", bias_col=None,
              act=None, scale=1.0, tag="evac"):
    def evac(mc, m0, m1, t0, t1, ps):
        dst = out_tiles[mc]
        if act is not None:
            def fn(e, dst=dst, mc=mc, m0=m0, m1=m1, t0=t0, t1=t1, ps=ps):
                bias = bias_col[0:m1 - m0, mc:mc + 1] if bias_col is not None else 0.0
                return e.activation(out=dst[0:m1 - m0, t0:t1],
                                    in_=ps[0:m1 - m0, 0:t1 - t0],
                                    func=act, bias=bias, scale=scale)
            reads = [(ps, ALL)] + ([(bias_col, ALL)] if bias_col is not None else [])
            em.op("act", fn, reads=reads, writes=[(dst, ("t", t0))],
                  tag=f"{tag}_{mc}_{t0}")
        elif bias_col is not None:
            def fn(e, dst=dst, mc=mc, m0=m0, m1=m1, t0=t0, t1=t1, ps=ps):
                return e.tensor_scalar(out=dst[0:m1 - m0, t0:t1],
                                       in0=ps[0:m1 - m0, 0:t1 - t0],
                                       scalar1=bias_col[0:m1 - m0, mc:mc + 1],
                                       scalar2=None, op0=ALU.add)
            em.op(eng, fn, reads=[(ps, ALL), (bias_col, ALL)],
                  writes=[(dst, ("t", t0))], tag=f"{tag}_{mc}_{t0}")
        else:
            def fn(e, dst=dst, m0=m0, m1=m1, t0=t0, t1=t1, ps=ps):
                return e.tensor_copy(out=dst[0:m1 - m0, t0:t1],
                                     in_=ps[0:m1 - m0, 0:t1 - t0])
            em.op(eng, fn, reads=[(ps, ALL)], writes=[(dst, ("t", t0))],
                  tag=f"{tag}_{mc}_{t0}")
    return evac


class NormScratch:

    def __init__(self, em, ntok, rms_only=False, ps_pool=None):
        self.sq = em.sb("ns_sq", [128, NTILE], BF16)
        self.sum = None if rms_only else em.sb("ns_sum", [1, ntok])
        self.sqs = em.sb("ns_sqs", [1, ntok])
        self.rowa = None if rms_only else em.sb("ns_rowa", [1, ntok])
        self.rowb = em.sb("ns_rowb", [1, ntok])
        self.R_bc = em.sb("ns_R", [128, ntok])
        self.B_bc = None if rms_only else em.sb("ns_B", [128, ntok])
        if ps_pool is not None:
            self.ps_s, self.ps_q, self.ps_bc = ps_pool[0], ps_pool[1], ps_pool[2]
        else:
            self.ps_s = em.psum("ns_ps_s")
            self.ps_q = em.psum("ns_ps_q")
            self.ps_bc = em.psum("ns_ps_bc")


def emit_norm(em, sc, ns, in_tiles, out_tiles, nch, ntok, rms, eps=1e-5,
              gamma_col=None, tag="ln", apply_engs=("dve", "pool")):
    nt = cdiv(ntok, NTILE)
    ones_col = sc.ones_col_bf()
    ones1 = sc.ones1()
    sq, sum_sb, sqs_sb = ns.sq, ns.sum, ns.sqs
    mean, var = ns.rowa, ns.rowb
    ps_s, ps_q, ps_bc = ns.ps_s, ns.ps_q, ns.ps_bc
    R_bc, B_bc = ns.R_bc, ns.B_bc

    for tc in range(nt):
        t0, t1 = tc * NTILE, min(ntok, (tc + 1) * NTILE)
        if not rms:
            def fn_s(e, t0=t0, t1=t1):
                last = None
                for kc in range(nch):
                    last = e.matmul(out=ps_s[0:1, 0:t1 - t0], lhsT=ones_col[:, 0:1],
                                    rhs=in_tiles[kc][:, t0:t1],
                                    start=(kc == 0), stop=(kc == nch - 1))
                return last
            em.op("pe", fn_s, reads=[ones_col] + [(tl, ALL) for tl in in_tiles],
                  writes=[ps_s], tag=f"{tag}_sum_{tc}")
            def fn_es(e, t0=t0, t1=t1):
                return e.tensor_copy(out=sum_sb[0:1, t0:t1], in_=ps_s[0:1, 0:t1 - t0])
            em.op("dve", fn_es, reads=[ps_s], writes=[(sum_sb, tc)], tag=f"{tag}_esum_{tc}")
        for kc in range(nch):
            def fn_sqr(e, kc=kc, t0=t0, t1=t1):
                return e.activation(out=sq[:, 0:t1 - t0], in_=in_tiles[kc][:, t0:t1],
                                    func=AF.Square)
            em.op("act", fn_sqr, reads=[(in_tiles[kc], ALL)], writes=[sq],
                  tag=f"{tag}_sq_{kc}_{tc}")
            def fn_q(e, kc=kc, t0=t0, t1=t1):
                return e.matmul(out=ps_q[0:1, 0:t1 - t0], lhsT=ones_col[:, 0:1],
                                rhs=sq[:, 0:t1 - t0],
                                start=(kc == 0), stop=(kc == nch - 1))
            em.op("pe", fn_q, reads=[ones_col, sq], writes=[(ps_q, ALL)],
                  tag=f"{tag}_sqs_{kc}_{tc}")
        def fn_eq(e, t0=t0, t1=t1):
            return e.tensor_copy(out=sqs_sb[0:1, t0:t1], in_=ps_q[0:1, 0:t1 - t0])
        em.op("dve", fn_eq, reads=[ps_q], writes=[(sqs_sb, tc)], tag=f"{tag}_esqs_{tc}")

    n = float(nch * 128)
    if rms:
        em.op("dve", lambda e: e.tensor_scalar(out=var[0:1, :], in0=sqs_sb[0:1, :],
                                               scalar1=1.0 / n, scalar2=eps,
                                               op0=ALU.mult, op1=ALU.add),
              reads=[sqs_sb], writes=[var], tag=tag + "_var")
    else:
        em.op("dve", lambda e: e.tensor_scalar_mul(out=mean[0:1, :], in0=sum_sb[0:1, :],
                                                   scalar1=1.0 / n),
              reads=[sum_sb], writes=[mean], tag=tag + "_mean")
        em.op("dve", lambda e: e.tensor_mul(out=sum_sb[0:1, :], in0=mean[0:1, :],
                                            in1=mean[0:1, :]),
              reads=[mean], writes=[sum_sb], tag=tag + "_m2")
        em.op("dve", lambda e: e.tensor_scalar(out=var[0:1, :], in0=sqs_sb[0:1, :],
                                               scalar1=1.0 / n, scalar2=eps,
                                               op0=ALU.mult, op1=ALU.add),
              reads=[sqs_sb], writes=[var], tag=tag + "_ex2")
        em.op("dve", lambda e: e.tensor_sub(out=var[0:1, :], in0=var[0:1, :],
                                            in1=sum_sb[0:1, :]),
              reads=[sum_sb, (var, "a")], writes=[var], tag=tag + "_varsub")
    em.op("dve", lambda e: e.reciprocal(out=sqs_sb[0:1, :], in_=var[0:1, :]),
          reads=[var], writes=[sqs_sb], tag=tag + "_recip")
    rstd = var
    em.op("act", lambda e: e.activation(out=rstd[0:1, :], in_=sqs_sb[0:1, :], func=AF.Sqrt),
          reads=[sqs_sb], writes=[rstd], tag=tag + "_sqrt")
    if not rms:
        em.op("dve", lambda e: e.scalar_tensor_tensor(out=mean[0:1, :], in0=mean[0:1, :],
                                                      scalar=-1.0, in1=rstd[0:1, :],
                                                      op0=ALU.mult, op1=ALU.mult),
              reads=[(mean, "x"), rstd], writes=[mean], tag=tag + "_negmr")
    negmr = mean
    for tc in range(nt):
        t0, t1 = tc * NTILE, min(ntok, (tc + 1) * NTILE)
        def fn_bc(e, t0=t0, t1=t1):
            return e.matmul(out=ps_bc[:, 0:t1 - t0], lhsT=ones1[0:1, :],
                            rhs=rstd[0:1, t0:t1], start=True, stop=True)
        em.op("pe", fn_bc, reads=[ones1, rstd], writes=[ps_bc], tag=f"{tag}_bcR_{tc}")
        def fn_ebc(e, t0=t0, t1=t1):
            return e.tensor_copy(out=R_bc[:, t0:t1], in_=ps_bc[:, 0:t1 - t0])
        em.op("dve", fn_ebc, reads=[ps_bc], writes=[(R_bc, tc)], tag=f"{tag}_ebcR_{tc}")
        if not rms:
            def fn_bc2(e, t0=t0, t1=t1):
                return e.matmul(out=ps_bc[:, 0:t1 - t0], lhsT=ones1[0:1, :],
                                rhs=negmr[0:1, t0:t1], start=True, stop=True)
            em.op("pe", fn_bc2, reads=[ones1, negmr], writes=[ps_bc], tag=f"{tag}_bcB_{tc}")
            def fn_ebc2(e, t0=t0, t1=t1):
                return e.tensor_copy(out=B_bc[:, t0:t1], in_=ps_bc[:, 0:t1 - t0])
            em.op("dve", fn_ebc2, reads=[ps_bc], writes=[(B_bc, tc)], tag=f"{tag}_ebcB_{tc}")
    for kc in range(nch):
        eng = apply_engs[kc % len(apply_engs)]
        def fn_ap(e, kc=kc):
            return e.tensor_mul(out=out_tiles[kc][:, :], in0=in_tiles[kc][:, :],
                                in1=R_bc[:, :])
        em.op(eng, fn_ap, reads=[(in_tiles[kc], ALL), R_bc],
              writes=[(out_tiles[kc], ALL)], tag=f"{tag}_ap_{kc}")
        if not rms:
            def fn_ap2(e, kc=kc):
                return e.tensor_add(out=out_tiles[kc][:, :], in0=out_tiles[kc][:, :],
                                    in1=B_bc[:, :])
            em.op(eng, fn_ap2, reads=[(out_tiles[kc], "x"), B_bc],
                  writes=[(out_tiles[kc], ALL)], tag=f"{tag}_ap2_{kc}")
        if gamma_col is not None:
            def fn_ap3(e, kc=kc):
                return e.tensor_scalar(out=out_tiles[kc][:, :], in0=out_tiles[kc][:, :],
                                       scalar1=gamma_col[:, kc:kc + 1], scalar2=None,
                                       op0=ALU.mult)
            em.op(eng, fn_ap3, reads=[(out_tiles[kc], "x"), gamma_col],
                  writes=[(out_tiles[kc], ALL)], tag=f"{tag}_ap3_{kc}")
    return R_bc, B_bc



def load_fm_tiles(em, dram_buf, nch, ntok, dtype=BF16, eng="pool", tag="ldfm",
                  row0=0):
    tiles = []
    for kc in range(nch):
        t = em.sb(f"{tag}_{kc}", [128, ntok], dtype)
        em.dma(eng, t[:, :], dram_buf[row0 + kc * 128: row0 + (kc + 1) * 128, :],
               reads=[dram_buf], writes=[t], tag=f"{tag}_{kc}")
        tiles.append(t)
    return tiles


def proj_to_dram(em, sc, wt, in_tiles, n_in, n_out, ntok, out_dram, psums,
                 tag, bias_col=None, act=None, out_dtype=BF16, eng_evac="dve",
                 dma_eng="sp", stag=None, nstag=2):
    if stag is None:
        stag = [em.sb(f"{tag}_stg{i}", [128, ntok], out_dtype) for i in range(nstag)]
    nt = cdiv(ntok, NTILE)
    done_tiles = {}

    def evac(mc, m0, m1, t0, t1, ps):
        dst = stag[mc % len(stag)]
        if act is not None:
            def fn(e, dst=dst, mc=mc, m0=m0, m1=m1, t0=t0, t1=t1, ps=ps):
                bias = bias_col[0:m1 - m0, mc:mc + 1] if bias_col is not None else 0.0
                return e.activation(out=dst[0:m1 - m0, t0:t1],
                                    in_=ps[0:m1 - m0, 0:t1 - t0],
                                    func=act, bias=bias, scale=1.0)
            reads = [(ps, ALL)] + ([(bias_col, ALL)] if bias_col is not None else [])
            em.op("act", fn, reads=reads, writes=[(dst, t0)], tag=f"{tag}_ev_{mc}_{t0}")
        elif bias_col is not None:
            def fn(e, dst=dst, mc=mc, m0=m0, m1=m1, t0=t0, t1=t1, ps=ps):
                return e.tensor_scalar(out=dst[0:m1 - m0, t0:t1],
                                       in0=ps[0:m1 - m0, 0:t1 - t0],
                                       scalar1=bias_col[0:m1 - m0, mc:mc + 1],
                                       scalar2=None, op0=ALU.add)
            em.op(eng_evac, fn, reads=[(ps, ALL), (bias_col, ALL)],
                  writes=[(dst, t0)], tag=f"{tag}_ev_{mc}_{t0}")
        else:
            def fn(e, dst=dst, m0=m0, m1=m1, t0=t0, t1=t1, ps=ps):
                return e.tensor_copy(out=dst[0:m1 - m0, t0:t1],
                                     in_=ps[0:m1 - m0, 0:t1 - t0])
            em.op(eng_evac, fn, reads=[(ps, ALL)], writes=[(dst, t0)],
                  tag=f"{tag}_ev_{mc}_{t0}")
        if t1 >= ntok:
            m0c, m1c = mc * 128, min(n_out, (mc + 1) * 128)
            em.dma(dma_eng, out_dram[m0c:m1c, :], dst[0:m1c - m0c, :],
                   reads=[(dst, ALL)], writes=[(out_dram, mc)],
                   tag=f"{tag}_st_{mc}")

    emit_proj(em, wt, in_tiles, n_in, n_out, ntok, psums, evac, tag=tag)



class Stage:
    def __init__(self, name, build, prep, post):
        self.name = name
        self.build = build
        self.prep = prep
        self.post = post

    def run(self, H, trace=False):
        nc = bass.Bass("TRN2", target_bir_lowering=False)
        with ExitStack() as ctx:
            em = Em(nc, ctx)
            sc = StageCtx(em)
            self.build(em, sc, H)
            em.emit(full_sync=FULL_SYNC)
        in_maps = []
        for c in range(NCORES):
            m = dict(sc.static)
            m.update(self.prep(c, H))
            in_maps.append(m)
        res = run_bass_kernel_spmd(nc, in_maps, core_ids=list(range(NCORES)),
                                   trace=trace)
        self.post(res.results, H)
        return res


def fm_slice(x_btc, c):
    b, h = c // 2, c % 2
    return np.ascontiguousarray(x_btc[b, h * TLOC:(h + 1) * TLOC, :].T)



def build_l0a(em, sc, H):
    W = H["weights"]
    x_in = em.dram("x_in", [C, TLOC], F32, kind="ExternalInput")
    xn_out = em.dram("xn_out", [C, TLOC], BF16, kind="ExternalOutput")
    z_out = em.dram("z_out", [2048, TLOC], BF16, kind="ExternalOutput")
    xbc_out = em.dram("xbc_out", [2176, TLOC], BF16, kind="ExternalOutput")
    dtr_out = em.dram("dtr_out", [32, TLOC], F32, kind="ExternalOutput")

    x_bf = load_fm_tiles(em, x_in, 8, TLOC, BF16, tag="xbf")
    xn = [em.sb(f"xn_{i}", [128, TLOC], BF16) for i in range(8)]
    ns = NormScratch(em, TLOC)
    emit_norm(em, sc, ns, x_bf, xn, 8, TLOC, rms=False, tag="ln0")
    for kc in range(8):
        em.dma("sp", xn_out[kc * 128:(kc + 1) * 128, :], xn[kc][:, :],
               reads=[(xn[kc], ALL)], writes=[(xn_out, kc)], tag=f"st_xn_{kc}")
    r1 = x_bf
    emit_norm(em, sc, ns, xn, r1, 8, TLOC, rms=True, tag="rms1")

    ipw = np.asarray(W["L0_attn_in_proj_w"], np.float32)
    wz = sc.weight_bf16("W_z", ipw[:, :2048])
    wxbc = sc.weight_bf16("W_xbc", ipw[:, 2048:4224])
    wdt = sc.weight_bf16("W_dt", ipw[:, 4224:])
    psums = [em.psum(f"pp{i}") for i in range(4)]
    stag = [em.sb(f"stg{i}", [128, TLOC], BF16) for i in range(2)]
    proj_to_dram(em, sc, wz, r1, 1024, 2048, TLOC, z_out, psums, "pz", stag=stag)
    proj_to_dram(em, sc, wxbc, r1, 1024, 2176, TLOC, xbc_out, psums, "pxbc", stag=stag)
    proj_to_dram(em, sc, wdt, r1, 1024, 32, TLOC, dtr_out, psums, "pdt",
                 out_dtype=F32, nstag=1)
    em.barrier([xn_out, z_out, xbc_out, dtr_out])


def prep_l0a(c, H):
    return {"x_in": fm_slice(H["x"], c)}


def post_l0a(results, H):
    H["xn0"] = [r["xn_out"] for r in results]
    H["z0"] = [r["z_out"] for r in results]
    H["xbc0"] = [r["xbc_out"] for r in results]
    H["dtr0"] = [r["dtr_out"] for r in results]


STAGE_L0A = Stage("l0a", build_l0a, prep_l0a, post_l0a)



L2A_PH = 9
QCH = 128
NCH = TLOC // QCH
L0B_PH = 9
L0B_XSEVAC = True
BIGNEG = -1.0e6


def build_l0b(em, sc, H):
    W = H["weights"]
    xbc_in = em.dram("xbc_in", [2176, 3 + TLOC], BF16, kind="ExternalInput")
    dtr_in = em.dram("dtr_in", [32, TLOC], F32, kind="ExternalInput")
    y_out = em.dram("y_out", [TLOC, 2048], BF16, kind="ExternalOutput")
    s_out = em.dram("s_out", [64, 32 * 64], F32, kind="ExternalOutput")
    eg_out = em.dram("eg_out", [32, TLOC], F32, kind="ExternalOutput")
    c_out = em.dram("c_out", [64, TLOC], BF16, kind="ExternalOutput")

    cw = np.asarray(W["L0_attn_conv_w"], np.float32)
    convw = np.zeros((128, 17 * 4), np.float32)
    convb = np.zeros((128, 17), np.float32)
    cb = np.asarray(W["L0_attn_conv_b"], np.float32)
    for kc in range(17):
        convw[:, kc * 4:(kc + 1) * 4] = cw[kc * 128:(kc + 1) * 128, :]
        convb[:, kc] = cb[kc * 128:(kc + 1) * 128]
    convw_t = sc.const_tile("convw", convw)
    convb_t = sc.const_tile("convb", convb)
    dtb_t = sc.const_tile("dtbias", np.asarray(W["L0_attn_dt_bias"], np.float32)[:, None])
    alog_t = sc.const_tile("alog", np.asarray(W["L0_attn_A_log"], np.float32)[:, None])
    d_t = sc.const_tile("dvec", np.asarray(W["L0_attn_D"], np.float32)[None, :])
    uincl = np.triu(np.ones((128, 128), np.float32))
    uincl_f = sc.const_tile("uincl_f", uincl)
    uincl_b = sc.const_tile("uincl_b", uincl.astype(BF16NP), BF16)
    ones128_b = sc.const_tile("ones128b", np.ones((128, 128), BF16NP), BF16)
    mshift = np.zeros((128, 128), np.float32)
    for t_ in range(127):
        mshift[t_ + 1, t_] = BIGNEG
    mshift_b = sc.const_tile("mshift", mshift.astype(BF16NP), BF16)
    ident_f = sc.const_tile("ident_f", np.eye(128, dtype=np.float32))
    ident_b = sc.const_tile("ident_b", np.eye(128).astype(BF16NP), BF16)
    ones1 = sc.ones1()

    xbc = load_fm_tiles(em, xbc_in, 17, 3 + TLOC, BF16, tag="xbc")
    dtr = em.sb("dtr", [32, TLOC])
    em.dma("sp", dtr[:, :], dtr_in.ap(), reads=[dtr_in], writes=[dtr], tag="ld_dtr")

    acc = [em.sb("cacc0", [128, TLOC])]
    for kc in range(17):
        a = acc[0]
        eng = "dve"
        def fn0(e, kc=kc, a=a):
            return e.tensor_scalar(out=a[:, :], in0=xbc[kc][:, 0:TLOC],
                                   scalar1=convw_t[:, kc * 4:kc * 4 + 1],
                                   scalar2=None, op0=ALU.mult)
        em.op(eng, fn0, reads=[(xbc[kc], ALL), convw_t], writes=[a], tag=f"cv0_{kc}")
        for k in range(1, 4):
            def fnk(e, kc=kc, k=k, a=a):
                return e.scalar_tensor_tensor(out=a[:, :], in0=xbc[kc][:, k:k + TLOC],
                                              scalar=convw_t[:, kc * 4 + k:kc * 4 + k + 1],
                                              in1=a[:, :], op0=ALU.mult, op1=ALU.add)
            em.op(eng, fnk, reads=[(xbc[kc], ALL), convw_t, (a, "x")], writes=[a],
                  tag=f"cv{k}_{kc}")
        def fns(e, kc=kc, a=a):
            return e.activation(out=xbc[kc][:, 3:3 + TLOC], in_=a[:, :], func=AF.Silu,
                                bias=convb_t[:, kc:kc + 1], scale=1.0)
        em.op("act", fns, reads=[a, convb_t], writes=[(xbc[kc], ALL)], tag=f"cvs_{kc}")

    def xc(kc):
        return xbc[kc]

    csb = em.sb("csb", [64, TLOC], BF16)
    em.op("dve", lambda e: e.tensor_copy(out=csb[:, :], in_=xbc[16][64:128, 3:3 + TLOC]),
          reads=[(xbc[16], ALL)], writes=[csb], tag="csb_cp")
    em.dma("sp", c_out.ap(), csb[:, :], reads=[csb], writes=[c_out], tag="st_c")

    dt = em.sb("dt", [32, TLOC])
    la = em.sb("la", [32, TLOC])
    ell = em.sb("ell", [32, TLOC])
    eg = em.sb("eg", [32, TLOC])
    dec = em.sb("dec", [32, TLOC])
    ee = em.sb("ee", [32, TLOC])
    row32 = em.sb("row32", [32, TLOC])
    acol = em.sb("acol", [32, 1])
    loff = em.sb("loff", [32, 1])
    ones32 = em.sb("ones32", [32, QCH])
    em.op("dve", lambda e: e.memset(ones32[:, :], 1.0), writes=[ones32], tag="ones32")
    em.op("dve", lambda e: e.memset(eg[:, :], 0.0), writes=[(eg, "init")], tag="eg0")
    em.op("dve", lambda e: e.memset(loff[:, :], 0.0), writes=[loff], tag="loff0")
    em.op("act", lambda e: e.activation(out=acol[:, :], in_=alog_t[:, :], func=AF.Exp),
          reads=[alog_t], writes=[acol], tag="acol_exp")
    em.op("dve", lambda e: e.tensor_scalar_mul(out=acol[:, :], in0=acol[:, :], scalar1=-1.0),
          reads=[(acol, "x")], writes=[acol], tag="acol_neg")
    em.op("dve", lambda e: e.tensor_scalar(out=row32[:, :], in0=dtr[:, :],
                                           scalar1=dtb_t[:, 0:1], scalar2=None, op0=ALU.add),
          reads=[dtr, dtb_t], writes=[row32], tag="dt_u")
    em.op("dve", lambda e: e.tensor_scalar_max(out=dt[:, :], in0=row32[:, :], scalar1=0.0),
          reads=[row32], writes=[dt], tag="dt_relu")
    em.op("act", lambda e: e.activation(out=la[:, :], in_=row32[:, :], func=AF.Abs),
          reads=[row32], writes=[la], tag="dt_abs")
    em.op("act", lambda e: e.activation(out=la[:, :], in_=la[:, :], func=AF.Exp, scale=-1.0),
          reads=[(la, "x")], writes=[la], tag="dt_exp")
    em.op("act", lambda e: e.activation(out=la[:, :], in_=la[:, :], func=AF.Ln, bias=1.0),
          reads=[(la, "x")], writes=[la], tag="dt_ln")
    em.op("dve", lambda e: e.tensor_add(out=dt[:, :], in0=dt[:, :], in1=la[:, :]),
          reads=[(dt, "x"), la], writes=[dt], tag="dt_add")
    em.op("dve", lambda e: e.tensor_scalar(out=la[:, :], in0=dt[:, :], scalar1=acol[:, 0:1],
                                           scalar2=None, op0=ALU.mult),
          reads=[dt, acol], writes=[la], tag="la_mul")
    for c_ in range(NCH):
        t0, t1 = c_ * QCH, (c_ + 1) * QCH
        def f_scan(e, t0=t0, t1=t1):
            return e.tensor_tensor_scan(out=ell[:, t0:t1], data0=ones32[:, :],
                                        data1=la[:, t0:t1], initial=0.0,
                                        op0=ALU.mult, op1=ALU.add)
        em.op("dve", f_scan, reads=[ones32, la], writes=[(ell, c_)], tag=f"scan_{c_}")
        def f_ee(e, t0=t0, t1=t1):
            return e.activation(out=ee[:, t0:t1], in_=ell[:, t0:t1], func=AF.Exp)
        em.op("act", f_ee, reads=[(ell, c_)], writes=[(ee, c_)], tag=f"ee_{c_}")
        def f_dec(e, t0=t0, t1=t1):
            return e.tensor_scalar(out=row32[:, t0:t1], in0=ell[:, t0:t1],
                                   scalar1=ell[:, t1 - 1:t1], scalar2=-1.0,
                                   op0=ALU.subtract, op1=ALU.mult)
        em.op("dve", f_dec, reads=[(ell, c_)], writes=[(row32, c_)], tag=f"decl_{c_}")
        def f_dece(e, t0=t0, t1=t1):
            return e.activation(out=dec[:, t0:t1], in_=row32[:, t0:t1], func=AF.Exp)
        em.op("act", f_dece, reads=[(row32, c_)], writes=[(dec, c_)], tag=f"dece_{c_}")
        def f_eg(e, t0=t0, t1=t1):
            return e.activation(out=eg[:, t0:t1], in_=ell[:, t0:t1], func=AF.Exp,
                                bias=loff[:, 0:1])
        em.op("act", f_eg, reads=[(ell, c_), loff], writes=[(eg, c_)], tag=f"eg_{c_}")
        def f_loff(e, t0=t0, t1=t1):
            return e.tensor_add(out=loff[:, :], in0=loff[:, :], in1=ell[:, t1 - 1:t1])
        em.op("dve", f_loff, reads=[(loff, "x"), (ell, c_)], writes=[loff], tag=f"loff_{c_}")
    em.dma("sp", eg_out.ap(), eg[:, :], reads=[eg], writes=[eg_out], tag="st_eg")

    ps_t = em.psum("ps_t")
    ps_bc = ps_t
    dtT = em.sb("dtT", [128, 32 * NCH])
    decT = em.sb("decT", [128, 32 * NCH])
    eeT = em.sb("eeT", [128, 32 * NCH])
    nellT = em.sb("nellT", [128, 32 * NCH])
    laT = em.sb("laT", [128, 32 * NCH])
    lamr = em.sb("lamr", [1, 32 * NCH])
    lambc = em.sb("lambc", [128, 32 * NCH])
    for c_ in range(NCH):
        t0, t1 = c_ * QCH, (c_ + 1) * QCH
        cc = c_ * 32
        for src, dst, cast in ((dt, dtT, None), (dec, decT, None), (ee, eeT, None),
                               (ell, nellT, "neg"), (la, laT, None)):
            def f_tr(e, src=src, t0=t0, t1=t1):
                return e.transpose(out=ps_t[:, 0:32], in_=src[:, t0:t1],
                                   identity=ident_f[0:32, 0:32])
            em.op("pe", f_tr, reads=[(src, ALL), ident_f], writes=[ps_t], tag=f"tr_{dst.name}_{c_}")
            if cast == "neg":
                def f_ev(e, dst=dst, cc=cc):
                    return e.tensor_scalar_mul(out=dst[:, cc:cc + 32], in0=ps_t[:, 0:32],
                                               scalar1=-1.0)
            else:
                def f_ev(e, dst=dst, cc=cc):
                    return e.tensor_copy(out=dst[:, cc:cc + 32], in_=ps_t[:, 0:32])
            em.op("dve", f_ev, reads=[ps_t], writes=[(dst, c_)], tag=f"ev_{dst.name}_{c_}")
        def f_trl(e, t1=t1):
            return e.transpose(out=ps_t[0:1, 32:64], in_=ell[:, t1 - 1:t1],
                               identity=ident_f[0:32, 0:32])
        em.op("pe", f_trl, reads=[(ell, ALL), ident_f], writes=[(ps_t, "l")], tag=f"trl_{c_}")
        def f_evl(e, cc=cc):
            return e.tensor_copy(out=lamr[0:1, cc:cc + 32], in_=ps_t[0:1, 32:64])
        em.op("dve", f_evl, reads=[(ps_t, "l")], writes=[(lamr, c_)], tag=f"evl_{c_}")
        def f_bcl(e, cc=cc):
            return e.matmul(out=ps_bc[:, 64:96], lhsT=ones1[0:1, :], rhs=lamr[0:1, cc:cc + 32],
                            start=True, stop=True)
        em.op("pe", f_bcl, reads=[ones1, (lamr, c_)], writes=[ps_bc], tag=f"bcl_{c_}")
        def f_bce(e, cc=cc):
            return e.activation(out=lambc[:, cc:cc + 32], in_=ps_bc[:, 64:96], func=AF.Exp)
        em.op("act", f_bce, reads=[ps_bc], writes=[(lambc, c_)], tag=f"bce_{c_}")

    dbc = em.sb("dbc", [128, 32])
    def f_dbc(e):
        return e.matmul(out=ps_bc[:, 96:128], lhsT=ones1[0:1, :], rhs=d_t[0:1, :],
                        start=True, stop=True)
    em.op("pe", f_dbc, reads=[ones1, d_t], writes=[ps_bc], tag="dbc_mm")
    em.op("dve", lambda e: e.tensor_copy(out=dbc[:, :], in_=ps_bc[:, 96:128]),
          reads=[ps_bc], writes=[dbc], tag="dbc_ev")

    S_f = em.sb("S_f", [64, 32 * 64])
    S_b = em.sb("S_b", [64, 32 * 64], BF16)
    em.op("dve", lambda e: e.memset(S_f[:, :], 0.0), writes=[S_f], tag="S0")
    em.op("pool", lambda e: e.memset(S_b[:, :], 0.0), writes=[S_b], tag="S0b")

    ps_g = em.psum("ps_g")
    ps_m = [em.psum(f"ps_m{i}") for i in range(2)]
    ps_xs = em.psum("ps_xs", (128, 1024), BF16)
    ps_bt = ps_xs
    ps_ya = em.psum("ps_ya")
    ps_yb = em.psum("ps_yb")
    ps_ss = em.psum("ps_ss")
    gsb = em.sb("gsb", [128, 128], BF16)
    lau = [em.sb(f"lau{i}", [128, 128], BF16) for i in range(4)]
    st_all = em.sb("st_all", [128, 32 * 128], BF16)
    dtx = em.sb("dtx", [128, 32 * 64], BF16)
    xst = em.sb("xst", [128, 32 * 64], BF16)
    bdecs = [em.sb(f"bdec{i}", [128, 64], BF16) for i in range(4)]
    btok = em.sb("btok", [128, 64], BF16)
    ytmp = em.sb("ytmp", [128, 64])
    ytmp2 = em.sb("ytmp2", [128, 64])
    ytq = em.sb("ytq", [128, 256])
    yasb = em.sb("yasb", [128, 256])
    ystg = [em.sb(f"ystg{i}", [128, 2048], BF16) for i in range(2)]

    for c_ in range(NCH):
        t0, t1 = c_ * QCH, (c_ + 1) * QCH
        cc32 = c_ * 32
        def f_g(e, t0=t0, t1=t1):
            return e.matmul(out=ps_g[:, 0:128], lhsT=xbc[16][0:64, 3 + t0:3 + t1],
                            rhs=csb[:, t0:t1], start=True, stop=True)
        em.op("pe", f_g, reads=[(xbc[16], ALL), csb], writes=[ps_g], tag=f"g_{c_}")
        em.op("dve", lambda e: e.tensor_copy(out=gsb[:, :], in_=ps_g[:, 0:128]),
              reads=[ps_g], writes=[gsb], tag=f"gev_{c_}")
        def f_bt(e, t0=t0, t1=t1):
            return e.transpose(out=ps_bt[:, 0:64], in_=xbc[16][0:64, 3 + t0:3 + t1],
                               identity=ident_b[0:64, 0:64])
        em.op("pe", f_bt, reads=[(xbc[16], ALL), ident_b], writes=[(ps_bt, ALL)], tag=f"bt_{c_}")
        em.op("dve", lambda e: e.tensor_copy(out=btok[:, :], in_=ps_bt[:, 0:64]),
              reads=[(ps_bt, ALL)], writes=[btok], tag=f"btev_{c_}")

        for h0 in range(0, 32 if L0B_PH >= 2 else 0, 4):
            pm = ps_m[(h0 // 4) % 2]
            def f_lau(e, h0=h0, cc32=cc32):
                last = None
                for j in range(4):
                    last = e.tensor_scalar(out=lau[j][:, :], in0=uincl_b[:, :],
                                           scalar1=laT[:, cc32 + h0 + j:cc32 + h0 + j + 1],
                                           scalar2=None, op0=ALU.mult)
                return last
            em.op("dve", f_lau, reads=[uincl_b, (laT, c_)],
                  writes=[(lau[j], ALL) for j in range(4)], tag=f"lau_{c_}_{h0}")
            def f_mm(e, pm=pm):
                last = None
                for j in range(4):
                    e.matmul(out=pm[:, j * 128:(j + 1) * 128], lhsT=ones128_b[:, :],
                             rhs=lau[j][:, :], start=True, stop=False)
                    last = e.matmul(out=pm[:, j * 128:(j + 1) * 128], lhsT=uincl_b[:, :],
                                    rhs=mshift_b[:, :], start=False, stop=True)
                return last
            em.op("pe", f_mm, reads=[ones128_b, uincl_b, mshift_b] +
                  [(lau[j], ALL) for j in range(4)], writes=[(pm, ALL)], tag=f"mm_{c_}_{h0}")
            def f_exp(e, h0=h0, pm=pm, cc32=cc32):
                last = None
                for j in range(4):
                    h = h0 + j
                    last = e.activation(out=st_all[:, h * 128:(h + 1) * 128],
                                        in_=pm[:, j * 128:(j + 1) * 128],
                                        func=AF.Exp, bias=nellT[:, cc32 + h:cc32 + h + 1])
                return last
            em.op("act", f_exp, reads=[(pm, ALL), (nellT, c_)],
                  writes=[(st_all, ("h", h0 + j)) for j in range(4)], tag=f"exp_{c_}_{h0}")
            def f_gm(e, h0=h0):
                last = None
                for j in range(4):
                    h = h0 + j
                    last = e.tensor_mul(out=st_all[:, h * 128:(h + 1) * 128],
                                        in0=st_all[:, h * 128:(h + 1) * 128], in1=gsb[:, :])
                return last
            em.op("pool", f_gm, reads=[(st_all, ("h", h0 + j)) for j in range(4)] + [gsb],
                  writes=[(st_all, ("h", h0 + j)) for j in range(4)], tag=f"gm_{c_}_{h0}")

        for grp in range(8 if L0B_PH >= 3 else 0):
            for par in range(2):
                def f_xt(e, grp=grp, par=par, t0=t0, t1=t1):
                    last = None
                    for hh in (par, par + 2):
                        h = grp * 4 + hh
                        last = e.transpose(out=ps_xs[:, hh * 64:hh * 64 + 64],
                                           in_=xbc[h // 2][64 * (h % 2):64 * (h % 2) + 64,
                                                           3 + t0:3 + t1],
                                           identity=ident_b[64 * (h % 2):64 * (h % 2) + 64,
                                                            64 * (h % 2):64 * (h % 2) + 64])
                    return last
                em.op("pe", f_xt, reads=[(xbc[(grp * 4 + hh) // 2], ALL) for hh in (par, par + 2)] +
                      [ident_b], writes=[(ps_xs, ALL)], tag=f"xt_{c_}_{grp}_{par}")
            for hh in range(4 if L0B_XSEVAC else 0):
                h = grp * 4 + hh
                def f_xsc(e, h=h, hh=hh):
                    return e.tensor_copy(out=xst[:, h * 64:(h + 1) * 64],
                                         in_=ps_xs[:, hh * 64:hh * 64 + 64])
                em.op("dve", f_xsc, reads=[(ps_xs, ALL)], writes=[(xst, h)],
                      tag=f"xsc_{c_}_{h}")
                def f_dtx(e, h=h, cc32=cc32):
                    return e.tensor_scalar(out=dtx[:, h * 64:(h + 1) * 64],
                                           in0=xst[:, h * 64:(h + 1) * 64],
                                           scalar1=dtT[:, cc32 + h:cc32 + h + 1],
                                           scalar2=None, op0=ALU.mult)
                em.op("dve", f_dtx, reads=[(xst, h), (dtT, c_)], writes=[(dtx, h)],
                      tag=f"dtx_{c_}_{h}")

        ystage = ystg[c_ % 2]
        for grp in range(4 if L0B_PH >= 4 else 0):
            for hf in range(2):
                hs = [grp * 8 + hf * 4 + j for j in range(4)]
                def f_y(e, hs=hs, t0=t0, t1=t1):
                    last = None
                    for h in hs:
                        hh = h % 8
                        e.matmul(out=ps_ya[:, hh * 64:hh * 64 + 64],
                                 lhsT=st_all[:, h * 128:(h + 1) * 128],
                                 rhs=dtx[:, h * 64:(h + 1) * 64], start=True, stop=True)
                        last = e.matmul(out=ps_yb[:, hh * 64:hh * 64 + 64],
                                        lhsT=csb[:, t0:t1],
                                        rhs=S_b[:, h * 64:(h + 1) * 64], start=True, stop=True)
                    return last
                em.op("pe", f_y, reads=[csb] + [(st_all, ("h", h)) for h in hs] +
                      [(dtx, h) for h in hs] + [(S_b, h) for h in hs],
                      writes=[(ps_ya, ALL), (ps_yb, ALL)], tag=f"y_{c_}_{grp}_{hf}")
                def f_yaev(e, hs=hs):
                    last = None
                    for h in hs:
                        hh = h % 8
                        last = e.tensor_copy(out=yasb[:, (h % 4) * 64:(h % 4) * 64 + 64],
                                             in_=ps_ya[:, hh * 64:hh * 64 + 64])
                    return last
                em.op("dve", f_yaev, reads=[(ps_ya, ALL)], writes=[(yasb, ALL)],
                      tag=f"yaev_{c_}_{grp}_{hf}")
                def f_y12(e, hs=hs, cc32=cc32):
                    last = None
                    for h in hs:
                        hh = h % 8
                        last = e.scalar_tensor_tensor(out=ytq[:, (h % 4) * 64:(h % 4) * 64 + 64],
                                                      in0=ps_yb[:, hh * 64:hh * 64 + 64],
                                                      scalar=eeT[:, cc32 + h:cc32 + h + 1],
                                                      in1=yasb[:, (h % 4) * 64:(h % 4) * 64 + 64],
                                                      op0=ALU.mult, op1=ALU.add)
                    return last
                em.op("dve", f_y12, reads=[(yasb, ALL), (ps_yb, ALL), (eeT, c_)],
                      writes=[(ytq, ALL)], tag=f"y12_{c_}_{grp}_{hf}")
                def f_y3(e, hs=hs, ystage=ystage):
                    last = None
                    for h in hs:
                        last = e.scalar_tensor_tensor(out=ystage[:, h * 64:(h + 1) * 64],
                                                      in0=xst[:, h * 64:(h + 1) * 64],
                                                      scalar=dbc[:, h:h + 1],
                                                      in1=ytq[:, (h % 4) * 64:(h % 4) * 64 + 64],
                                                      op0=ALU.mult, op1=ALU.add)
                    return last
                em.op("dve", f_y3, reads=[(ytq, ALL), dbc] + [(xst, h) for h in hs],
                      writes=[(ystage, h) for h in hs], tag=f"y3_{c_}_{grp}_{hf}")
        em.dma("sp", y_out[t0:t1, :], ystage[:, :], reads=[(ystage, ALL)],
               writes=[(y_out, c_)], tag=f"st_y_{c_}")

        for grp in range(4 if L0B_PH >= 5 else 0):
            for hf in range(2):
                hs = [grp * 8 + hf * 4 + j for j in range(4)]
                def f_bd(e, hs=hs, cc32=cc32):
                    last = None
                    for h in hs:
                        last = e.tensor_scalar(out=bdecs[h % 4][:, :], in0=btok[:, :],
                                               scalar1=decT[:, cc32 + h:cc32 + h + 1],
                                               scalar2=None, op0=ALU.mult)
                    return last
                em.op("dve", f_bd, reads=[btok, (decT, c_)],
                      writes=[(bdecs[h % 4], ALL) for h in hs], tag=f"bd_{c_}_{grp}_{hf}")
                def f_sm(e, hs=hs):
                    last = None
                    for h in hs:
                        hh = h % 8
                        last = e.matmul(out=ps_ss[0:64, hh * 64:hh * 64 + 64],
                                        lhsT=bdecs[h % 4][:, :],
                                        rhs=dtx[:, h * 64:(h + 1) * 64], start=True, stop=True)
                    return last
                em.op("pe", f_sm, reads=[(bdecs[h % 4], ALL) for h in hs] +
                      [(dtx, h) for h in hs], writes=[(ps_ss, ALL)], tag=f"sm_{c_}_{grp}_{hf}")
                def f_su(e, hs=hs, cc32=cc32):
                    last = None
                    for h in hs:
                        hh = h % 8
                        last = e.scalar_tensor_tensor(out=S_f[:, h * 64:(h + 1) * 64],
                                                      in0=S_f[:, h * 64:(h + 1) * 64],
                                                      scalar=lambc[0:64, cc32 + h:cc32 + h + 1],
                                                      in1=ps_ss[0:64, hh * 64:hh * 64 + 64],
                                                      op0=ALU.mult, op1=ALU.add)
                    return last
                em.op("dve", f_su, reads=[(ps_ss, ALL), (lambc, c_)] +
                      [(S_f, h) for h in hs],
                      writes=[(S_f, h) for h in hs], tag=f"su_{c_}_{grp}_{hf}")
                def f_s3(e, hs=hs):
                    last = None
                    for h in hs:
                        last = e.tensor_copy(out=S_b[:, h * 64:(h + 1) * 64],
                                             in_=S_f[:, h * 64:(h + 1) * 64])
                    return last
                em.op("pool", f_s3, reads=[(S_f, h) for h in hs],
                      writes=[(S_b, h) for h in hs], tag=f"s3_{c_}_{grp}_{hf}")

    em.dma("sp", s_out.ap(), S_f[:, :], reads=[S_f], writes=[s_out], tag="st_s")
    em.barrier([y_out, s_out, eg_out, c_out])


def prep_l0b(c, H):
    xbc = np.asarray(H["xbc0"][c])
    if c % 2 == 1:
        prev3 = np.asarray(H["xbc0"][c - 1])[:, -3:]
    else:
        prev3 = np.zeros((2176, 3), BF16NP)
    return {"xbc_in": np.ascontiguousarray(np.concatenate([prev3, xbc], axis=1)),
            "dtr_in": np.asarray(H["dtr0"][c])}


def post_l0b(results, H):
    H["y0"] = [r["y_out"] for r in results]
    H["send0"] = [r["s_out"] for r in results]
    H["eg0"] = [r["eg_out"] for r in results]
    H["cfm0"] = [r["c_out"] for r in results]


STAGE_L0B = Stage("l0b", build_l0b, prep_l0b, post_l0b)



def build_l0c(em, sc, H):
    W = H["weights"]
    y_in = em.dram("y_in", [TLOC, 2048], BF16, kind="ExternalInput")
    z_in = em.dram("z_in", [TLOC, 2048], BF16, kind="ExternalInput")
    eg_in = em.dram("eg_in", [32, TLOC], F32, kind="ExternalInput")
    c_in = em.dram("c_in", [64, TLOC], BF16, kind="ExternalInput")
    sin_in = em.dram("sin_in", [64, 32 * 64], F32, kind="ExternalInput")
    x_in = em.dram("x_in", [C, TLOC], F32, kind="ExternalInput")
    x1_out = em.dram("x1_out", [C, TLOC], F32, kind="ExternalOutput")
    qkv_out = em.dram("qkv_out", [1152, TLOC], BF16, kind="ExternalOutput")

    ident_f = sc.const_tile("ident_f", np.eye(128, dtype=np.float32))
    ident_b = sc.const_tile("ident_b", np.eye(128).astype(BF16NP), BF16)
    ones1 = sc.ones1()
    mnorm_row = sc.cin("mnorm_row", np.asarray(W["L0_attn_mnorm_w"], np.float32)[None, :])
    mnr_sb = em.sb("mnr_sb", [1, 2048])
    em.dma("sp", mnr_sb[:, :], mnorm_row.ap(), reads=[mnorm_row], writes=[mnr_sb], tag="ld_mnr")

    y = load_fm_tiles(em, y_in, 16, 2048, BF16, tag="y")
    eg = em.sb("eg", [32, TLOC])
    em.dma("sp", eg[:, :], eg_in.ap(), reads=[eg_in], writes=[eg], tag="ld_eg")
    cfm = em.sb("cfm", [64, TLOC], BF16)
    em.dma("sp", cfm[:, :], c_in.ap(), reads=[c_in], writes=[cfm], tag="ld_c")
    sin_b = em.sb("sin_b", [64, 2048], BF16)
    em.dma("pool", sin_b[:, :], sin_in.ap(), reads=[sin_in], writes=[sin_b], tag="ld_sin")

    ps_t = em.psum("ps_t")
    ps_co = em.psum("ps_co")
    ps_bc = em.psum("ps_bc")
    egT = em.sb("egT", [128, 32 * NCH])
    for c_ in range(NCH):
        t0, t1 = c_ * QCH, (c_ + 1) * QCH
        def f_tr(e, t0=t0, t1=t1):
            return e.transpose(out=ps_t[:, 0:32], in_=eg[:, t0:t1], identity=ident_f[0:32, 0:32])
        em.op("pe", f_tr, reads=[eg, ident_f], writes=[ps_t], tag=f"egtr_{c_}")
        def f_ev(e, c_=c_):
            return e.tensor_copy(out=egT[:, c_ * 32:(c_ + 1) * 32], in_=ps_t[:, 0:32])
        em.op("dve", f_ev, reads=[ps_t], writes=[(egT, c_)], tag=f"egev_{c_}")

    for c_ in range(NCH):
        t0, t1 = c_ * QCH, (c_ + 1) * QCH
        for grp in range(4):
            for hh in range(8):
                h = grp * 8 + hh
                def f_cm(e, h=h, hh=hh, t0=t0, t1=t1):
                    return e.matmul(out=ps_co[:, hh * 64:hh * 64 + 64],
                                    lhsT=cfm[:, t0:t1],
                                    rhs=sin_b[:, h * 64:(h + 1) * 64], start=True, stop=True)
                em.op("pe", f_cm, reads=[cfm, (sin_b, h)], writes=[(ps_co, ALL)],
                      tag=f"cm_{c_}_{h}")
            for hh in range(8):
                h = grp * 8 + hh
                def f_ce(e, h=h, hh=hh, c_=c_):
                    return e.scalar_tensor_tensor(out=y[c_][:, h * 64:(h + 1) * 64],
                                                  in0=ps_co[:, hh * 64:hh * 64 + 64],
                                                  scalar=egT[:, c_ * 32 + h:c_ * 32 + h + 1],
                                                  in1=y[c_][:, h * 64:(h + 1) * 64],
                                                  op0=ALU.mult, op1=ALU.add)
                em.op("dve", f_ce, reads=[(ps_co, ALL), (egT, c_), (y[c_], ("y", h))],
                      writes=[(y[c_], ("y", h))], tag=f"ce_{c_}_{h}")

    zbuf = [em.sb(f"zb{i}", [128, 2048], BF16) for i in range(2)]
    sqb = em.sb("sqb", [128, 2048], BF16)
    rcol = em.sb("rcol", [128, 2])
    mnbc = em.sb("mnbc", [128, 2048], BF16)
    for tb in range(4):
        def f_mn(e, tb=tb):
            return e.matmul(out=ps_bc[:, 0:512], lhsT=ones1[0:1, :],
                            rhs=mnr_sb[0:1, tb * 512:(tb + 1) * 512], start=True, stop=True)
        em.op("pe", f_mn, reads=[ones1, mnr_sb], writes=[ps_bc], tag=f"mnbc_{tb}")
        def f_mne(e, tb=tb):
            return e.tensor_copy(out=mnbc[:, tb * 512:(tb + 1) * 512], in_=ps_bc[:, 0:512])
        em.op("dve", f_mne, reads=[ps_bc], writes=[(mnbc, tb)], tag=f"mnbce_{tb}")
    for c_ in range(NCH):
        zb = zbuf[c_ % 2]
        em.dma("pool", zb[:, :], z_in[c_ * 128:(c_ + 1) * 128, :], reads=[z_in],
               writes=[zb], tag=f"ldz_{c_}")
        def f_sz(e, zb=zb):
            return e.activation(out=zb[:, :], in_=zb[:, :], func=AF.Silu)
        em.op("act", f_sz, reads=[(zb, "x")], writes=[zb], tag=f"silz_{c_}")
        def f_g(e, c_=c_, zb=zb):
            return e.tensor_mul(out=y[c_][:, :], in0=y[c_][:, :], in1=zb[:, :])
        em.op("dve", f_g, reads=[(y[c_], "x"), zb], writes=[(y[c_], ALL)], tag=f"gmul_{c_}")
        def f_sq(e, c_=c_):
            return e.activation(out=sqb[:, :], in_=y[c_][:, :], func=AF.Square)
        em.op("act", f_sq, reads=[(y[c_], ALL)], writes=[sqb], tag=f"gsq_{c_}")
        def f_rs(e):
            return e.tensor_reduce(out=rcol[:, 0:1], in_=sqb[:, :], axis=AX.X, op=ALU.add)
        em.op("dve", f_rs, reads=[sqb], writes=[(rcol, 0)], tag=f"grs_{c_}")
        def f_vr(e):
            return e.tensor_scalar(out=rcol[:, 0:1], in0=rcol[:, 0:1], scalar1=1.0 / 2048.0,
                                   scalar2=1e-5, op0=ALU.mult, op1=ALU.add)
        em.op("dve", f_vr, reads=[(rcol, 0)], writes=[(rcol, 0)], tag=f"gvr_{c_}")
        def f_rc(e):
            return e.reciprocal(out=rcol[:, 1:2], in_=rcol[:, 0:1])
        em.op("dve", f_rc, reads=[(rcol, 0)], writes=[(rcol, 1)], tag=f"grc_{c_}")
        def f_sqt(e):
            return e.activation(out=rcol[:, 1:2], in_=rcol[:, 1:2], func=AF.Sqrt)
        em.op("act", f_sqt, reads=[(rcol, 1)], writes=[(rcol, 1)], tag=f"gsqt_{c_}")
        def f_ap(e, c_=c_):
            return e.tensor_scalar(out=y[c_][:, :], in0=y[c_][:, :], scalar1=rcol[:, 1:2],
                                   scalar2=None, op0=ALU.mult)
        em.op("dve", f_ap, reads=[(y[c_], "x2"), (rcol, 1)], writes=[(y[c_], ALL)],
              tag=f"gap_{c_}")
        def f_ap2(e, c_=c_):
            return e.tensor_mul(out=y[c_][:, :], in0=y[c_][:, :], in1=mnbc[:, :])
        em.op("dve", f_ap2, reads=[(y[c_], "x3"), (mnbc, ALL)], writes=[(y[c_], ALL)],
              tag=f"gap2_{c_}")

    opw = sc.weight_bf16("W_op", np.asarray(W["L0_attn_out_proj_w"], np.float32))
    ps_tr = em.psum("ps_tr", (128, 1024), BF16)
    ps_mm = [em.psum(f"ps_mm{i}") for i in range(2)]
    gfm = [em.sb(f"gfm_{k}", [128, 512], BF16) for k in range(16)]
    xs_b = [em.sb(f"xsb{i}", [128, 512]) for i in range(2)]
    x1st = [em.sb(f"x1st{i}", [128, 512]) for i in range(2)]
    x1b = y[0:8]
    for tb in range(4):
        for k in range(16):
            for j in range(4):
                tc = tb * 4 + j
                def f_tp(e, k=k, j=j, tc=tc):
                    return e.transpose(out=ps_tr[:, j * 128:(j + 1) * 128],
                                       in_=y[tc][:, k * 128:(k + 1) * 128],
                                       identity=ident_b[:, :])
                em.op("pe", f_tp, reads=[(y[tc], ALL), ident_b], writes=[(ps_tr, ALL)],
                      tag=f"gtp_{tb}_{k}_{j}")
            def f_te(e, k=k):
                return e.tensor_copy(out=gfm[k][:, :], in_=ps_tr[:, 0:512])
            em.op("dve", f_te, reads=[(ps_tr, ALL)], writes=[gfm[k]], tag=f"gte_{tb}_{k}")
        for mc in range(8):
            ps = ps_mm[mc % 2]
            xb = xs_b[mc % 2]
            st = x1st[mc % 2]
            em.dma("sp", xb[:, :], x_in[mc * 128:(mc + 1) * 128, tb * 512:(tb + 1) * 512],
                   reads=[x_in], writes=[xb], tag=f"ldx_{tb}_{mc}")
            def f_mm(e, mc=mc, ps=ps):
                last = None
                for kc in range(16):
                    last = e.matmul(out=ps[:, 0:512], lhsT=opw.sl(kc, mc * 128, (mc + 1) * 128),
                                    rhs=gfm[kc][:, :], start=(kc == 0), stop=(kc == 15))
                return last
            em.op("pe", f_mm, reads=[(opw.buf, ALL)] + [(gfm[kc], ALL) for kc in range(16)],
                  writes=[(ps, ALL)], tag=f"op_mm_{tb}_{mc}")
            def f_x1(e, mc=mc, ps=ps, xb=xb, st=st):
                return e.tensor_add(out=st[:, :], in0=ps[:, 0:512], in1=xb[:, :])
            em.op("dve", f_x1, reads=[(ps, ALL), xb], writes=[st], tag=f"x1_{tb}_{mc}")
            em.dma("sp", x1_out[mc * 128:(mc + 1) * 128, tb * 512:(tb + 1) * 512], st[:, :],
                   reads=[st], writes=[(x1_out, (tb, mc))], tag=f"stx1_{tb}_{mc}")

    for mc in range(8):
        em.dma("pool", x1b[mc][:, :], x1_out[mc * 128:(mc + 1) * 128, :],
               reads=[(x1_out, ALL)], writes=[(x1b[mc], ALL)], tag=f"ldx1b_{mc}")
    r2 = y[8:16]
    ns = NormScratch(em, TLOC, rms_only=True, ps_pool=[ps_t, ps_co, ps_bc])
    emit_norm(em, sc, ns, x1b, r2, 8, TLOC, rms=True, tag="rms2")
    aw = sc.weight_bf16("W_attn", np.asarray(W["L0_attn_attn_w"], np.float32))
    proj_to_dram(em, sc, aw, r2, 1024, 1152, TLOC, qkv_out, ps_mm, "qkv", stag=zbuf)
    em.barrier([x1_out, qkv_out])


def prep_l0c(c, H):
    if c % 2 == 1:
        sin = np.asarray(H["send0"][c - 1], np.float32)
    else:
        sin = np.zeros((64, 2048), np.float32)
    return {"y_in": np.asarray(H["y0"][c]),
            "z_in": np.ascontiguousarray(np.asarray(H["z0"][c]).T),
            "eg_in": np.asarray(H["eg0"][c]),
            "c_in": np.asarray(H["cfm0"][c]),
            "sin_in": np.ascontiguousarray(sin),
            "x_in": np.ascontiguousarray(np.asarray(H["xn0"][c], np.float32))}


def post_l0c(results, H):
    H["x1"] = [r["x1_out"] for r in results]
    H["qkv0"] = [r["qkv_out"] for r in results]


STAGE_L0C = Stage("l0c", build_l0c, prep_l0c, post_l0c)



def build_l0d(em, sc, H):
    TF = T
    q_in = em.dram("q_in", [512, TF], BF16, kind="ExternalInput")
    k_in = em.dram("k_in", [128, TF], BF16, kind="ExternalInput")
    v_in = em.dram("v_in", [128, TF], BF16, kind="ExternalInput")
    ctx_out = em.dram("ctx_out", [512, TF], BF16, kind="ExternalOutput")

    ident_b = sc.const_tile("ident_b", np.eye(128).astype(BF16NP), BF16)
    ones1 = sc.ones1()
    onescol_b = sc.ones_col_bf()
    BIG = 30000.0
    masks = []
    for di in range(4):
        d = di * 128
        m = np.zeros((128, 512), np.float32)
        for r in range(128):
            m[r, :] = np.where(np.arange(512) >= r + d, 0.0, -BIG)
        masks.append(sc.const_tile(f"cmask{di}", m))

    q = load_fm_tiles(em, q_in, 4, TF, BF16, tag="q")
    kf = em.sb("kf", [128, TF], BF16)
    em.dma("pool", kf[:, :], k_in.ap(), reads=[k_in], writes=[kf], tag="ld_k")
    vf = em.sb("vf", [128, TF], BF16)
    em.dma("pool", vf[:, :], v_in.ap(), reads=[v_in], writes=[vf], tag="ld_v")

    ps_tr = em.psum("ps_tr", (128, 1024), BF16)
    ps_s = [em.psum(f"ps_s{i}") for i in range(3)]
    ps_av = em.psum("ps_av")
    ps_b = em.psum("ps_b")
    ps_m = em.psum("ps_m")

    NTK = TF // 128
    vtok = em.sb("vtok", [128, NTK * 65], BF16)
    em.op("dve", lambda e: e.memset(vtok[:, :], 1.0), writes=[(vtok, ALL)], tag="vt_ones")
    for i0 in range(0, NTK, 4):
        for j in range(4):
            i = i0 + j
            def f_vt(e, i=i, j=j):
                return e.transpose(out=ps_tr[:, j * 128:j * 128 + 64],
                                   in_=vf[0:64, i * 128:(i + 1) * 128],
                                   identity=ident_b[0:64, 0:64])
            em.op("pe", f_vt, reads=[vf, ident_b], writes=[(ps_tr, ALL)], tag=f"vt_{i}")
        for j in range(4):
            i = i0 + j
            def f_ve(e, i=i, j=j):
                return e.tensor_copy(out=vtok[:, i * 65:i * 65 + 64],
                                     in_=ps_tr[:, j * 128:j * 128 + 64])
            em.op("dve", f_ve, reads=[(ps_tr, ALL)], writes=[(vtok, i)], tag=f"ve_{i}")

    sqb = em.sb("sqb", [128, TF], BF16)
    qn = em.sb("qn", [1, TF])
    mrow = em.sb("mrow", [1, 16])
    nbcol = em.sb("nbcol", [128, 8])
    def f_ksq(e):
        return e.activation(out=sqb[:, :], in_=kf[:, :], func=AF.Square)
    em.op("act", f_ksq, reads=[kf], writes=[sqb], tag="ksq")
    for tb in range(8):
        def f_ks(e, tb=tb):
            return e.matmul(out=ps_m[0:1, 0:512], lhsT=onescol_b[0:64, 0:1],
                            rhs=sqb[0:64, tb * 512:(tb + 1) * 512], start=True, stop=True)
        em.op("pe", f_ks, reads=[onescol_b, sqb], writes=[ps_m], tag=f"ks_{tb}")
        def f_ke(e, tb=tb):
            return e.tensor_copy(out=qn[0:1, tb * 512:(tb + 1) * 512], in_=ps_m[0:1, 0:512])
        em.op("dve", f_ke, reads=[ps_m], writes=[(qn, tb)], tag=f"ke_{tb}")
    em.op("dve", lambda e: e.tensor_reduce(out=mrow[0:1, 8:9], in_=qn[0:1, :], axis=AX.X,
                                           op=ALU.max),
          reads=[qn], writes=[(mrow, "k")], tag="kmax")
    for h in range(8):
        qt = q[h // 2]
        base = 64 * (h % 2)
        def f_qsq(e, qt=qt, base=base):
            return e.activation(out=sqb[base:base + 64, :], in_=qt[base:base + 64, :],
                                func=AF.Square)
        em.op("act", f_qsq, reads=[(qt, ALL)], writes=[(sqb, h)], tag=f"qsq_{h}")
        for tb in range(8):
            def f_qs(e, tb=tb, base=base):
                return e.matmul(out=ps_m[0:1, 0:512], lhsT=onescol_b[base:base + 64, 0:1],
                                rhs=sqb[base:base + 64, tb * 512:(tb + 1) * 512],
                                start=True, stop=True)
            em.op("pe", f_qs, reads=[onescol_b, (sqb, h)], writes=[ps_m], tag=f"qs_{h}_{tb}")
            def f_qe(e, tb=tb):
                return e.tensor_copy(out=qn[0:1, tb * 512:(tb + 1) * 512],
                                     in_=ps_m[0:1, 0:512])
            em.op("dve", f_qe, reads=[ps_m], writes=[(qn, tb)], tag=f"qe_{h}_{tb}")
        def f_qm(e, h=h):
            return e.tensor_reduce(out=mrow[0:1, h:h + 1], in_=qn[0:1, :], axis=AX.X,
                                   op=ALU.max)
        em.op("dve", f_qm, reads=[qn], writes=[(mrow, ("q", h))], tag=f"qmax_{h}")
        def f_bd(e, h=h):
            return e.tensor_mul(out=mrow[0:1, h:h + 1], in0=mrow[0:1, h:h + 1],
                                in1=mrow[0:1, 8:9])
        em.op("dve", f_bd, reads=[(mrow, ("q", h)), (mrow, "k")], writes=[(mrow, ("q", h))],
              tag=f"bmul_{h}")
        def f_bs(e, h=h):
            return e.activation(out=mrow[0:1, h:h + 1], in_=mrow[0:1, h:h + 1], func=AF.Sqrt)
        em.op("act", f_bs, reads=[(mrow, ("q", h))], writes=[(mrow, ("q", h))], tag=f"bsq_{h}")
        def f_bn(e, h=h):
            return e.tensor_scalar_mul(out=mrow[0:1, h:h + 1], in0=mrow[0:1, h:h + 1],
                                       scalar1=-0.125)
        em.op("dve", f_bn, reads=[(mrow, ("q", h))], writes=[(mrow, ("q", h))], tag=f"bneg_{h}")
        def f_bb(e, h=h):
            return e.matmul(out=ps_m[:, 512 - 8 + h:512 - 7 + h], lhsT=ones1[0:1, :],
                            rhs=mrow[0:1, h:h + 1], start=True, stop=True)
        em.op("pe", f_bb, reads=[ones1, (mrow, ("q", h))], writes=[(ps_m, ("b", h))],
              tag=f"bbc_{h}")
        def f_be(e, h=h):
            return e.tensor_copy(out=nbcol[:, h:h + 1], in_=ps_m[:, 512 - 8 + h:512 - 7 + h])
        em.op("dve", f_be, reads=[(ps_m, ("b", h))], writes=[(nbcol, h)], tag=f"bev_{h}")

    pt = [em.sb(f"pt{i}", [128, 512], BF16) for i in range(4)]
    msk = [em.sb(f"msk{i}", [128, 512]) for i in range(4)]
    dn = em.sb("dn", [1, 512])
    rdbc = em.sb("rdbc", [64, 512])
    ctx = [em.sb(f"ctx_{i}", [128, TF], BF16) for i in range(4)]
    ps4 = ps_s + [ps_m]
    for h in range(8):
        qt = q[h // 2]
        base = 64 * (h % 2)
        for tb in range(8):
            ntk = 4 * tb + 4
            def emit_av(pair, ntk=ntk):
                pts = [pt[i % 4] for i in pair]
                def f_av(e, pair=pair, ntk=ntk, pts=pts):
                    last = None
                    for i, ptb in zip(pair, pts):
                        last = e.matmul(out=ps_av[0:65, 0:512],
                                        lhsT=vtok[:, i * 65:i * 65 + 65],
                                        rhs=ptb[:, :], start=(i == 0), stop=(i == ntk - 1))
                    return last
                em.op("pe", f_av, reads=[(vtok, ALL)] + [(p, ALL) for p in pts],
                      writes=[(ps_av, ALL)], tag=f"av_{h}_{tb}_{pair[0]}")
            pendq = []
            for u in range(0, ntk, 2):
                pair = [i for i in (u, u + 1) if i < ntk]
                psl = [ps4[i % 4] for i in pair]
                ptl = [pt[i % 4] for i in pair]
                mkl = [msk[i % 4] for i in pair]
                def f_sc(e, pair=pair, tb=tb, base=base, qt=qt, psl=psl):
                    last = None
                    for i, pss in zip(pair, psl):
                        last = e.matmul(out=pss[:, 0:512],
                                        lhsT=kf[base:base + 64, i * 128:(i + 1) * 128],
                                        rhs=qt[base:base + 64, tb * 512:(tb + 1) * 512],
                                        start=True, stop=True)
                    return last
                em.op("pe", f_sc, reads=[kf, (qt, ALL)],
                      writes=[(p, ALL) for p in psl], tag=f"sc_{h}_{tb}_{u}")
                diags = [i for i in pair if i >= 4 * tb]
                if diags:
                    def f_mk(e, pair=pair, tb=tb, psl=psl, mkl=mkl):
                        last = None
                        for i, pss, mkb in zip(pair, psl, mkl):
                            if i >= 4 * tb:
                                last = e.tensor_add(out=mkb[:, :], in0=pss[:, 0:512],
                                                    in1=masks[i - 4 * tb][:, :])
                        return last
                    em.op("dve", f_mk, reads=[(p, ALL) for p in psl] +
                          [masks[i - 4 * tb] for i in diags],
                          writes=[(msk[i % 4], ALL) for i in diags], tag=f"mk_{h}_{tb}_{u}")
                def f_ex(e, pair=pair, tb=tb, h=h, psl=psl, ptl=ptl, mkl=mkl):
                    last = None
                    for i, pss, ptb, mkb in zip(pair, psl, ptl, mkl):
                        srcap = mkb[:, :] if i >= 4 * tb else pss[:, 0:512]
                        last = e.activation(out=ptb[:, :], in_=srcap, func=AF.Exp,
                                            bias=nbcol[:, h:h + 1], scale=0.125)
                    return last
                em.op("act", f_ex, reads=[(p, ALL) for p in psl] +
                      [(msk[i % 4], ALL) for i in diags] + [(nbcol, h)],
                      writes=[(p, ALL) for p in ptl], tag=f"ex_{h}_{tb}_{u}")
                if pendq:
                    emit_av(pendq.pop(0))
                pendq.append(pair)
            for pp in pendq:
                emit_av(pp)
            def f_dn(e):
                return e.tensor_copy(out=dn[0:1, :], in_=ps_av[64:65, 0:512])
            em.op("dve", f_dn, reads=[(ps_av, ALL)], writes=[dn], tag=f"dn_{h}_{tb}")
            def f_rd(e):
                return e.reciprocal(out=dn[0:1, :], in_=dn[0:1, :])
            em.op("dve", f_rd, reads=[(dn, "x")], writes=[dn], tag=f"rd_{h}_{tb}")
            def f_bc(e):
                return e.matmul(out=ps_b[0:64, 0:512], lhsT=ones1[0:1, 0:64],
                                rhs=dn[0:1, :], start=True, stop=True)
            em.op("pe", f_bc, reads=[ones1, dn], writes=[ps_b], tag=f"bc_{h}_{tb}")
            def f_rde(e):
                return e.tensor_copy(out=rdbc[:, :], in_=ps_b[0:64, 0:512])
            em.op("dve", f_rde, reads=[ps_b], writes=[rdbc], tag=f"rde_{h}_{tb}")
            def f_cx(e, h=h, tb=tb, base=base):
                return e.tensor_mul(out=ctx[h // 2][base:base + 64, tb * 512:(tb + 1) * 512],
                                    in0=ps_av[0:64, 0:512], in1=rdbc[:, :])
            em.op("dve", f_cx, reads=[(ps_av, ALL), rdbc],
                  writes=[(ctx[h // 2], (h, tb))], tag=f"cx_{h}_{tb}")
    for i in range(4):
        em.dma("sp", ctx_out[i * 128:(i + 1) * 128, :], ctx[i][:, :],
               reads=[(ctx[i], ALL)], writes=[(ctx_out, i)], tag=f"st_ctx_{i}")
    em.barrier([ctx_out])


def prep_l0d(c, H):
    b, hh = c // 2, c % 2
    qa = np.asarray(H["qkv0"][2 * b])
    qb = np.asarray(H["qkv0"][2 * b + 1])
    q = np.concatenate([qa[hh * 512:(hh + 1) * 512, :], qb[hh * 512:(hh + 1) * 512, :]], axis=1)
    k = np.concatenate([qa[1024:1088, :], qb[1024:1088, :]], axis=1)
    v = np.concatenate([qa[1088:1152, :], qb[1088:1152, :]], axis=1)
    k2 = np.concatenate([k, k], axis=0)
    v2 = np.concatenate([v, v], axis=0)
    return {"q_in": np.ascontiguousarray(q), "k_in": np.ascontiguousarray(k2),
            "v_in": np.ascontiguousarray(v2)}


def post_l0d(results, H):
    H["ctx0"] = [r["ctx_out"] for r in results]


STAGE_L0D = Stage("l0d", build_l0d, prep_l0d, post_l0d)



def build_l0e(em, sc, H):
    W = H["weights"]
    ctx_in = em.dram("ctx_in", [1024, TLOC], BF16, kind="ExternalInput")
    x1_in = em.dram("x1_in", [1024, TLOC], F32, kind="ExternalInput")
    x2_out = em.dram("x2_out", [1024, TLOC], F32, kind="ExternalOutput")
    r3_out = em.dram("r3_out", [1024, TLOC], BF16, kind="ExternalOutput")

    ctx = load_fm_tiles(em, ctx_in, 8, TLOC, BF16, tag="ctx")
    x1 = load_fm_tiles(em, x1_in, 8, TLOC, F32, tag="x1", eng="sp")
    pw = sc.weight_bf16("W_proj", np.asarray(W["L0_attn_proj_w"], np.float32))
    psums = [em.psum(f"pp{i}") for i in range(3)]
    x2b = [em.sb(f"x2b_{i}", [128, TLOC], BF16) for i in range(8)]
    for mc in range(8):
        for tc in range(4):
            ps = psums[(mc * 4 + tc) % 2]
            def fmm(e, mc=mc, tc=tc, ps=ps):
                last = None
                for kc in range(8):
                    last = e.matmul(out=ps[:, 0:512], lhsT=pw.sl(kc, mc * 128, (mc + 1) * 128),
                                    rhs=ctx[kc][:, tc * 512:(tc + 1) * 512],
                                    start=(kc == 0), stop=(kc == 7))
                return last
            em.op("pe", fmm, reads=[(pw.buf, ALL)] + [(ctx[k], ALL) for k in range(8)],
                  writes=[(ps, ALL)], tag=f"pj_{mc}_{tc}")
            def fx2(e, mc=mc, tc=tc, ps=ps):
                return e.tensor_add(out=x1[mc][:, tc * 512:(tc + 1) * 512],
                                    in0=ps[:, 0:512],
                                    in1=x1[mc][:, tc * 512:(tc + 1) * 512])
            em.op("dve", fx2, reads=[(ps, ALL), (x1[mc], ("x", tc))],
                  writes=[(x1[mc], ("w", tc))], tag=f"x2_{mc}_{tc}")
        em.dma("sp", x2_out[mc * 128:(mc + 1) * 128, :], x1[mc][:, :],
               reads=[(x1[mc], ALL)], writes=[(x2_out, mc)], tag=f"stx2_{mc}")
        def fxb(e, mc=mc):
            return e.tensor_copy(out=x2b[mc][:, :], in_=x1[mc][:, :])
        em.op("pool", fxb, reads=[(x1[mc], ALL)], writes=[x2b[mc]], tag=f"x2b_{mc}")
    r3 = ctx
    ns = NormScratch(em, TLOC, rms_only=True, ps_pool=psums)
    emit_norm(em, sc, ns, x2b, r3, 8, TLOC, rms=True, tag="rms3")
    for mc in range(8):
        em.dma("sp", r3_out[mc * 128:(mc + 1) * 128, :], r3[mc][:, :],
               reads=[(r3[mc], ALL)], writes=[(r3_out, mc)], tag=f"str3_{mc}")
    em.barrier([x2_out, r3_out])


def prep_l0e(c, H):
    b, half = c // 2, c % 2
    sl = slice(half * TLOC, (half + 1) * TLOC)
    ctx = np.concatenate([np.asarray(H["ctx0"][2 * b])[:, sl],
                          np.asarray(H["ctx0"][2 * b + 1])[:, sl]], axis=0)
    return {"ctx_in": np.ascontiguousarray(ctx),
            "x1_in": np.ascontiguousarray(np.asarray(H["x1"][c], np.float32))}


def post_l0e(results, H):
    H["x2"] = [r["x2_out"] for r in results]
    H["r3"] = [r["r3_out"] for r in results]


STAGE_L0E = Stage("l0e", build_l0e, prep_l0e, post_l0e)



def build_l0f(em, sc, H):
    W = H["weights"]
    TW = TLOC + 1
    r3_in = em.dram("r3_in", [1024, TW], BF16, kind="ExternalInput")
    x2_in = em.dram("x2_in", [1024, TLOC], F32, kind="ExternalInput")
    s_out = em.dram("s_out_f", [1024, TLOC], F32, kind="ExternalOutput")
    kk_dram = em.dram("kk_scratch", [4096, TLOC], BF16, kind="Internal")

    maak = np.asarray(W["L0_attn_time_maa_k"], np.float32)
    maar = np.asarray(W["L0_attn_time_maa_r"], np.float32)
    maak_t = sc.const_tile("maak", maak.reshape(8, 128).T)
    maar_t = sc.const_tile("maar", maar.reshape(8, 128).T)

    kwv = np.asarray(W["L0_attn_key_w"], np.float32).reshape(8, 128, 4096)
    kw_d = sc.cin("W_key_r", np.ascontiguousarray(kwv.transpose(1, 0, 2).reshape(128, 8 * 4096)))
    vwv = np.asarray(W["L0_attn_value_w"], np.float32).reshape(32, 128, 1024)
    vw_d = sc.cin("W_val_r", np.ascontiguousarray(vwv.transpose(1, 0, 2).reshape(128, 32 * 1024)))
    rw = sc.weight_bf16("W_rec", np.asarray(W["L0_attn_recept_w"], np.float32))

    r3 = load_fm_tiles(em, r3_in, 8, TW, BF16, tag="r3")
    xk = [em.sb(f"xk_{i}", [128, TLOC], BF16) for i in range(8)]
    xr = [em.sb(f"xr_{i}", [128, TLOC], BF16) for i in range(8)]
    xx = em.sb("xx", [128, TLOC])
    for kc in range(8):
        def fxx(e, kc=kc):
            return e.tensor_sub(out=xx[:, :], in0=r3[kc][:, 0:TLOC], in1=r3[kc][:, 1:TW])
        em.op("dve", fxx, reads=[(r3[kc], ALL)], writes=[xx], tag=f"xx_{kc}")
        def fxk(e, kc=kc):
            return e.scalar_tensor_tensor(out=xk[kc][:, :], in0=xx[:, :],
                                          scalar=maak_t[:, kc:kc + 1],
                                          in1=r3[kc][:, 1:TW], op0=ALU.mult, op1=ALU.add)
        em.op("dve", fxk, reads=[xx, maak_t, (r3[kc], ALL)], writes=[xk[kc]], tag=f"xk_{kc}")
        def fxr(e, kc=kc):
            return e.scalar_tensor_tensor(out=xr[kc][:, :], in0=xx[:, :],
                                          scalar=maar_t[:, kc:kc + 1],
                                          in1=r3[kc][:, 1:TW], op0=ALU.mult, op1=ALU.add)
        em.op("dve", fxr, reads=[xx, maar_t, (r3[kc], ALL)], writes=[xr[kc]], tag=f"xr_{kc}")

    MU = float(np.sqrt(0.5))
    DEN = float(np.sqrt(1.0 / (4.0 * np.pi)) * np.sqrt(2.0))
    erfb = sc.const_tile("erfb", np.full((128, 1), -MU / DEN, np.float32))
    ps_k = [em.psum(f"psk{i}") for i in range(2)]
    ps_v = [em.psum(f"psv{i}") for i in range(2)]
    ps_r = [em.psum(f"psr{i}") for i in range(2)]
    kwb = [em.sb(f"kwb{i}", [128, 8 * 128], BF16) for i in range(2)]
    vwb = [em.sb(f"vwb{i}", [128, 32 * 128], BF16) for i in range(2)]
    erf_s = em.sb("erf_s", [128, TLOC], BF16)
    kw_r = kw_d[:, :].rearrange("p (k m) -> p k m", k=8)
    vw_r = vw_d[:, :].rearrange("p (k m) -> p k m", k=32)

    for mc in range(32):
        kb = kwb[mc % 2]
        em.dma("pool", kb[:, :].rearrange("p (k m) -> p k m", k=8),
               kw_r[:, :, mc * 128:(mc + 1) * 128],
               reads=[kw_d], writes=[kb], tag=f"ldkw_{mc}")
        for tc in range(4):
            ps = ps_k[tc % 2]
            def fkm(e, mc=mc, tc=tc, ps=ps, kb=kb):
                last = None
                for kc in range(8):
                    last = e.matmul(out=ps[:, 0:512], lhsT=kb[:, kc * 128:(kc + 1) * 128],
                                    rhs=xk[kc][:, tc * 512:(tc + 1) * 512],
                                    start=(kc == 0), stop=(kc == 7))
                return last
            em.op("pe", fkm, reads=[(kb, ALL)] + [(xk[k], ALL) for k in range(8)],
                  writes=[(ps, ALL)], tag=f"km_{mc}_{tc}")
            def fke(e, tc=tc, ps=ps):
                return e.activation(out=erf_s[:, tc * 512:(tc + 1) * 512], in_=ps[:, 0:512],
                                    func=AF.Erf, bias=erfb[:, 0:1], scale=1.0 / DEN)
            em.op("act", fke, reads=[(ps, ALL), erfb], writes=[(erf_s, tc)], tag=f"ke_{mc}_{tc}")
            def fks(e, tc=tc):
                return e.tensor_scalar(out=erf_s[:, tc * 512:(tc + 1) * 512],
                                       in0=erf_s[:, tc * 512:(tc + 1) * 512], scalar1=0.5,
                                       scalar2=0.5, op0=ALU.mult, op1=ALU.add)
            em.op("dve", fks, reads=[(erf_s, tc)], writes=[(erf_s, ("s", tc))],
                  tag=f"ks_{mc}_{tc}")
        em.dma("sp", kk_dram[mc * 128:(mc + 1) * 128, :], erf_s[:, :],
               reads=[(erf_s, ALL)], writes=[(kk_dram, mc)], tag=f"stkk_{mc}")

    kks = [em.sb(f"kk_{i}", [128, 512], BF16) for i in range(32)]
    rsg = em.sb("rsg", [128, 512], BF16)
    sstg = em.sb("sstg", [128, 512])
    xbuf = em.sb("xbuf", [128, 512])
    for tb in range(4):
        t0, t1 = tb * 512, (tb + 1) * 512
        for mc in range(32):
            em.dma("pool", kks[mc][:, :], kk_dram[mc * 128:(mc + 1) * 128, t0:t1],
                   reads=[(kk_dram, mc)], writes=[kks[mc]], tag=f"ldkk_{tb}_{mc}")
        for m2 in range(8):
            vb = vwb[m2 % 2]
            em.dma("pool", vb[:, :].rearrange("p (k m) -> p k m", k=32),
                   vw_r[:, :, m2 * 128:(m2 + 1) * 128],
                   reads=[vw_d], writes=[vb], tag=f"ldvw_{tb}_{m2}")
            psv = ps_v[m2 % 2]
            def fvm(e, m2=m2, psv=psv, vb=vb):
                last = None
                for mc in range(32):
                    last = e.matmul(out=psv[:, 0:512], lhsT=vb[:, mc * 128:(mc + 1) * 128],
                                    rhs=kks[mc][:, :], start=(mc == 0), stop=(mc == 31))
                return last
            em.op("pe", fvm, reads=[(vb, ALL)] + [(kks[mc], ALL) for mc in range(32)],
                  writes=[(psv, ALL)], tag=f"vm_{tb}_{m2}")
            psr = ps_r[m2 % 2]
            def frm(e, m2=m2, t0=t0, t1=t1, psr=psr):
                last = None
                for kc in range(8):
                    last = e.matmul(out=psr[:, 0:512], lhsT=rw.sl(kc, m2 * 128, (m2 + 1) * 128),
                                    rhs=xr[kc][:, t0:t1], start=(kc == 0), stop=(kc == 7))
                return last
            em.op("pe", frm, reads=[(rw.buf, ALL)] + [(xr[k], ALL) for k in range(8)],
                  writes=[(psr, ALL)], tag=f"rm_{tb}_{m2}")
            def frs(e, psr=psr):
                return e.activation(out=rsg[:, :], in_=psr[:, 0:512], func=AF.Sigmoid)
            em.op("act", frs, reads=[(psr, ALL)], writes=[rsg], tag=f"rs_{tb}_{m2}")
            em.dma("sp", xbuf[:, :], x2_in[m2 * 128:(m2 + 1) * 128, t0:t1],
                   reads=[x2_in], writes=[xbuf], tag=f"ldx2_{tb}_{m2}")
            def fcm(e, psv=psv):
                return e.tensor_mul(out=sstg[:, :], in0=psv[:, 0:512], in1=rsg[:, :])
            em.op("dve", fcm, reads=[(psv, ALL), rsg], writes=[sstg], tag=f"cm_{tb}_{m2}")
            def fso(e):
                return e.tensor_add(out=sstg[:, :], in0=sstg[:, :], in1=xbuf[:, :])
            em.op("dve", fso, reads=[(sstg, "x"), xbuf], writes=[sstg], tag=f"so_{tb}_{m2}")
            em.dma("sp", s_out[m2 * 128:(m2 + 1) * 128, t0:t1], sstg[:, :],
                   reads=[sstg], writes=[(s_out, (tb, m2))], tag=f"sts_{tb}_{m2}")
    em.barrier([s_out])


def prep_l0f(c, H):
    r3 = np.asarray(H["r3"][c])
    if c % 2 == 1:
        bcol = np.asarray(H["r3"][c - 1])[:, -1:]
    else:
        bcol = np.zeros((1024, 1), BF16NP)
    return {"r3_in": np.ascontiguousarray(np.concatenate([bcol, r3], axis=1)),
            "x2_in": np.ascontiguousarray(np.asarray(H["x2"][c], np.float32))}


def post_l0f(results, H):
    H["s0"] = [r["s_out_f"] for r in results]


STAGE_L0F = Stage("l0f", build_l0f, prep_l0f, post_l0f)



def build_l0g(em, sc, H):
    W = H["weights"]
    xn_in = em.dram("xn_in", [1024, TLOC], BF16, kind="ExternalInput")
    s_in = em.dram("s_in", [1024, TLOC], F32, kind="ExternalInput")
    x_in = em.dram("x_in", [1024, TLOC], F32, kind="ExternalInput")
    xo_out = em.dram("xo_out", [1024, TLOC], F32, kind="ExternalOutput")

    ones1 = sc.ones1()
    ident_f = sc.const_tile("ident_f", np.eye(128, dtype=np.float32))
    rbr_t = sc.const_tile("rbr", np.asarray(W["L0_ffn_router_b"], np.float32)[None, :])
    sb1c = sc.const_tile("sb1c", np.asarray(W["L0_ffn_sb1"], np.float32).reshape(16, 128).T)
    sb2c = sc.const_tile("sb2c", np.asarray(W["L0_ffn_sb2"], np.float32).reshape(8, 128).T)
    rb1c = sc.const_tile("rb1c", np.ascontiguousarray(
        np.asarray(W["L0_ffn_rb1"], np.float32).reshape(8, 16, 128).transpose(2, 0, 1).reshape(128, 128)))
    rb2r = sc.const_tile("rb2r", np.asarray(W["L0_ffn_rb2"], np.float32))
    rb2row = sc.const_tile("rb2row", np.asarray(W["L0_ffn_rb2"], np.float32).reshape(1, 8 * 1024).astype(BF16NP), BF16)
    ones1b = sc.const_tile("ones1b", np.ones((1, 128), BF16NP), BF16)

    xn = load_fm_tiles(em, xn_in, 8, TLOC, BF16, tag="xn")
    rww = sc.weight_bf16("W_rtr", np.asarray(W["L0_ffn_router_w"], np.float32))
    w1h = [sc.cin(f"W1r_{e}", np.ascontiguousarray(
            np.asarray(W["L0_ffn_rw1"], np.float32)[e].reshape(8, 128, 2048)
            .transpose(1, 0, 2).reshape(128, 8 * 2048))) for e in range(8)]
    w2h = [sc.cin(f"W2r_{e}", np.ascontiguousarray(
            np.asarray(W["L0_ffn_rw2"], np.float32)[e].reshape(16, 128, 1024)
            .transpose(1, 0, 2).reshape(128, 16 * 1024))) for e in range(8)]
    ws1h = sc.cin("Ws1r", np.ascontiguousarray(
        np.asarray(W["L0_ffn_sw1"], np.float32).reshape(8, 128, 2048)
        .transpose(1, 0, 2).reshape(128, 8 * 2048)))
    ws2h = sc.cin("Ws2r", np.ascontiguousarray(
        np.asarray(W["L0_ffn_sw2"], np.float32).reshape(16, 128, 1024)
        .transpose(1, 0, 2).reshape(128, 16 * 1024)))

    ps_r = em.psum("ps_r")
    ps_t = em.psum("ps_t")
    ps_h = [em.psum(f"ps_h{i}") for i in range(2)]
    ps_o = [em.psum(f"ps_o{i}") for i in range(2)]
    ps_b = em.psum("ps_b")

    sco = em.sb("sco", [128, 8])
    scc = em.sb("scc", [128, 3])
    wsp = em.sb("wsp", [128, 128])
    wrow = em.sb("wrow", [1, 8 * TLOC], BF16)
    em.op("dve", lambda e: e.memset(wsp[:, :], 0.0), writes=[wsp], tag="wsp0")
    for tt in range(16):
        def f_rt(e, tt=tt):
            for kc in range(8):
                e.matmul(out=ps_r[0:128, 0:8], lhsT=xn[kc][:, tt * 128:(tt + 1) * 128],
                         rhs=rww.sl(kc, 0, 8), start=(kc == 0), stop=False)
            return e.matmul(out=ps_r[0:128, 0:8], lhsT=ones1[0:1, :], rhs=rbr_t[0:1, :],
                            start=False, stop=True)
        em.op("pe", f_rt, reads=[(rww.buf, ALL), rbr_t, ones1] + [(xn[k], ALL) for k in range(8)],
              writes=[ps_r], tag=f"rt_{tt}")
        def f_re(e):
            return e.tensor_copy(out=sco[:, :], in_=ps_r[0:128, 0:8])
        em.op("dve", f_re, reads=[ps_r], writes=[sco], tag=f"re_{tt}")
        def f_mx(e):
            return e.tensor_reduce(out=scc[:, 0:1], in_=sco[:, :], axis=AX.X, op=ALU.max,
                                   negate=True)
        em.op("dve", f_mx, reads=[(sco, ALL)], writes=[(scc, 0)], tag=f"mx_{tt}")
        def f_ex(e):
            return e.activation(out=sco[:, :], in_=sco[:, :], func=AF.Exp, bias=scc[:, 0:1])
        em.op("act", f_ex, reads=[(sco, ALL), (scc, 0)], writes=[(sco, ALL)], tag=f"exr_{tt}")
        def f_sm(e):
            return e.tensor_reduce(out=scc[:, 1:2], in_=sco[:, :], axis=AX.X, op=ALU.add)
        em.op("dve", f_sm, reads=[(sco, ALL)], writes=[(scc, 1)], tag=f"sm_{tt}")
        def f_rc(e):
            return e.reciprocal(out=scc[:, 2:3], in_=scc[:, 1:2])
        em.op("dve", f_rc, reads=[(scc, 1)], writes=[(scc, 2)], tag=f"rc_{tt}")
        def f_pb(e):
            return e.tensor_scalar(out=sco[:, :], in0=sco[:, :], scalar1=scc[:, 2:3],
                                   scalar2=None, op0=ALU.mult)
        em.op("dve", f_pb, reads=[(sco, ALL), (scc, 2)], writes=[(sco, ALL)], tag=f"pb_{tt}")
        def f_m8(e):
            return e.max(out=scc[:, 0:8].rearrange("p (a b) -> p (a b)") if False else wsp[:, 120:128], in_=sco[:, :])
        em.op("dve", f_m8, reads=[(sco, ALL)], writes=[(wsp, "m8")], tag=f"m8_{tt}")
        def f_ke(e):
            return e.tensor_scalar(out=wsp[:, 112:120], in0=sco[:, :], scalar1=wsp[:, 121:122],
                                   scalar2=None, op0=ALU.is_ge)
        em.op("dve", f_ke, reads=[(sco, ALL), (wsp, "m8")], writes=[(wsp, "k")], tag=f"kg_{tt}")
        def f_wm(e):
            return e.tensor_mul(out=wsp[:, 112:120], in0=wsp[:, 112:120], in1=sco[:, :])
        em.op("dve", f_wm, reads=[(wsp, "k"), (sco, ALL)], writes=[(wsp, "w")], tag=f"wm_{tt}")
        for g in range(2):
            for j in range(4):
                def f_cp(e, g=g, j=j):
                    return e.tensor_copy(out=wsp[:, 32 * j:32 * j + 1],
                                         in_=wsp[:, 112 + 4 * g + j:112 + 4 * g + j + 1])
                em.op("dve", f_cp, reads=[(wsp, "w")], writes=[(wsp, ALL)], tag=f"cp_{tt}_{g}_{j}")
            def f_tp(e):
                return e.transpose(out=ps_t[:, 0:128], in_=wsp[:, 0:128], identity=ident_f[:, :])
            em.op("pe", f_tp, reads=[(wsp, ALL), ident_f], writes=[ps_t], tag=f"tp_{tt}_{g}")
            for j in range(4):
                ee = 4 * g + j
                def f_ev(e, j=j, ee=ee, tt=tt):
                    return e.tensor_copy(out=wrow[0:1, ee * TLOC + tt * 128:ee * TLOC + (tt + 1) * 128],
                                         in_=ps_t[32 * j:32 * j + 1, 0:128])
                em.op("dve", f_ev, reads=[ps_t], writes=[(wrow, (ee, tt))], tag=f"wev_{tt}_{ee}")

    THL = TLOC // 2
    acc = [em.sb(f"acc_{m2}", [128, THL]) for m2 in range(8)]
    w1b = [em.sb("w1b0", [128, 8 * 2048], BF16)]
    w2b = [em.sb("w2b0", [128, 16 * 1024], BF16)]
    ht = [em.sb(f"ht_{i}", [128, 512], BF16) for i in range(16)]
    wbc = em.sb("wbc", [128, 512])
    gbuf = em.sb("gbuf", [128, 512], BF16)
    sbuf2 = em.sb("sbuf2", [128, 512])
    xbuf2 = em.sb("xbuf2", [128, 512])
    for th in range(2):
        h0 = th * THL
        for m2 in range(8):
            def f_z(e, m2=m2):
                return e.memset(acc[m2][:, :], 0.0)
            em.op("dve", f_z, reads=[(acc[m2], ALL)], writes=[(acc[m2], ALL)], tag=f"accz_{th}_{m2}")
        for ei in range(9):
            shared = (ei == 8)
            e_ = ei if not shared else 0
            w1s, w2s = (w1h[e_], w2h[e_]) if not shared else (ws1h, ws2h)
            w1t, w2t = w1b[0], w2b[0]
            em.dma("pool", w1t[:, :], w1s[0:128, :], reads=[w1s], writes=[w1t], tag=f"ldw1_{th}_{ei}")
            em.dma("pool", w2t[:, :], w2s[0:128, :], reads=[w2s], writes=[w2t], tag=f"ldw2_{th}_{ei}")
            for tb in range(2):
                t0, t1 = h0 + tb * 512, h0 + (tb + 1) * 512
                a0, a1 = tb * 512, (tb + 1) * 512
                if not shared:
                    def f_wb(e, ei=ei, t0=t0, t1=t1):
                        return e.matmul(out=ps_b[:, 0:512], lhsT=ones1b[0:1, :],
                                        rhs=wrow[0:1, ei * TLOC + t0:ei * TLOC + t1],
                                        start=True, stop=True)
                    em.op("pe", f_wb, reads=[ones1b, (wrow, ALL)], writes=[ps_b], tag=f"wb_{th}_{ei}_{tb}")
                    def f_we(e):
                        return e.tensor_copy(out=wbc[:, :], in_=ps_b[:, 0:512])
                    em.op("dve", f_we, reads=[ps_b], writes=[wbc], tag=f"we_{th}_{ei}_{tb}")
                for hc in range(16):
                    ph = ps_h[hc % 2]
                    def f_h1(e, hc=hc, t0=t0, t1=t1, ph=ph, w1t=w1t):
                        last = None
                        for kc in range(8):
                            last = e.matmul(out=ph[:, 0:512],
                                            lhsT=w1t[:, kc * 2048 + hc * 128:kc * 2048 + (hc + 1) * 128],
                                            rhs=xn[kc][:, t0:t1], start=(kc == 0), stop=(kc == 7))
                        return last
                    em.op("pe", f_h1, reads=[(w1t, ALL)] + [(xn[k], ALL) for k in range(8)],
                          writes=[(ph, ALL)], tag=f"h1_{th}_{ei}_{tb}_{hc}")
                    if shared:
                        def f_ge(e, hc=hc, ph=ph):
                            return e.activation(out=ht[hc][:, :], in_=ph[:, 0:512], func=AF.Gelu,
                                                bias=sb1c[:, hc:hc + 1])
                        em.op("act", f_ge, reads=[(ph, ALL), sb1c], writes=[(ht[hc], ALL)],
                              tag=f"ge_{th}_{ei}_{tb}_{hc}")
                    else:
                        def f_ge(e, ei=ei, hc=hc, ph=ph):
                            return e.activation(out=ht[hc][:, :], in_=ph[:, 0:512], func=AF.Gelu,
                                                bias=rb1c[:, ei * 16 + hc:ei * 16 + hc + 1])
                        em.op("act", f_ge, reads=[(ph, ALL), rb1c], writes=[(ht[hc], ALL)],
                              tag=f"ge_{th}_{ei}_{tb}_{hc}")
                        def f_sc2(e, hc=hc):
                            return e.tensor_mul(out=ht[hc][:, :], in0=ht[hc][:, :], in1=wbc[:, :])
                        em.op("dve", f_sc2, reads=[(ht[hc], ALL), wbc], writes=[(ht[hc], ALL)],
                              tag=f"hs_{th}_{ei}_{tb}_{hc}")
                for m2 in range(8):
                    po = ps_o[m2 % 2]
                    def f_o(e, m2=m2, po=po, w2t=w2t, shared=shared, ei=ei, t0=t0, t1=t1):
                        last = None
                        for hc in range(16):
                            last = e.matmul(out=po[:, 0:512],
                                            lhsT=w2t[:, hc * 1024 + m2 * 128:hc * 1024 + (m2 + 1) * 128],
                                            rhs=ht[hc][:, :], start=(hc == 0),
                                            stop=(shared and hc == 15))
                        if not shared:
                            last = e.matmul(out=po[:, 0:512],
                                            lhsT=rb2row[0:1, ei * 1024 + m2 * 128:ei * 1024 + (m2 + 1) * 128],
                                            rhs=wrow[0:1, ei * TLOC + t0:ei * TLOC + t1],
                                            start=False, stop=True)
                        return last
                    em.op("pe", f_o, reads=[(w2t, ALL), rb2row, (wrow, ALL)] +
                          [(ht[hc], ALL) for hc in range(16)],
                          writes=[(po, ALL)], tag=f"o_{th}_{ei}_{tb}_{m2}")
                    def f_acc(e, m2=m2, po=po, a0=a0, a1=a1):
                        return e.tensor_add(out=acc[m2][:, a0:a1], in0=acc[m2][:, a0:a1],
                                            in1=po[:, 0:512])
                    em.op("dve", f_acc, reads=[(acc[m2], ("a", tb)), (po, ALL)],
                          writes=[(acc[m2], ("a", tb))], tag=f"acc_{th}_{ei}_{tb}_{m2}")
        for m2 in range(8):
            for sb_ in range(2):
                c0, c1 = sb_ * 512, (sb_ + 1) * 512
                def f_g(e, m2=m2, c0=c0, c1=c1):
                    return e.activation(out=gbuf[:, :], in_=acc[m2][:, c0:c1], func=AF.Silu,
                                        bias=sb2c[:, m2:m2 + 1])
                em.op("act", f_g, reads=[(acc[m2], ALL), sb2c], writes=[gbuf], tag=f"g_{th}_{m2}_{sb_}")
                em.dma("sp", sbuf2[:, :], s_in[m2 * 128:(m2 + 1) * 128, h0 + c0:h0 + c1],
                       reads=[s_in], writes=[sbuf2], tag=f"lds_{th}_{m2}_{sb_}")
                em.dma("sp", xbuf2[:, :], x_in[m2 * 128:(m2 + 1) * 128, h0 + c0:h0 + c1],
                       reads=[x_in], writes=[xbuf2], tag=f"ldx_{th}_{m2}_{sb_}")
                def f_sg(e):
                    return e.tensor_mul(out=sbuf2[:, :], in0=sbuf2[:, :], in1=gbuf[:, :])
                em.op("dve", f_sg, reads=[(sbuf2, "x"), gbuf], writes=[sbuf2], tag=f"sg_{th}_{m2}_{sb_}")
                def f_xn2(e):
                    return e.tensor_add(out=xbuf2[:, :], in0=xbuf2[:, :], in1=sbuf2[:, :])
                em.op("dve", f_xn2, reads=[(xbuf2, "x"), sbuf2], writes=[xbuf2], tag=f"xo_{th}_{m2}_{sb_}")
                em.dma("sp", xo_out[m2 * 128:(m2 + 1) * 128, h0 + c0:h0 + c1], xbuf2[:, :],
                       reads=[xbuf2], writes=[(xo_out, (th, m2, sb_))], tag=f"stxo_{th}_{m2}_{sb_}")
    em.barrier([xo_out])


def prep_l0g(c, H):
    return {"xn_in": np.ascontiguousarray(np.asarray(H["xn0"][c])),
            "s_in": np.ascontiguousarray(np.asarray(H["s0"][c], np.float32)),
            "x_in": fm_slice(H["x"], c)}


def post_l0g(results, H):
    H["xL1"] = [r["xo_out"] for r in results]


STAGE_L0G = Stage("l0g", build_l0g, prep_l0g, post_l0g)



def build_l1a(em, sc, H):
    W = H["weights"]
    x_in = em.dram("x_in", [C, TLOC], F32, kind="ExternalInput")
    ropec_in = em.dram("ropec_in", [128, TLOC], BF16, kind="ExternalInput")
    ropes_in = em.dram("ropes_in", [128, TLOC], BF16, kind="ExternalInput")
    xn_out = em.dram("xn_out", [C, TLOC], BF16, kind="ExternalOutput")
    q_out = em.dram("q_out", [1024, TLOC], BF16, kind="ExternalOutput")
    k_out = em.dram("k_out", [1024, TLOC], BF16, kind="ExternalOutput")
    v_out = em.dram("v_out", [1024, TLOC], BF16, kind="ExternalOutput")

    x_bf = load_fm_tiles(em, x_in, 8, TLOC, BF16, tag="xbf")
    xn = [em.sb(f"xn_{i}", [128, TLOC], BF16) for i in range(8)]
    ns = NormScratch(em, TLOC)
    emit_norm(em, sc, ns, x_bf, xn, 8, TLOC, rms=False, tag="ln1")
    for kc in range(8):
        em.dma("sp", xn_out[kc * 128:(kc + 1) * 128, :], xn[kc][:, :],
               reads=[(xn[kc], ALL)], writes=[(xn_out, kc)], tag=f"st_xn_{kc}")
    rc = em.sb("ropec", [128, TLOC], BF16)
    rs = em.sb("ropes", [128, TLOC], BF16)
    em.dma("sp", rc[:, :], ropec_in.ap(), reads=[ropec_in], writes=[rc], tag="ld_rc")
    em.dma("sp", rs[:, :], ropes_in.ap(), reads=[ropes_in], writes=[rs], tag="ld_rs")

    wq_d = sc.cin("Wq_r", np.ascontiguousarray(np.asarray(W["L1_attn_Wq"], np.float32)
                  .reshape(8, 128, 1024).transpose(1, 0, 2).reshape(128, 8 * 1024)))
    wk_d = sc.cin("Wk_r", np.ascontiguousarray(np.asarray(W["L1_attn_Wk"], np.float32)
                  .reshape(8, 128, 1024).transpose(1, 0, 2).reshape(128, 8 * 1024)))
    wv_d = sc.cin("Wv_r", np.ascontiguousarray(np.asarray(W["L1_attn_Wv"], np.float32)
                  .reshape(8, 128, 1024).transpose(1, 0, 2).reshape(128, 8 * 1024)))
    wbuf = em.sb("wbuf", [128, 8 * 1024], BF16)
    wq = WTile(wbuf, 1024, 1024)
    wk = wq
    wv = wq
    wsrc = {"q": wq_d, "k": wk_d, "v": wv_d}
    bqc = sc.const_tile("bqc", np.asarray(W["L1_attn_bq"], np.float32).reshape(8, 128).T)
    bkc = sc.const_tile("bkc", np.asarray(W["L1_attn_bk"], np.float32).reshape(8, 128).T)
    bvc = sc.const_tile("bvc", np.asarray(W["L1_attn_bv"], np.float32).reshape(8, 128).T)
    psums = [em.psum(f"pp{i}") for i in range(3)]
    qt = [em.sb(f"qt_{i}", [128, TLOC], BF16) for i in range(8)]
    shf = em.sb("shf", [128, TLOC], BF16)
    tmpr = em.sb("tmpr", [128, TLOC], BF16)
    rmask = [(i + 16) % 32 for i in range(32)]
    for which, wt, bc, outd, rope in (("q", wq, bqc, q_out, True), ("k", wk, bkc, k_out, True),
                                      ("v", wv, bvc, v_out, False)):
        em.dma("pool", wbuf[:, :], wsrc[which][0:128, :], reads=[wsrc[which]],
               writes=[wbuf], tag=f"ldw_{which}")
        ev = evac_copy(em, qt, eng="dve", bias_col=bc, tag=f"ev{which}")
        emit_proj(em, wt, xn, 1024, 1024, TLOC, psums, ev, tag=f"p{which}")
        for mc in range(8):
            if rope:
                def f_sh(e, mc=mc):
                    return e.stream_shuffle(out=shf[:, :], in_=qt[mc][:, :], mask=rmask)
                em.op("dve", f_sh, reads=[(qt[mc], ALL)], writes=[shf], tag=f"sh_{which}_{mc}")
                def f_m1(e, mc=mc):
                    return e.tensor_mul(out=shf[:, :], in0=shf[:, :], in1=rs[:, :])
                em.op("dve", f_m1, reads=[(shf, "x"), rs], writes=[shf], tag=f"m1_{which}_{mc}")
                def f_m2(e, mc=mc):
                    return e.tensor_mul(out=tmpr[:, :], in0=qt[mc][:, :], in1=rc[:, :])
                em.op("pool", f_m2, reads=[(qt[mc], ALL), rc], writes=[tmpr], tag=f"m2_{which}_{mc}")
                def f_m3(e, mc=mc):
                    return e.tensor_add(out=qt[mc][:, :], in0=tmpr[:, :], in1=shf[:, :])
                em.op("dve", f_m3, reads=[tmpr, shf], writes=[(qt[mc], ALL)], tag=f"m3_{which}_{mc}")
            em.dma("sp", outd[mc * 128:(mc + 1) * 128, :], qt[mc][:, :],
                   reads=[(qt[mc], ALL)], writes=[(outd, mc)], tag=f"st_{which}_{mc}")
    em.barrier([xn_out, q_out, k_out, v_out])


def l1_rope_tables(c):
    half = c % 2
    pos = np.arange(half * TLOC, (half + 1) * TLOC, dtype=np.float64)
    inv = 1.0 / (10000.0 ** (np.arange(0, 64, 2, dtype=np.float64) / 64))
    fr = np.outer(pos, inv)
    cosv, sinv = np.cos(fr), np.sin(fr)
    Ct = np.ones((128, TLOC), np.float64)
    St = np.zeros((128, TLOC), np.float64)
    for hh in range(2):
        base = hh * 64
        for r in range(16):
            Ct[base + r] = cosv[:, r]; St[base + r] = -sinv[:, r]
            Ct[base + 16 + r] = cosv[:, 16 + r]; St[base + 16 + r] = -sinv[:, 16 + r]
    return Ct.astype(BF16NP), St.astype(BF16NP)


def prep_l1a(c, H):
    Ct, St = l1_rope_tables(c)
    return {"x_in": np.ascontiguousarray(np.asarray(H["xL1"][c], np.float32)),
            "ropec_in": Ct, "ropes_in": St}


def post_l1a(results, H):
    H["xn1"] = [r["xn_out"] for r in results]
    H["q1"] = [r["q_out"] for r in results]
    H["k1"] = [r["k_out"] for r in results]
    H["v1"] = [r["v_out"] for r in results]


STAGE_L1A = Stage("l1a", build_l1a, prep_l1a, post_l1a)


def build_l1b(em, sc, H):
    TF = T
    q_in = em.dram("q_in", [512, TF], BF16, kind="ExternalInput")
    k_in = em.dram("k_in", [512, TF], BF16, kind="ExternalInput")
    v_in = em.dram("v_in", [512, TF], BF16, kind="ExternalInput")
    ctx_out = em.dram("ctx_out", [512, TF], BF16, kind="ExternalOutput")

    ident_b = sc.const_tile("ident_b", np.eye(128).astype(BF16NP), BF16)
    ones1 = sc.ones1()
    onescol_b = sc.ones_col_bf()
    BIG = 30000.0
    masks = []
    for di in range(4):
        d = di * 128
        m = np.zeros((128, 512), np.float32)
        for r in range(128):
            m[r, :] = np.where(np.arange(512) >= r + d, 0.0, -BIG)
        masks.append(sc.const_tile(f"cmask{di}", m))

    q = load_fm_tiles(em, q_in, 4, TF, BF16, tag="q")
    kt = load_fm_tiles(em, k_in, 4, TF, BF16, tag="k")
    vt = load_fm_tiles(em, v_in, 4, TF, BF16, tag="v")

    ps_tr = em.psum("ps_tr", (128, 1024), BF16)
    ps_s = [em.psum(f"ps_s{i}") for i in range(3)]
    ps_av = em.psum("ps_av")
    ps_b = em.psum("ps_b")
    ps_m = em.psum("ps_m")

    NTK = TF // 128
    vtok = em.sb("vtok", [128, NTK * 65], BF16)
    sqb = em.sb("sqb", [128, TF], BF16)
    qn = em.sb("qn", [1, TF])
    mrow = em.sb("mrow", [1, 24])
    nbcol = em.sb("nbcol", [128, 8])
    pt = [em.sb(f"pt{i}", [128, 512], BF16) for i in range(4)]
    msk = [em.sb(f"msk{i}", [128, 512]) for i in range(4)]
    ps4 = ps_s + [ps_m]
    dn = em.sb("dn", [1, 512])
    rdbc = em.sb("rdbc", [64, 512])
    ctx = [em.sb(f"ctx_{i}", [128, TF], BF16) for i in range(4)]

    for h in range(8):
        base = 64 * (h % 2)
        qtile, ktile, vtile = q[h // 2], kt[h // 2], vt[h // 2]
        def f_vz(e):
            return e.memset(vtok[:, :], 1.0)
        em.op("dve", f_vz, reads=[(vtok, ALL)], writes=[(vtok, ALL)], tag=f"vz_{h}")
        for i0 in range(0, NTK, 4):
            for j in range(4):
                i = i0 + j
                def f_vt(e, i=i, j=j, vtile=vtile, base=base):
                    return e.transpose(out=ps_tr[:, j * 128:j * 128 + 64],
                                       in_=vtile[base:base + 64, i * 128:(i + 1) * 128],
                                       identity=ident_b[base:base + 64, base:base + 64])
                em.op("pe", f_vt, reads=[(vtile, ALL), ident_b], writes=[(ps_tr, ALL)],
                      tag=f"vt_{h}_{i}")
            for j in range(4):
                i = i0 + j
                def f_ve(e, i=i, j=j):
                    return e.tensor_copy(out=vtok[:, i * 65:i * 65 + 64],
                                         in_=ps_tr[:, j * 128:j * 128 + 64])
                em.op("dve", f_ve, reads=[(ps_tr, ALL)], writes=[(vtok, i)], tag=f"ve_{h}_{i}")
        for which, tile, mi in (("k", ktile, 8 + h), ("q", qtile, h)):
            def f_sq(e, tile=tile, base=base):
                return e.activation(out=sqb[base:base + 64, :], in_=tile[base:base + 64, :],
                                    func=AF.Square)
            em.op("act", f_sq, reads=[(tile, ALL)], writes=[(sqb, h)], tag=f"sq_{which}_{h}")
            for tb in range(8):
                def f_s(e, tb=tb, base=base):
                    return e.matmul(out=ps_m[0:1, 0:512], lhsT=onescol_b[base:base + 64, 0:1],
                                    rhs=sqb[base:base + 64, tb * 512:(tb + 1) * 512],
                                    start=True, stop=True)
                em.op("pe", f_s, reads=[onescol_b, (sqb, h)], writes=[ps_m], tag=f"s_{which}_{h}_{tb}")
                def f_e(e, tb=tb):
                    return e.tensor_copy(out=qn[0:1, tb * 512:(tb + 1) * 512],
                                         in_=ps_m[0:1, 0:512])
                em.op("dve", f_e, reads=[ps_m], writes=[(qn, tb)], tag=f"e_{which}_{h}_{tb}")
            def f_mx(e, mi=mi):
                return e.tensor_reduce(out=mrow[0:1, mi:mi + 1], in_=qn[0:1, :], axis=AX.X,
                                       op=ALU.max)
            em.op("dve", f_mx, reads=[qn], writes=[(mrow, mi)], tag=f"mx_{which}_{h}")
        def f_bm(e, h=h):
            return e.tensor_mul(out=mrow[0:1, 16 + h:17 + h], in0=mrow[0:1, h:h + 1],
                                in1=mrow[0:1, 8 + h:9 + h])
        em.op("dve", f_bm, reads=[(mrow, h), (mrow, 8 + h)], writes=[(mrow, 16 + h)], tag=f"bm_{h}")
        def f_bs(e, h=h):
            return e.activation(out=mrow[0:1, 16 + h:17 + h], in_=mrow[0:1, 16 + h:17 + h],
                                func=AF.Sqrt)
        em.op("act", f_bs, reads=[(mrow, 16 + h)], writes=[(mrow, 16 + h)], tag=f"bs_{h}")
        def f_bn(e, h=h):
            return e.tensor_scalar_mul(out=mrow[0:1, 16 + h:17 + h], in0=mrow[0:1, 16 + h:17 + h],
                                       scalar1=-0.125)
        em.op("dve", f_bn, reads=[(mrow, 16 + h)], writes=[(mrow, 16 + h)], tag=f"bn_{h}")
        def f_bb(e, h=h):
            return e.matmul(out=ps_m[:, 256:257], lhsT=ones1[0:1, :],
                            rhs=mrow[0:1, 16 + h:17 + h], start=True, stop=True)
        em.op("pe", f_bb, reads=[ones1, (mrow, 16 + h)], writes=[ps_m], tag=f"bb_{h}")
        def f_be(e, h=h):
            return e.tensor_copy(out=nbcol[:, h:h + 1], in_=ps_m[:, 256:257])
        em.op("dve", f_be, reads=[ps_m], writes=[(nbcol, h)], tag=f"be_{h}")
        for tb in range(8):
            ntk = 4 * tb + 4
            def emit_av(pair, ntk=ntk, h=h, tb=tb):
                pts = [pt[i % 4] for i in pair]
                def f_av(e, pair=pair, ntk=ntk, pts=pts):
                    last = None
                    for i, ptb in zip(pair, pts):
                        last = e.matmul(out=ps_av[0:65, 0:512],
                                        lhsT=vtok[:, i * 65:i * 65 + 65],
                                        rhs=ptb[:, :], start=(i == 0), stop=(i == ntk - 1))
                    return last
                em.op("pe", f_av, reads=[(vtok, ALL)] + [(p, ALL) for p in pts],
                      writes=[(ps_av, ALL)], tag=f"av_{h}_{tb}_{pair[0]}")
            pendq = []
            for u in range(0, ntk, 2):
                pair = [i for i in (u, u + 1) if i < ntk]
                psl = [ps4[i % 4] for i in pair]
                ptl = [pt[i % 4] for i in pair]
                mkl = [msk[i % 4] for i in pair]
                def f_sc(e, pair=pair, tb=tb, base=base, qtile=qtile, ktile=ktile, psl=psl):
                    last = None
                    for i, pss in zip(pair, psl):
                        last = e.matmul(out=pss[:, 0:512],
                                        lhsT=ktile[base:base + 64, i * 128:(i + 1) * 128],
                                        rhs=qtile[base:base + 64, tb * 512:(tb + 1) * 512],
                                        start=True, stop=True)
                    return last
                em.op("pe", f_sc, reads=[(ktile, ALL), (qtile, ALL)],
                      writes=[(p, ALL) for p in psl], tag=f"sc_{h}_{tb}_{u}")
                diags = [i for i in pair if i >= 4 * tb]
                if diags:
                    def f_mk(e, pair=pair, tb=tb, psl=psl, mkl=mkl):
                        last = None
                        for i, pss, mkb in zip(pair, psl, mkl):
                            if i >= 4 * tb:
                                last = e.tensor_add(out=mkb[:, :], in0=pss[:, 0:512],
                                                    in1=masks[i - 4 * tb][:, :])
                        return last
                    em.op("dve", f_mk, reads=[(p, ALL) for p in psl] +
                          [masks[i - 4 * tb] for i in diags],
                          writes=[(msk[i % 4], ALL) for i in diags], tag=f"mk_{h}_{tb}_{u}")
                def f_ex(e, pair=pair, tb=tb, h=h, psl=psl, ptl=ptl, mkl=mkl):
                    last = None
                    for i, pss, ptb, mkb in zip(pair, psl, ptl, mkl):
                        srcap = mkb[:, :] if i >= 4 * tb else pss[:, 0:512]
                        last = e.activation(out=ptb[:, :], in_=srcap, func=AF.Exp,
                                            bias=nbcol[:, h:h + 1], scale=0.125)
                    return last
                em.op("act", f_ex, reads=[(p, ALL) for p in psl] +
                      [(msk[i % 4], ALL) for i in diags] + [(nbcol, h)],
                      writes=[(p, ALL) for p in ptl], tag=f"ex_{h}_{tb}_{u}")
                if pendq:
                    emit_av(pendq.pop(0))
                pendq.append(pair)
            for pp in pendq:
                emit_av(pp)
            def f_dn(e):
                return e.tensor_copy(out=dn[0:1, :], in_=ps_av[64:65, 0:512])
            em.op("dve", f_dn, reads=[(ps_av, ALL)], writes=[dn], tag=f"dn_{h}_{tb}")
            def f_rd(e):
                return e.reciprocal(out=dn[0:1, :], in_=dn[0:1, :])
            em.op("dve", f_rd, reads=[(dn, "x")], writes=[dn], tag=f"rd_{h}_{tb}")
            def f_bc(e):
                return e.matmul(out=ps_b[0:64, 0:512], lhsT=ones1[0:1, 0:64],
                                rhs=dn[0:1, :], start=True, stop=True)
            em.op("pe", f_bc, reads=[ones1, dn], writes=[ps_b], tag=f"bc_{h}_{tb}")
            def f_rde(e):
                return e.tensor_copy(out=rdbc[:, :], in_=ps_b[0:64, 0:512])
            em.op("dve", f_rde, reads=[ps_b], writes=[rdbc], tag=f"rde_{h}_{tb}")
            def f_cx(e, h=h, tb=tb, base=base):
                return e.tensor_mul(out=ctx[h // 2][base:base + 64, tb * 512:(tb + 1) * 512],
                                    in0=ps_av[0:64, 0:512], in1=rdbc[:, :])
            em.op("dve", f_cx, reads=[(ps_av, ALL), rdbc],
                  writes=[(ctx[h // 2], (h, tb))], tag=f"cx_{h}_{tb}")
    for i in range(4):
        em.dma("sp", ctx_out[i * 128:(i + 1) * 128, :], ctx[i][:, :],
               reads=[(ctx[i], ALL)], writes=[(ctx_out, i)], tag=f"st_ctx_{i}")
    em.barrier([ctx_out])


def prep_l1b(c, H):
    b, hh = c // 2, c % 2
    def cat(key):
        a = np.asarray(H[key][2 * b]); b2 = np.asarray(H[key][2 * b + 1])
        return np.ascontiguousarray(np.concatenate([a[hh * 512:(hh + 1) * 512, :],
                                                    b2[hh * 512:(hh + 1) * 512, :]], axis=1))
    return {"q_in": cat("q1"), "k_in": cat("k1"), "v_in": cat("v1")}


def post_l1b(results, H):
    H["ctx1"] = [r["ctx_out"] for r in results]


STAGE_L1B = Stage("l1b", build_l1b, prep_l1b, post_l1b)


def build_l1c(em, sc, H):
    W = H["weights"]
    ctx_in = em.dram("ctx_in", [1024, TLOC], BF16, kind="ExternalInput")
    s_out = em.dram("s_out_c", [1024, TLOC], F32, kind="ExternalOutput")
    ctx = load_fm_tiles(em, ctx_in, 8, TLOC, BF16, tag="ctx")
    wd = sc.weight_bf16("W_d", np.asarray(W["L1_attn_Wd"], np.float32))
    bdc = sc.const_tile("bdc", np.asarray(W["L1_attn_bd"], np.float32).reshape(8, 128).T)
    psums = [em.psum(f"pp{i}") for i in range(3)]
    proj_to_dram(em, sc, wd, ctx, 1024, 1024, TLOC, s_out, psums, "pd",
                 bias_col=bdc, out_dtype=F32)
    em.barrier([s_out])


def prep_l1c(c, H):
    b, half = c // 2, c % 2
    sl = slice(half * TLOC, (half + 1) * TLOC)
    ctx = np.concatenate([np.asarray(H["ctx1"][2 * b])[:, sl],
                          np.asarray(H["ctx1"][2 * b + 1])[:, sl]], axis=0)
    return {"ctx_in": np.ascontiguousarray(ctx)}


def post_l1c(results, H):
    H["s1"] = [r["s_out_c"] for r in results]


STAGE_L1C = Stage("l1c", build_l1c, prep_l1c, post_l1c)


def build_l1d(em, sc, H):
    W = H["weights"]
    BIGG = 20000.0
    xn_in = em.dram("xn_in", [1024, TLOC], BF16, kind="ExternalInput")
    s_in = em.dram("s_in", [1024, TLOC], F32, kind="ExternalInput")
    x_in = em.dram("x_in", [1024, TLOC], F32, kind="ExternalInput")
    xo_out = em.dram("xo_out", [1024, TLOC], F32, kind="ExternalOutput")

    ones1 = sc.ones1()
    ones1b = sc.const_tile("ones1b", np.ones((1, 128), BF16NP), BF16)
    ident_f = sc.const_tile("ident_f", np.eye(128, dtype=np.float32))
    gbt = sc.cin("gate_b_t", np.tile(np.asarray(W["L1_ffn_gate_b"], np.float32)[None, :],
                                     (128, 1)))
    gbt_sb = em.sb("gbt_sb", [128, 16])
    em.dma("sp", gbt_sb[:, :], gbt.ap(), reads=[gbt], writes=[gbt_sb], tag="ld_gbt")
    be1c = sc.const_tile("be1c", np.ascontiguousarray(
        np.asarray(W["L1_ffn_be1"], np.float32).reshape(16, 4, 128).transpose(2, 0, 1).reshape(128, 64)))
    be3c = sc.const_tile("be3c", np.ascontiguousarray(
        np.asarray(W["L1_ffn_be3"], np.float32).reshape(16, 4, 128).transpose(2, 0, 1).reshape(128, 64)))
    bs1c = sc.const_tile("bs1c", np.asarray(W["L1_ffn_bs1"], np.float32).reshape(8, 128).T)
    bs3c = sc.const_tile("bs3c", np.asarray(W["L1_ffn_bs3"], np.float32).reshape(8, 128).T)
    bs2c = sc.const_tile("bs2c", np.asarray(W["L1_ffn_bs2"], np.float32).reshape(8, 128).T)
    be2row = sc.const_tile("be2row", np.asarray(W["L1_ffn_be2"], np.float32)
                           .reshape(1, 16 * 1024).astype(BF16NP), BF16)

    xn = load_fm_tiles(em, xn_in, 8, TLOC, BF16, tag="xn")
    gw = sc.weight_bf16("W_gate", np.ascontiguousarray(np.asarray(W["L1_ffn_gate_w"], np.float32).T))
    we1h = [sc.cin(f"We1r_{e}", np.ascontiguousarray(
            np.asarray(W["L1_ffn_We1"], np.float32)[e].reshape(8, 128, 512)
            .transpose(1, 0, 2).reshape(128, 8 * 512))) for e in range(16)]
    we3h = [sc.cin(f"We3r_{e}", np.ascontiguousarray(
            np.asarray(W["L1_ffn_We3"], np.float32)[e].reshape(8, 128, 512)
            .transpose(1, 0, 2).reshape(128, 8 * 512))) for e in range(16)]
    we2h = [sc.cin(f"We2r_{e}", np.ascontiguousarray(
            np.asarray(W["L1_ffn_We2"], np.float32)[e].reshape(4, 128, 1024)
            .transpose(1, 0, 2).reshape(128, 4 * 1024))) for e in range(16)]
    ws1h = sc.cin("Ws1r", np.ascontiguousarray(
        np.asarray(W["L1_ffn_ws1"], np.float32).reshape(8, 128, 1024).transpose(1, 0, 2).reshape(128, 8 * 1024)))
    ws3h = sc.cin("Ws3r", np.ascontiguousarray(
        np.asarray(W["L1_ffn_ws3"], np.float32).reshape(8, 128, 1024).transpose(1, 0, 2).reshape(128, 8 * 1024)))
    ws2h = sc.cin("Ws2r", np.ascontiguousarray(
        np.asarray(W["L1_ffn_ws2"], np.float32).reshape(8, 128, 1024).transpose(1, 0, 2).reshape(128, 8 * 1024)))

    ps_r = em.psum("ps_r")
    ps_t = em.psum("ps_t")
    ps_h = [em.psum(f"ps_h{i}") for i in range(2)]
    ps_o = [em.psum(f"ps_o{i}") for i in range(2)]
    ps_b = em.psum("ps_b")

    sco = em.sb("sco", [128, 16])
    sc2 = em.sb("sc2", [128, 16])
    scc = em.sb("scc", [128, 16])
    keep16 = em.sb("keep16", [128, 16])
    gsc = em.sb("gsc", [128, 16])
    wsp = em.sb("wsp", [128, 128])
    wrow = em.sb("wrow", [1, 16 * TLOC], BF16)
    em.op("dve", lambda e: e.memset(wsp[:, :], 0.0), reads=[], writes=[wsp], tag="wsp0")
    for tt in range(16):
        def f_rt(e, tt=tt):
            last = None
            for kc in range(8):
                last = e.matmul(out=ps_r[0:128, 0:16], lhsT=xn[kc][:, tt * 128:(tt + 1) * 128],
                                rhs=gw.sl(kc, 0, 16), start=(kc == 0), stop=(kc == 7))
            return last
        em.op("pe", f_rt, reads=[(gw.buf, ALL)] + [(xn[k], ALL) for k in range(8)],
              writes=[ps_r], tag=f"rt_{tt}")
        def f_re(e):
            return e.tensor_copy(out=sco[:, :], in_=ps_r[0:128, 0:16])
        em.op("dve", f_re, reads=[ps_r], writes=[(sco, ALL)], tag=f"re_{tt}")
        def f_mx(e):
            return e.tensor_reduce(out=scc[:, 0:1], in_=sco[:, :], axis=AX.X, op=ALU.max,
                                   negate=True)
        em.op("dve", f_mx, reads=[(sco, ALL)], writes=[(scc, 0)], tag=f"mx_{tt}")
        def f_ex(e):
            return e.activation(out=sco[:, :], in_=sco[:, :], func=AF.Exp, bias=scc[:, 0:1])
        em.op("act", f_ex, reads=[(sco, ALL), (scc, 0)], writes=[(sco, ALL)], tag=f"exr_{tt}")
        def f_sm(e):
            return e.tensor_reduce(out=scc[:, 1:2], in_=sco[:, :], axis=AX.X, op=ALU.add)
        em.op("dve", f_sm, reads=[(sco, ALL)], writes=[(scc, 1)], tag=f"sm_{tt}")
        def f_rc(e):
            return e.reciprocal(out=scc[:, 2:3], in_=scc[:, 1:2])
        em.op("dve", f_rc, reads=[(scc, 1)], writes=[(scc, 2)], tag=f"rc_{tt}")
        def f_pb(e):
            return e.tensor_scalar(out=sco[:, :], in0=sco[:, :], scalar1=scc[:, 2:3],
                                   scalar2=None, op0=ALU.mult)
        em.op("dve", f_pb, reads=[(sco, ALL), (scc, 2)], writes=[(sco, ALL)], tag=f"pb_{tt}")
        def f_sb(e):
            return e.tensor_add(out=sc2[:, :], in0=sco[:, :], in1=gbt_sb[:, :])
        em.op("dve", f_sb, reads=[(sco, ALL), gbt_sb], writes=[(sc2, ALL)], tag=f"sb_{tt}")
        def f_gp(e):
            return e.memset(gsc[:, 4:8], -BIGG)
        em.op("dve", f_gp, reads=[], writes=[(gsc, "pad")], tag=f"gp_{tt}")
        for g in range(4):
            def f_gc(e, g=g):
                return e.tensor_copy(out=gsc[:, 0:4], in_=sc2[:, 4 * g:4 * g + 4])
            em.op("dve", f_gc, reads=[(sc2, ALL)], writes=[(gsc, "lo")], tag=f"gc_{tt}_{g}")
            def f_g8(e, g=g):
                return e.max(out=gsc[:, 8:16], in_=gsc[:, 0:8])
            em.op("dve", f_g8, reads=[(gsc, "lo"), (gsc, "pad")], writes=[(gsc, "hi")],
                  tag=f"g8_{tt}_{g}")
            def f_ga(e, g=g):
                return e.tensor_add(out=scc[:, 4 + g:5 + g], in0=gsc[:, 8:9], in1=gsc[:, 9:10])
            em.op("dve", f_ga, reads=[(gsc, "hi")], writes=[(scc, "gs")], tag=f"ga_{tt}_{g}")
        def f_gc2(e):
            return e.tensor_copy(out=gsc[:, 0:4], in_=scc[:, 4:8])
        em.op("dve", f_gc2, reads=[(scc, "gs")], writes=[(gsc, "lo")], tag=f"gc2_{tt}")
        def f_gm(e):
            return e.max(out=scc[:, 8:16], in_=gsc[:, 0:8])
        em.op("dve", f_gm, reads=[(gsc, "lo"), (gsc, "pad")], writes=[(scc, "hi")], tag=f"gm_{tt}")
        def f_kg(e):
            return e.tensor_scalar(out=scc[:, 4:8], in0=scc[:, 4:8], scalar1=scc[:, 9:10],
                                   scalar2=None, op0=ALU.is_ge)
        em.op("dve", f_kg, reads=[(scc, "gs"), (scc, "hi")], writes=[(scc, "gs")], tag=f"kg_{tt}")
        for g in range(4):
            for j in range(4):
                def f_ek(e, g=g, j=j):
                    return e.tensor_copy(out=keep16[:, 4 * g + j:4 * g + j + 1],
                                         in_=scc[:, 4 + g:5 + g])
                em.op("dve", f_ek, reads=[(scc, "gs")], writes=[(keep16, g)], tag=f"ek_{tt}_{g}_{j}")
        def f_mk1(e):
            return e.tensor_scalar_add(out=sc2[:, :], in0=sc2[:, :], scalar1=BIGG)
        em.op("dve", f_mk1, reads=[(sc2, ALL)], writes=[(sc2, ALL)], tag=f"mk1_{tt}")
        def f_mk2(e):
            return e.tensor_mul(out=sc2[:, :], in0=sc2[:, :], in1=keep16[:, :])
        em.op("dve", f_mk2, reads=[(sc2, ALL), (keep16, 0), (keep16, 1), (keep16, 2),
                                   (keep16, 3)], writes=[(sc2, ALL)], tag=f"mk2_{tt}")
        def f_mk3(e):
            return e.tensor_scalar_add(out=sc2[:, :], in0=sc2[:, :], scalar1=-BIGG)
        em.op("dve", f_mk3, reads=[(sc2, ALL)], writes=[(sc2, ALL)], tag=f"mk3_{tt}")
        def f_m8(e):
            return e.max(out=wsp[:, 120:128], in_=sc2[:, :])
        em.op("dve", f_m8, reads=[(sc2, ALL)], writes=[(wsp, "m8")], tag=f"m8_{tt}")
        def f_ke(e):
            return e.tensor_scalar(out=wsp[:, 96:112], in0=sc2[:, :], scalar1=wsp[:, 121:122],
                                   scalar2=None, op0=ALU.is_ge)
        em.op("dve", f_ke, reads=[(sc2, ALL), (wsp, "m8")], writes=[(wsp, "k")], tag=f"ke_{tt}")
        def f_wm(e):
            return e.tensor_mul(out=wsp[:, 96:112], in0=wsp[:, 96:112], in1=sco[:, :])
        em.op("dve", f_wm, reads=[(wsp, "k"), (sco, ALL)], writes=[(wsp, "w")], tag=f"wm_{tt}")
        for g in range(4):
            for j in range(4):
                def f_cp(e, g=g, j=j):
                    return e.tensor_copy(out=wsp[:, 32 * j:32 * j + 1],
                                         in_=wsp[:, 96 + 4 * g + j:97 + 4 * g + j])
                em.op("dve", f_cp, reads=[(wsp, "w")], writes=[(wsp, ALL)], tag=f"cp_{tt}_{g}_{j}")
            def f_tp(e):
                return e.transpose(out=ps_t[:, 0:128], in_=wsp[:, 0:128], identity=ident_f[:, :])
            em.op("pe", f_tp, reads=[(wsp, ALL), ident_f], writes=[ps_t], tag=f"tp_{tt}_{g}")
            for j in range(4):
                ee = 4 * g + j
                def f_ev(e, j=j, ee=ee, tt=tt):
                    return e.tensor_copy(out=wrow[0:1, ee * TLOC + tt * 128:ee * TLOC + (tt + 1) * 128],
                                         in_=ps_t[32 * j:32 * j + 1, 0:128])
                em.op("dve", f_ev, reads=[ps_t], writes=[(wrow, (ee, tt))], tag=f"wev_{tt}_{ee}")

    THL = TLOC // 4
    acc = [em.sb(f"acc_{m2}", [128, THL]) for m2 in range(8)]
    w1b = em.sb("w1b0", [128, 8 * 1024], BF16)
    w3b = em.sb("w3b0", [128, 8 * 1024], BF16)
    w2b = em.sb("w2b0", [128, 8 * 1024], BF16)
    ht = [em.sb(f"ht_{i}", [128, 512], BF16) for i in range(8)]
    h3t = em.sb("h3t", [128, 512])
    wbc = em.sb("wbc", [128, 512])
    gbuf = ht[0]
    sbuf2 = h3t
    xbuf2 = wbc

    for th in range(4):
        h0 = th * THL
        for m2 in range(8):
            def f_z(e, m2=m2):
                return e.memset(acc[m2][:, :], 0.0)
            em.op("dve", f_z, reads=[(acc[m2], ALL)], writes=[(acc[m2], ALL)], tag=f"az_{th}_{m2}")
        for ei in range(17):
            shared = (ei == 16)
            nhc = 8 if shared else 4
            if not shared:
                em.dma("pool", w1b[:, 0:8 * 512], we1h[ei][0:128, :], reads=[we1h[ei]],
                       writes=[w1b], tag=f"lw1_{th}_{ei}")
                em.dma("pool", w3b[:, 0:8 * 512], we3h[ei][0:128, :], reads=[we3h[ei]],
                       writes=[w3b], tag=f"lw3_{th}_{ei}")
                em.dma("pool", w2b[:, 0:4 * 1024], we2h[ei][0:128, :], reads=[we2h[ei]],
                       writes=[w2b], tag=f"lw2_{th}_{ei}")
            else:
                em.dma("pool", w1b[:, :], ws1h[0:128, :], reads=[ws1h], writes=[w1b],
                       tag=f"lsw1_{th}")
                em.dma("pool", w3b[:, :], ws3h[0:128, :], reads=[ws3h], writes=[w3b],
                       tag=f"lsw3_{th}")
                em.dma("pool", w2b[:, :], ws2h[0:128, :], reads=[ws2h], writes=[w2b],
                       tag=f"lsw2_{th}")
            w1t, w3t, w2t = w1b, w3b, w2b
            h1w = 1024 if shared else 512
            for tb in range(1):
                t0, t1 = h0 + tb * 512, h0 + (tb + 1) * 512
                a0, a1 = tb * 512, (tb + 1) * 512
                if not shared:
                    def f_wb(e, ei=ei, t0=t0, t1=t1):
                        return e.matmul(out=ps_b[:, 0:512], lhsT=ones1b[0:1, :],
                                        rhs=wrow[0:1, ei * TLOC + t0:ei * TLOC + t1],
                                        start=True, stop=True)
                    em.op("pe", f_wb, reads=[ones1b, (wrow, ALL)], writes=[ps_b],
                          tag=f"wb_{th}_{ei}_{tb}")
                    def f_we(e):
                        return e.tensor_copy(out=wbc[:, :], in_=ps_b[:, 0:512])
                    em.op("dve", f_we, reads=[ps_b], writes=[wbc], tag=f"we_{th}_{ei}_{tb}")
                for hc in range(nhc):
                    ph = ps_h[hc % 2]
                    def f_h1(e, hc=hc, t0=t0, t1=t1, ph=ph, w1t=w1t, h1w=h1w):
                        last = None
                        for kc in range(8):
                            last = e.matmul(out=ph[:, 0:512],
                                            lhsT=w1t[:, kc * h1w + hc * 128:kc * h1w + (hc + 1) * 128],
                                            rhs=xn[kc][:, t0:t1], start=(kc == 0), stop=(kc == 7))
                        return last
                    em.op("pe", f_h1, reads=[(w1t, ALL)] + [(xn[k], ALL) for k in range(8)],
                          writes=[(ph, ALL)], tag=f"h1_{th}_{ei}_{tb}_{hc}")
                    bc1 = bs1c if shared else be1c
                    bi1 = hc if shared else ei * 4 + hc
                    def f_si(e, hc=hc, ph=ph, bc1=bc1, bi1=bi1):
                        return e.activation(out=ht[hc][:, :], in_=ph[:, 0:512], func=AF.Silu,
                                            bias=bc1[:, bi1:bi1 + 1])
                    em.op("act", f_si, reads=[(ph, ALL), bc1], writes=[(ht[hc], ALL)],
                          tag=f"si_{th}_{ei}_{tb}_{hc}")
                    ph3 = ps_h[(hc + 1) % 2]
                    def f_h3(e, hc=hc, t0=t0, t1=t1, ph3=ph3, w3t=w3t, h1w=h1w):
                        last = None
                        for kc in range(8):
                            last = e.matmul(out=ph3[:, 0:512],
                                            lhsT=w3t[:, kc * h1w + hc * 128:kc * h1w + (hc + 1) * 128],
                                            rhs=xn[kc][:, t0:t1], start=(kc == 0), stop=(kc == 7))
                        return last
                    em.op("pe", f_h3, reads=[(w3t, ALL)] + [(xn[k], ALL) for k in range(8)],
                          writes=[(ph3, ALL)], tag=f"h3_{th}_{ei}_{tb}_{hc}")
                    bc3 = bs3c if shared else be3c
                    def f_b3(e, hc=hc, ph3=ph3, bc3=bc3, bi1=bi1):
                        return e.tensor_scalar(out=h3t[:, :], in0=ph3[:, 0:512],
                                               scalar1=bc3[:, bi1:bi1 + 1], scalar2=None,
                                               op0=ALU.add)
                    em.op("dve", f_b3, reads=[(ph3, ALL), bc3], writes=[h3t],
                          tag=f"b3_{th}_{ei}_{tb}_{hc}")
                    def f_hp(e, hc=hc):
                        return e.tensor_mul(out=ht[hc][:, :], in0=ht[hc][:, :], in1=h3t[:, :])
                    em.op("dve", f_hp, reads=[(ht[hc], ALL), h3t], writes=[(ht[hc], ALL)],
                          tag=f"hp_{th}_{ei}_{tb}_{hc}")
                    if not shared:
                        def f_hw(e, hc=hc):
                            return e.tensor_mul(out=ht[hc][:, :], in0=ht[hc][:, :], in1=wbc[:, :])
                        em.op("dve", f_hw, reads=[(ht[hc], "w"), wbc], writes=[(ht[hc], ALL)],
                              tag=f"hw_{th}_{ei}_{tb}_{hc}")
                for m2 in range(8):
                    po = ps_o[m2 % 2]
                    def f_o(e, m2=m2, po=po, w2t=w2t, shared=shared, ei=ei, t0=t0, t1=t1, nhc=nhc):
                        last = None
                        for hc in range(nhc):
                            last = e.matmul(out=po[:, 0:512],
                                            lhsT=w2t[:, hc * 1024 + m2 * 128:hc * 1024 + (m2 + 1) * 128],
                                            rhs=ht[hc][:, :], start=(hc == 0),
                                            stop=(shared and hc == nhc - 1))
                        if not shared:
                            last = e.matmul(out=po[:, 0:512],
                                            lhsT=be2row[0:1, ei * 1024 + m2 * 128:ei * 1024 + (m2 + 1) * 128],
                                            rhs=wrow[0:1, ei * TLOC + t0:ei * TLOC + t1],
                                            start=False, stop=True)
                        return last
                    em.op("pe", f_o, reads=[(w2t, ALL), be2row, (wrow, ALL)] +
                          [(ht[hc], ALL) for hc in range(nhc)],
                          writes=[(po, ALL)], tag=f"o_{th}_{ei}_{tb}_{m2}")
                    def f_acc(e, m2=m2, po=po, a0=a0, a1=a1):
                        return e.tensor_add(out=acc[m2][:, a0:a1], in0=acc[m2][:, a0:a1],
                                            in1=po[:, 0:512])
                    em.op("dve", f_acc, reads=[(acc[m2], ("a", tb)), (po, ALL)],
                          writes=[(acc[m2], ("a", tb))], tag=f"ac_{th}_{ei}_{tb}_{m2}")
        for m2 in range(8):
            for sb_ in range(1):
                c0, c1 = sb_ * 512, (sb_ + 1) * 512
                def f_g(e, m2=m2, c0=c0, c1=c1):
                    return e.activation(out=gbuf[:, :], in_=acc[m2][:, c0:c1], func=AF.Silu,
                                        bias=bs2c[:, m2:m2 + 1])
                em.op("act", f_g, reads=[(acc[m2], ALL), bs2c], writes=[gbuf], tag=f"g_{th}_{m2}_{sb_}")
                em.dma("sp", sbuf2[:, :], s_in[m2 * 128:(m2 + 1) * 128, h0 + c0:h0 + c1],
                       reads=[s_in], writes=[sbuf2], tag=f"lds_{th}_{m2}_{sb_}")
                em.dma("sp", xbuf2[:, :], x_in[m2 * 128:(m2 + 1) * 128, h0 + c0:h0 + c1],
                       reads=[x_in], writes=[xbuf2], tag=f"ldx_{th}_{m2}_{sb_}")
                def f_sg(e):
                    return e.tensor_mul(out=sbuf2[:, :], in0=sbuf2[:, :], in1=gbuf[:, :])
                em.op("dve", f_sg, reads=[(sbuf2, "x"), gbuf], writes=[sbuf2], tag=f"sg_{th}_{m2}_{sb_}")
                def f_xo(e):
                    return e.tensor_add(out=xbuf2[:, :], in0=xbuf2[:, :], in1=sbuf2[:, :])
                em.op("dve", f_xo, reads=[(xbuf2, "x"), sbuf2], writes=[xbuf2], tag=f"xo_{th}_{m2}_{sb_}")
                em.dma("sp", xo_out[m2 * 128:(m2 + 1) * 128, h0 + c0:h0 + c1], xbuf2[:, :],
                       reads=[xbuf2], writes=[(xo_out, (th, m2, sb_))], tag=f"so_{th}_{m2}_{sb_}")
    em.barrier([xo_out])


def prep_l1d(c, H):
    return {"xn_in": np.ascontiguousarray(np.asarray(H["xn1"][c])),
            "s_in": np.ascontiguousarray(np.asarray(H["s1"][c], np.float32)),
            "x_in": np.ascontiguousarray(np.asarray(H["xL1"][c], np.float32))}


def post_l1d(results, H):
    H["xL2"] = [r["xo_out"] for r in results]


STAGE_L1D = Stage("l1d", build_l1d, prep_l1d, post_l1d)



def build_l2a(em, sc, H):
    W = H["weights"]
    x_in = em.dram("x_in", [C, TLOC], F32, kind="ExternalInput")
    xn_out = em.dram("xn_out", [C, TLOC], BF16, kind="ExternalOutput")
    qf_out = em.dram("qf_out", [2048, TLOC], BF16, kind="ExternalOutput")
    eh_out = em.dram("eh_out", [TLOC, 1024], BF16, kind="ExternalOutput")
    vf_out = em.dram("vf_out", [512, TLOC], BF16, kind="ExternalOutput")
    gram_out = em.dram("gram_out", [16, 8 * 48], F32, kind="ExternalOutput")
    bn_out = em.dram("bn_out", [128, 4], F32, kind="ExternalOutput")

    ident_b = sc.const_tile("ident_b", np.eye(128).astype(BF16NP), BF16)
    nw_col = sc.const_tile("nw_col", np.asarray(W["L2_attn_norm_weight"], np.float32).reshape(8, 128).T)
    bqc = sc.const_tile("bqc", np.asarray(W["L2_ffn_bq"], np.float32).reshape(16, 128).T)

    x_bf = load_fm_tiles(em, x_in, 8, TLOC, BF16, tag="xbf")
    xn = [em.sb(f"xn_{i}", [128, TLOC], BF16) for i in range(8)]
    ns = NormScratch(em, TLOC)
    emit_norm(em, sc, ns, x_bf, xn, 8, TLOC, rms=False, tag="ln2")
    for kc in range(8):
        em.dma("sp", xn_out[kc * 128:(kc + 1) * 128, :], xn[kc][:, :],
               reads=[(xn[kc], ALL)], writes=[(xn_out, kc)], tag=f"st_xn_{kc}")
    xr = x_bf
    emit_norm(em, sc, ns, xn, xr, 8, TLOC, rms=True, eps=1e-6, gamma_col=nw_col, tag="rmsa")

    PH = L2A_PH
    wqkv = sc.weight_bf16("W_qkv", np.ascontiguousarray(np.asarray(W["L2_attn_w_qkv"], np.float32).T))
    ps_q = [em.psum(f"psq{i}") for i in range(2)]
    ps_t = em.psum("ps_t", (128, 1024), BF16)
    ps_g = em.psum("ps_g")
    qkvt = [em.sb(f"qkvt_{i}", [128, 384], BF16) for i in range(16)]
    for tt in range(16 if PH >= 2 else 0):
        ps = ps_q[tt % 2]
        def f_qkv(e, tt=tt, ps=ps):
            last = None
            for kc in range(8):
                last = e.matmul(out=ps[:, 0:384], lhsT=xr[kc][:, tt * 128:(tt + 1) * 128],
                                rhs=wqkv.sl(kc, 0, 384), start=(kc == 0), stop=(kc == 7))
            return last
        em.op("pe", f_qkv, reads=[(wqkv.buf, ALL)] + [(xr[k], ALL) for k in range(8)],
              writes=[(ps, ALL)], tag=f"qkv_{tt}")
        def f_ev(e, tt=tt, ps=ps):
            return e.tensor_copy(out=qkvt[tt][:, :], in_=ps[:, 0:384])
        em.op("dve", f_ev, reads=[(ps, ALL)], writes=[qkvt[tt]], tag=f"qkve_{tt}")
    gsb = em.sb("gsb", [16, 8 * 48])
    for h in range(8 if PH >= 3 else 0):
        for bi, (o0, o1) in enumerate(((0, 0), (0, 128), (128, 128))):
            def f_gr(e, h=h, bi=bi, o0=o0, o1=o1):
                last = None
                for tt in range(16):
                    last = e.matmul(out=ps_g[0:16, h * 48 + bi * 16:h * 48 + (bi + 1) * 16],
                                    lhsT=qkvt[tt][:, o0 + h * 16:o0 + (h + 1) * 16],
                                    rhs=qkvt[tt][:, o1 + h * 16:o1 + (h + 1) * 16],
                                    start=(tt == 0), stop=(tt == 15))
                return last
            em.op("pe", f_gr, reads=[(qkvt[tt], ALL) for tt in range(16)],
                  writes=[(ps_g, (h, bi))], tag=f"gr_{h}_{bi}")
    def f_ge(e):
        return e.tensor_copy(out=gsb[:, :], in_=ps_g[0:16, 0:384])
    em.op("dve", f_ge, reads=[(ps_g, (h, bi)) for h in range(8) for bi in range(3)],
          writes=[gsb], tag="ge")
    em.dma("sp", gram_out.ap(), gsb[:, :], reads=[gsb], writes=[gram_out], tag="st_gr")
    wvt = np.asarray(W["L2_attn_w_qkv"], np.float32)[256:384, :].T
    wvp = np.zeros((1024, 512), np.float32)
    for h in range(8):
        r0 = 128 * (h // 2) + 64 * (h % 2)
        wvp[:, r0:r0 + 16] = wvt[:, h * 16:(h + 1) * 16]
    wvpt = sc.weight_bf16("W_vp", wvp)
    vfp = [em.sb(f"vfp{i}", [128, TLOC], BF16) for i in range(4)]
    for mc in range(4):
        for tc in range(4):
            ps = ps_q[tc % 2]
            def f_vp(e, mc=mc, tc=tc, ps=ps):
                last = None
                for kc in range(8):
                    last = e.matmul(out=ps[:, 0:512], lhsT=wvpt.sl(kc, mc * 128, (mc + 1) * 128),
                                    rhs=xr[kc][:, tc * 512:(tc + 1) * 512],
                                    start=(kc == 0), stop=(kc == 7))
                return last
            em.op("pe", f_vp, reads=[(wvpt.buf, ALL)] + [(xr[k], ALL) for k in range(8)],
                  writes=[(ps, ALL)], tag=f"vp_{mc}_{tc}")
            def f_vpe(e, mc=mc, tc=tc, ps=ps):
                return e.tensor_copy(out=vfp[mc][:, tc * 512:(tc + 1) * 512], in_=ps[:, 0:512])
            em.op("dve", f_vpe, reads=[(ps, ALL)], writes=[(vfp[mc], tc)], tag=f"vpe_{mc}_{tc}")
    for i in range(4):
        em.dma("sp", vf_out[i * 128:(i + 1) * 128, :], vfp[i][:, :],
               reads=[(vfp[i], ALL)], writes=[(vf_out, i)], tag=f"st_vf_{i}")

    wq = sc.weight_bf16("W_q2", np.asarray(W["L2_ffn_Wq"], np.float32))
    qstg = [em.sb(f"qstg{i}", [128, TLOC], BF16) for i in range(2)]
    sums = em.sb("sums", [128, 33])
    sqt = em.sb("sqt", [128, TLOC], BF16)
    def f_s0(e):
        return e.memset(sums[:, :], 0.0)
    em.op("dve", f_s0, reads=[], writes=[(sums, ALL)], tag="s0")
    for mc in range(16 if PH >= 5 else 0):
        stg = qstg[mc % 2]
        for tc in range(4):
            ps = ps_q[tc % 2]
            def f_mm(e, mc=mc, tc=tc, ps=ps):
                last = None
                for kc in range(8):
                    last = e.matmul(out=ps[:, 0:512], lhsT=wq.sl(kc, mc * 128, (mc + 1) * 128),
                                    rhs=xn[kc][:, tc * 512:(tc + 1) * 512],
                                    start=(kc == 0), stop=(kc == 7))
                return last
            em.op("pe", f_mm, reads=[(wq.buf, ALL)] + [(xn[k], ALL) for k in range(8)],
                  writes=[(ps, ALL)], tag=f"q2_{mc}_{tc}")
            def f_ev(e, mc=mc, tc=tc, ps=ps, stg=stg):
                return e.tensor_scalar(out=stg[:, tc * 512:(tc + 1) * 512], in0=ps[:, 0:512],
                                       scalar1=bqc[:, mc:mc + 1], scalar2=None, op0=ALU.add)
            em.op("dve", f_ev, reads=[(ps, ALL), bqc], writes=[(stg, tc)], tag=f"q2e_{mc}_{tc}")
        em.dma("sp", qf_out[mc * 128:(mc + 1) * 128, :], stg[:, :],
               reads=[(stg, ALL)], writes=[(qf_out, mc)], tag=f"st_q2_{mc}")
        def f_rs(e, mc=mc, stg=stg):
            return e.tensor_reduce(out=sums[:, mc:mc + 1], in_=stg[:, :], axis=AX.X, op=ALU.add)
        em.op("dve", f_rs, reads=[(stg, ALL)], writes=[(sums, mc)], tag=f"rs_{mc}")
        def f_sq(e, stg=stg):
            return e.activation(out=sqt[:, :], in_=stg[:, :], func=AF.Square)
        em.op("act", f_sq, reads=[(stg, ALL)], writes=[sqt], tag=f"sq_{mc}")
        def f_rq(e, mc=mc):
            return e.tensor_reduce(out=sums[:, 16 + mc:17 + mc], in_=sqt[:, :], axis=AX.X,
                                   op=ALU.add)
        em.op("dve", f_rq, reads=[sqt], writes=[(sums, 16 + mc)], tag=f"rq_{mc}")
    bns = em.sb("bns", [128, 4])
    for col, rng in ((0, range(0, 16, 2)), (1, range(1, 16, 2)),
                     (2, range(16, 32, 2)), (3, range(17, 32, 2))):
        first = True
        for mc in rng:
            if first:
                def f_c(e, col=col, mc=mc):
                    return e.tensor_copy(out=bns[:, col:col + 1], in_=sums[:, mc:mc + 1])
                em.op("dve", f_c, reads=[(sums, ALL)], writes=[(bns, col)], tag=f"bn_{col}_{mc}")
                first = False
            else:
                def f_a(e, col=col, mc=mc):
                    return e.tensor_add(out=bns[:, col:col + 1], in0=bns[:, col:col + 1],
                                        in1=sums[:, mc:mc + 1])
                em.op("dve", f_a, reads=[(sums, ALL), (bns, col)], writes=[(bns, col)],
                      tag=f"bn_{col}_{mc}")
    em.dma("sp", bn_out.ap(), bns[:, :], reads=[bns], writes=[bn_out], tag="st_bn")

    wdn_d = sc.cin("W_dn_r", np.ascontiguousarray(np.asarray(W["L2_ffn_expert_down"], np.float32)
                   .T.reshape(8, 128, 1024).transpose(1, 0, 2).reshape(128, 8 * 1024)))
    em.dma("pool", wq.buf[:, 0:8 * 1024], wdn_d[0:128, :], reads=[wdn_d],
           writes=[(wq.buf, ALL)], tag="ld_wdn")
    wdn = WTile(wq.buf, 1024, 1024)
    ehst = [em.sb(f"ehst{i}", [128, 1024], BF16) for i in range(2)]
    for tt in range(16 if PH >= 6 else 0):
        stg = ehst[tt % 2]
        for nc_ in range(2):
            ps = ps_q[nc_ % 2]
            def f_eh(e, tt=tt, nc_=nc_, ps=ps):
                last = None
                for kc in range(8):
                    last = e.matmul(out=ps[:, 0:512], lhsT=xn[kc][:, tt * 128:(tt + 1) * 128],
                                    rhs=wdn.sl(kc, nc_ * 512, (nc_ + 1) * 512),
                                    start=(kc == 0), stop=(kc == 7))
                return last
            em.op("pe", f_eh, reads=[(wdn.buf, ALL)] + [(xn[k], ALL) for k in range(8)],
                  writes=[(ps, ALL)], tag=f"eh_{tt}_{nc_}")
            def f_ee(e, tt=tt, nc_=nc_, ps=ps, stg=stg):
                return e.activation(out=stg[:, nc_ * 512:(nc_ + 1) * 512], in_=ps[:, 0:512],
                                    func=AF.Gelu)
            em.op("act", f_ee, reads=[(ps, ALL)], writes=[(stg, nc_)], tag=f"ehe_{tt}_{nc_}")
        em.dma("sp", eh_out[tt * 128:(tt + 1) * 128, :], stg[:, :],
               reads=[(stg, ALL)], writes=[(eh_out, tt)], tag=f"st_eh_{tt}")
    em.barrier([xn_out, qf_out, eh_out, vf_out, gram_out, bn_out])


def prep_l2a(c, H):
    return {"x_in": np.ascontiguousarray(np.asarray(H["xL2"][c], np.float32))}


def post_l2a(results, H):
    for k, o in (("xn2", "xn_out"), ("qf2", "qf_out"), ("eh2", "eh_out"),
                 ("vf2", "vf_out"), ("gram2", "gram_out"), ("bn2", "bn_out")):
        H[k] = [r[o] for r in results]


STAGE_L2A = Stage("l2a", build_l2a, prep_l2a, post_l2a)



def build_l2b(em, sc, H):
    W = H["weights"]
    qf_in = em.dram("qf_in", [2048, TLOC], BF16, kind="ExternalInput")
    eh_in = em.dram("eh_in", [TLOC, 1024], BF16, kind="ExternalInput")
    vf_in = em.dram("vf_in", [512, TLOC], BF16, kind="ExternalInput")
    gram_in = em.dram("gram_in", [16, 2 * 384], F32, kind="ExternalInput")
    bn_in = em.dram("bn_in", [128, 4 * 8], F32, kind="ExternalInput")
    xn_in = em.dram("xn_in", [1024, TLOC], BF16, kind="ExternalInput")
    x_in = em.dram("x_in", [1024, TLOC], F32, kind="ExternalInput")
    temp_in = em.dram("temp_in", [8, 1], F32, kind="ExternalInput")
    xo_out = em.dram("xo_out", [1024, TLOC], F32, kind="ExternalOutput")

    ident_f = sc.const_tile("ident_f", np.eye(128, dtype=np.float32))
    ident_b = sc.const_tile("ident_b", np.eye(128).astype(BF16NP), BF16)
    ones1 = sc.ones1()
    onescol_b = sc.ones_col_bf()
    gam = sc.const_tile("gam", np.concatenate([np.asarray(W["L2_ffn_bn_gamma"], np.float32).reshape(2, 128).T], axis=0))
    bet = sc.const_tile("bet", np.asarray(W["L2_ffn_bn_beta"], np.float32).reshape(2, 128).T)
    sk0 = sc.const_tile("sk0", np.asarray(W["L2_ffn_sub_key0"], np.float32))
    sk1 = sc.const_tile("sk1", np.asarray(W["L2_ffn_sub_key1"], np.float32))

    bnp = em.sb("bnp", [128, 32])
    em.dma("sp", bnp[:, :], bn_in.ap(), reads=[bn_in], writes=[bnp], tag="ld_bn")
    bna = em.sb("bna", [128, 8])
    for col in range(4):
        def f_c(e, col=col):
            return e.tensor_copy(out=bna[:, col:col + 1], in_=bnp[:, col:col + 1])
        em.op("dve", f_c, reads=[bnp], writes=[(bna, col)], tag=f"bnc_{col}")
        for cr in range(1, 8):
            def f_a(e, col=col, cr=cr):
                return e.tensor_add(out=bna[:, col:col + 1], in0=bna[:, col:col + 1],
                                    in1=bnp[:, cr * 4 + col:cr * 4 + col + 1])
            em.op("dve", f_a, reads=[bnp, (bna, col)], writes=[(bna, col)], tag=f"bna_{col}_{cr}")
    NBN = float(B * T * 8)
    for g in range(2):
        def f_m(e, g=g):
            return e.tensor_scalar_mul(out=bna[:, g:g + 1], in0=bna[:, g:g + 1], scalar1=1.0 / NBN)
        em.op("dve", f_m, reads=[(bna, g)], writes=[(bna, g)], tag=f"bm_{g}")
        def f_v(e, g=g):
            return e.tensor_scalar_mul(out=bna[:, 2 + g:3 + g], in0=bna[:, 2 + g:3 + g],
                                       scalar1=1.0 / NBN)
        em.op("dve", f_v, reads=[(bna, 2 + g)], writes=[(bna, 2 + g)], tag=f"bv_{g}")
        def f_v2(e, g=g):
            return e.scalar_tensor_tensor(out=bna[:, 4 + g:5 + g], in0=bna[:, g:g + 1],
                                          scalar=-1.0, in1=bna[:, g:g + 1],
                                          op0=ALU.mult, op1=ALU.mult)
        em.op("dve", f_v2, reads=[(bna, g)], writes=[(bna, 4 + g)], tag=f"bv2_{g}")
        def f_v3(e, g=g):
            return e.tensor_add(out=bna[:, 2 + g:3 + g], in0=bna[:, 2 + g:3 + g],
                                in1=bna[:, 4 + g:5 + g])
        em.op("dve", f_v3, reads=[(bna, 2 + g), (bna, 4 + g)], writes=[(bna, 2 + g)], tag=f"bv3_{g}")
        def f_v4(e, g=g):
            return e.tensor_scalar_add(out=bna[:, 2 + g:3 + g], in0=bna[:, 2 + g:3 + g],
                                       scalar1=1e-5)
        em.op("dve", f_v4, reads=[(bna, 2 + g)], writes=[(bna, 2 + g)], tag=f"bv4_{g}")
        def f_r(e, g=g):
            return e.reciprocal(out=bna[:, 4 + g:5 + g], in_=bna[:, 2 + g:3 + g])
        em.op("dve", f_r, reads=[(bna, 2 + g)], writes=[(bna, 4 + g)], tag=f"br_{g}")
        def f_s(e, g=g):
            return e.activation(out=bna[:, 4 + g:5 + g], in_=bna[:, 4 + g:5 + g], func=AF.Sqrt)
        em.op("act", f_s, reads=[(bna, 4 + g)], writes=[(bna, 4 + g)], tag=f"bs_{g}")
        def f_ag(e, g=g):
            return e.tensor_mul(out=bna[:, 4 + g:5 + g], in0=bna[:, 4 + g:5 + g],
                                in1=gam[:, g:g + 1])
        em.op("dve", f_ag, reads=[(bna, 4 + g), gam], writes=[(bna, 4 + g)], tag=f"bag_{g}")
        def f_bb(e, g=g):
            return e.scalar_tensor_tensor(out=bna[:, 6 + g:7 + g], in0=bna[:, g:g + 1],
                                          scalar=-1.0, in1=bna[:, 4 + g:5 + g],
                                          op0=ALU.mult, op1=ALU.mult)
        em.op("dve", f_bb, reads=[(bna, g), (bna, 4 + g)], writes=[(bna, 6 + g)], tag=f"bb_{g}")
        def f_bb2(e, g=g):
            return e.tensor_add(out=bna[:, 6 + g:7 + g], in0=bna[:, 6 + g:7 + g],
                                in1=bet[:, g:g + 1])
        em.op("dve", f_bb2, reads=[(bna, 6 + g), bet], writes=[(bna, 6 + g)], tag=f"bb2_{g}")

    ps_a = em.psum("ps_a")
    ps_b2 = em.psum("ps_b2")
    ps_c = em.psum("ps_c")
    ps_t2 = em.psum("ps_t2", (128, 1024), BF16)
    kn = em.sb("kn", [32, 128], BF16)
    knT = [em.sb(f"knT{i}", [128, 32], BF16) for i in range(2)]
    ksc = em.sb("ksc", [32, 2])
    for i, skt in enumerate((sk0, sk1)):
        def f_k2(e, skt=skt):
            return e.activation(out=kn[:, :], in_=skt[:, :], func=AF.Square)
        em.op("act", f_k2, reads=[skt], writes=[(kn, ALL)], tag=f"k2_{i}")
        def f_kr(e):
            return e.tensor_reduce(out=ksc[:, 0:1], in_=kn[:, :], axis=AX.X, op=ALU.add)
        em.op("dve", f_kr, reads=[(kn, ALL)], writes=[(ksc, 0)], tag=f"kr_{i}")
        def f_ks(e):
            return e.activation(out=ksc[:, 0:1], in_=ksc[:, 0:1], func=AF.Sqrt)
        em.op("act", f_ks, reads=[(ksc, 0)], writes=[(ksc, 0)], tag=f"ks_{i}")
        def f_km(e):
            return e.tensor_scalar_max(out=ksc[:, 0:1], in0=ksc[:, 0:1], scalar1=1e-12)
        em.op("dve", f_km, reads=[(ksc, 0)], writes=[(ksc, 0)], tag=f"km_{i}")
        def f_kc(e):
            return e.reciprocal(out=ksc[:, 1:2], in_=ksc[:, 0:1])
        em.op("dve", f_kc, reads=[(ksc, 0)], writes=[(ksc, 1)], tag=f"kc_{i}")
        def f_kn(e, skt=skt):
            return e.tensor_scalar(out=kn[:, :], in0=skt[:, :], scalar1=ksc[:, 1:2],
                                   scalar2=None, op0=ALU.mult)
        em.op("dve", f_kn, reads=[skt, (ksc, 1)], writes=[(kn, ALL)], tag=f"kn_{i}")
        def f_kt(e):
            return e.transpose(out=ps_t2[0:128, 0:32], in_=kn[:, :],
                               identity=ident_b[0:32, 0:32])
        em.op("pe", f_kt, reads=[(kn, ALL), ident_b], writes=[(ps_t2, ALL)], tag=f"kt_{i}")
        def f_ke(e, i=i):
            return e.tensor_copy(out=knT[i][:, :], in_=ps_t2[0:128, 0:32])
        em.op("dve", f_ke, reads=[(ps_t2, ALL)], writes=[knT[i]], tag=f"ke_{i}")

    tmp_sb = em.sb("tmp_sb", [8, 1])
    em.dma("sp", tmp_sb[:, :], temp_in.ap(), reads=[temp_in], writes=[tmp_sb], tag="ld_tmp")

    grm = em.sb("grm", [16, 2 * 384])
    em.dma("sp", grm[:, :], gram_in.ap(), reads=[gram_in], writes=[grm], tag="ld_grm")
    gsum = em.sb("gsum", [16, 384])
    def f_gs(e):
        return e.tensor_add(out=gsum[:, :], in0=grm[0:16, 0:384], in1=grm[0:16, 384:768])
    em.op("dve", f_gs, reads=[grm], writes=[gsum], tag="gs")
    imask = sc.const_tile("imask", np.eye(16, dtype=np.float32))
    att = em.sb("att", [128, 64], BF16)
    attw = em.sb("attw", [32, 80])
    vfp = load_fm_tiles(em, vf_in, 4, TLOC, BF16, tag="vf")
    ctxp = [em.sb(f"ctxp{i}", [128, TLOC], BF16) for i in range(4)]
    for i in range(4):
        def f_cz(e, i=i):
            return e.memset(ctxp[i][:, :], 0.0)
        em.op("dve", f_cz, reads=[], writes=[(ctxp[i], ALL)], tag=f"cz_{i}")
    etmp = em.sb("etmp", [8, 1])
    def f_te(e):
        return e.activation(out=etmp[:, :], in_=tmp_sb[:, :], func=AF.Exp)
    em.op("act", f_te, reads=[tmp_sb], writes=[etmp], tag="te")
    etr = em.sb("etr", [1, 8])
    def f_tt(e):
        return e.transpose(out=ps_c[0:1, 32:40], in_=etmp[:, :], identity=ident_f[0:8, 0:8])
    em.op("pe", f_tt, reads=[etmp, ident_f], writes=[(ps_c, "t")], tag="tt")
    def f_tte(e):
        return e.tensor_copy(out=etr[:, :], in_=ps_c[0:1, 32:40])
    em.op("dve", f_tte, reads=[(ps_c, "t")], writes=[etr], tag="tte")
    etbc = em.sb("etbc", [128, 8])
    def f_teb(e):
        return e.matmul(out=ps_c[0:128, 80:88], lhsT=ones1[0:1, :], rhs=etr[0:1, :],
                        start=True, stop=True)
    em.op("pe", f_teb, reads=[ones1, etr], writes=[(ps_c, "tb")], tag="teb")
    def f_tee(e):
        return e.tensor_copy(out=etbc[:, :], in_=ps_c[0:128, 80:88])
    em.op("dve", f_tee, reads=[(ps_c, "tb")], writes=[etbc], tag="tee")
    em.op("dve", lambda e: e.memset(att[:, :], 0.0), reads=[], writes=[(att, ALL)], tag="attz")
    for h in range(8):
        g0 = h * 48
        g4 = 64 * (h % 2)
        for bi, col in ((0, 0), (2, 1)):
            def f_dg(e, g0=g0, bi=bi, col=col):
                return e.tensor_mul(out=attw[0:16, 16:32], in0=gsum[:, g0 + bi * 16:g0 + (bi + 1) * 16],
                                    in1=imask[0:16, :])
            em.op("dve", f_dg, reads=[gsum, imask], writes=[(attw, "d")], tag=f"dg_{h}_{bi}")
            def f_dr(e, col=col):
                return e.tensor_reduce(out=attw[0:16, col:col + 1], in_=attw[0:16, 16:32],
                                       axis=AX.X, op=ALU.add)
            em.op("dve", f_dr, reads=[(attw, "d")], writes=[(attw, ("n", col))], tag=f"dr_{h}_{bi}")
            def f_dsq(e, col=col):
                return e.activation(out=attw[0:16, col:col + 1], in_=attw[0:16, col:col + 1],
                                    func=AF.Sqrt)
            em.op("act", f_dsq, reads=[(attw, ("n", col))], writes=[(attw, ("n", col))],
                  tag=f"dsq_{h}_{bi}")
            def f_dmx(e, col=col):
                return e.tensor_scalar_max(out=attw[0:16, col:col + 1],
                                           in0=attw[0:16, col:col + 1], scalar1=1e-12)
            em.op("dve", f_dmx, reads=[(attw, ("n", col))], writes=[(attw, ("n", col))],
                  tag=f"dmx_{h}_{bi}")
            def f_drc(e, col=col):
                return e.reciprocal(out=attw[0:16, 2 + col:3 + col], in_=attw[0:16, col:col + 1])
            em.op("dve", f_drc, reads=[(attw, ("n", col))], writes=[(attw, ("r", col))],
                  tag=f"drc_{h}_{bi}")
        def f_sc(e, g0=g0):
            return e.tensor_scalar(out=attw[0:16, 36:52], in0=gsum[:, g0 + 16:g0 + 32],
                                   scalar1=attw[0:16, 2:3], scalar2=None, op0=ALU.mult)
        em.op("dve", f_sc, reads=[gsum, (attw, ("r", 0))], writes=[(attw, "sim")], tag=f"sc_{h}")
        def f_rkt(e):
            return e.transpose(out=ps_c[0:1, 40:56], in_=attw[0:16, 3:4],
                               identity=ident_f[0:16, 0:16])
        em.op("pe", f_rkt, reads=[(attw, ("r", 1)), ident_f], writes=[(ps_c, "rk")], tag=f"rkt_{h}")
        def f_rke(e):
            return e.tensor_copy(out=attw[0:1, 56:72], in_=ps_c[0:1, 40:56])
        em.op("dve", f_rke, reads=[(ps_c, "rk")], writes=[(attw, "rkr")], tag=f"rke_{h}")
        def f_rkb(e):
            return e.matmul(out=ps_c[0:16, 64:80], lhsT=ones1[0:1, 0:16], rhs=attw[0:1, 56:72],
                            start=True, stop=True)
        em.op("pe", f_rkb, reads=[ones1, (attw, "rkr")], writes=[(ps_c, "rkb")], tag=f"rkb_{h}")
        def f_sc2(e):
            return e.tensor_mul(out=attw[0:16, 36:52], in0=attw[0:16, 36:52],
                                in1=ps_c[0:16, 64:80])
        em.op("dve", f_sc2, reads=[(attw, "sim"), (ps_c, "rkb")], writes=[(attw, "sim")],
              tag=f"sc2_{h}")
        def f_sc3(e, h=h):
            return e.tensor_scalar(out=attw[0:16, 36:52], in0=attw[0:16, 36:52],
                                   scalar1=etbc[0:16, h:h + 1], scalar2=None, op0=ALU.mult)
        em.op("dve", f_sc3, reads=[(attw, "sim"), etbc], writes=[(attw, "sim")], tag=f"sc3_{h}")
        def f_smx(e):
            return e.tensor_reduce(out=attw[0:16, 52:53], in_=attw[0:16, 36:52], axis=AX.X,
                                   op=ALU.max, negate=True)
        em.op("dve", f_smx, reads=[(attw, "sim")], writes=[(attw, "mx")], tag=f"smx_{h}")
        def f_sex(e):
            return e.activation(out=attw[0:16, 36:52], in_=attw[0:16, 36:52], func=AF.Exp,
                                bias=attw[0:16, 52:53])
        em.op("act", f_sex, reads=[(attw, "sim"), (attw, "mx")], writes=[(attw, "sim")],
              tag=f"sex_{h}")
        def f_ssm(e):
            return e.tensor_reduce(out=attw[0:16, 53:54], in_=attw[0:16, 36:52], axis=AX.X,
                                   op=ALU.add)
        em.op("dve", f_ssm, reads=[(attw, "sim")], writes=[(attw, "sm")], tag=f"ssm_{h}")
        def f_src(e):
            return e.reciprocal(out=attw[0:16, 54:55], in_=attw[0:16, 53:54])
        em.op("dve", f_src, reads=[(attw, "sm")], writes=[(attw, "rs")], tag=f"src_{h}")
        def f_sps(e, g4=g4):
            return e.tensor_scalar(out=att[g4:g4 + 16, 0:16],
                                   in0=attw[0:16, 36:52], scalar1=attw[0:16, 54:55],
                                   scalar2=None, op0=ALU.mult)
        em.op("dve", f_sps, reads=[(attw, "sim"), (attw, "rs")], writes=[(att, ALL)],
              tag=f"sps_{h}")
        def f_atr(e, g4=g4):
            return e.transpose(out=att[g4:g4 + 32, 32:64], in_=att[g4:g4 + 32, 0:32])
        em.op("dve", f_atr, reads=[(att, ALL)], writes=[(att, ALL)], tag=f"atr_{h}")
        for tb in range(4):
            def f_avm(e, h=h, g4=g4, tb=tb):
                return e.matmul(out=ps_a[g4:g4 + 16, 0:512],
                                lhsT=att[g4:g4 + 16, 32:48],
                                rhs=vfp[h // 2][g4:g4 + 16, tb * 512:(tb + 1) * 512],
                                start=True, stop=True)
            em.op("pe", f_avm, reads=[(att, ALL), (vfp[h // 2], ALL)], writes=[(ps_a, ALL)],
                  tag=f"avm_{h}_{tb}")
            def f_ave(e, h=h, g4=g4, tb=tb):
                return e.tensor_copy(out=ctxp[h // 2][g4:g4 + 16, tb * 512:(tb + 1) * 512],
                                     in_=ps_a[g4:g4 + 16, 0:512])
            em.op("dve", f_ave, reads=[(ps_a, ALL)], writes=[(ctxp[h // 2], (h, tb))],
                  tag=f"ave_{h}_{tb}")

    wop = np.zeros((512, 1024), np.float32)
    wo = np.asarray(W["L2_attn_w_out"], np.float32)
    for h in range(8):
        r0 = 128 * (h // 2) + 64 * (h % 2)
        wop[r0:r0 + 16, :] = wo[:, h * 16:(h + 1) * 16].T
    wopt = sc.weight_bf16("W_out_p", wop)
    s2t = [em.sb(f"s2t_{i}", [128, TLOC], BF16) for i in range(8)]
    fstg = em.sb("fstg", [128, 1024], BF16)
    ps_s2 = [em.psum(f"ps_s2{i}") for i in range(2)]
    for mc in range(8):
        for tc in range(4):
            ps = ps_s2[tc % 2]
            def f_s2(e, mc=mc, tc=tc, ps=ps):
                last = None
                for kc in range(4):
                    last = e.matmul(out=ps[:, 0:512], lhsT=wopt.sl(kc, mc * 128, (mc + 1) * 128),
                                    rhs=ctxp[kc][:, tc * 512:(tc + 1) * 512],
                                    start=(kc == 0), stop=(kc == 3))
                return last
            em.op("pe", f_s2, reads=[(wopt.buf, ALL)] + [(ctxp[i], ALL) for i in range(4)],
                  writes=[(ps, ALL)], tag=f"s2_{mc}_{tc}")
            def f_s2e(e, mc=mc, tc=tc, ps=ps):
                return e.tensor_copy(out=s2t[mc][:, tc * 512:(tc + 1) * 512], in_=ps[:, 0:512])
            em.op("dve", f_s2e, reads=[(ps, ALL)], writes=[(s2t[mc], tc)], tag=f"s2e_{mc}_{tc}")

    upw = sc.weight_bf16("W_up", np.asarray(W["L2_ffn_expert_up"], np.float32))
    qc = [em.sb(f"qc{i}", [128, TLOC], BF16) for i in range(2)]
    s01 = em.sb("s01", [128, 64])
    e01 = em.sb("e01", [128, 64])
    msc = em.sb("msc", [128, 12])
    rnr = em.sb("rnr", [1, TLOC])
    rncol = em.sb("rncol", [128, 32])
    wsum = em.sb("wsum", [128, 1024])
    weh = em.sb("weh", [128, 1024], BF16)
    ehb = em.sb("ehb", [128, 1024], BF16)
    wfm = [em.sb(f"wfm{i}", [128, 128], BF16) for i in range(8)]
    ps_up = [em.psum(f"ps_up{i}") for i in range(2)]
    xot = em.sb("xot", [128, 512])
    sg2 = em.sb("sg2", [128, 512], BF16)
    sqs = em.sb("sqs", [128, TLOC], BF16)
    e0_all = em.sb("e0a", [128, 16 * 256])
    e1_all = em.sb("e1a", [128, 16 * 256], BF16)
    f2_dram = em.dram("f2_scratch", [1024, TLOC], BF16, kind="Internal")

    for h in range(8):
        for half in range(2):
            m = 2 * h + half
            qq = qc[m % 2]
            em.dma("pool", qq[:, :], qf_in[m * 128:(m + 1) * 128, :], reads=[qf_in],
                   writes=[qq], tag=f"ldq_{m}")
            def f_bn(e, qq=qq, half=half):
                return e.tensor_scalar(out=qq[:, :], in0=qq[:, :],
                                       scalar1=bna[:, 4 + half:5 + half],
                                       scalar2=bna[:, 6 + half:7 + half],
                                       op0=ALU.mult, op1=ALU.add)
            em.op("dve", f_bn, reads=[(qq, "x"), (bna, 4 + half), (bna, 6 + half)],
                  writes=[(qq, ALL)], tag=f"bn_{m}")
            def f_sq2(e, qq=qq):
                return e.activation(out=sqs[:, :], in_=qq[:, :], func=AF.Square)
            em.op("act", f_sq2, reads=[(qq, ALL)], writes=[sqs], tag=f"sq2_{m}")
            for tc in range(4):
                def f_ssm2(e, tc=tc):
                    return e.matmul(out=ps_b2[0:1, 0:512], lhsT=onescol_b[:, 0:1],
                                    rhs=sqs[:, tc * 512:(tc + 1) * 512], start=True, stop=True)
                em.op("pe", f_ssm2, reads=[onescol_b, sqs], writes=[ps_b2], tag=f"ssm2_{m}_{tc}")
                def f_sse(e, tc=tc):
                    return e.tensor_copy(out=rnr[0:1, tc * 512:(tc + 1) * 512],
                                         in_=ps_b2[0:1, 0:512])
                em.op("dve", f_sse, reads=[ps_b2], writes=[(rnr, tc)], tag=f"sse_{m}_{tc}")
            def f_rsq(e):
                return e.activation(out=rnr[0:1, :], in_=rnr[0:1, :], func=AF.Sqrt)
            em.op("act", f_rsq, reads=[(rnr, 0), (rnr, 1), (rnr, 2), (rnr, 3)],
                  writes=[(rnr, ALL)], tag=f"rsq_{m}")
            def f_rmx(e):
                return e.tensor_scalar_max(out=rnr[0:1, :], in0=rnr[0:1, :], scalar1=1e-12)
            em.op("dve", f_rmx, reads=[(rnr, ALL)], writes=[(rnr, ALL)], tag=f"rmx_{m}")
            def f_rrc(e):
                return e.reciprocal(out=rnr[0:1, :], in_=rnr[0:1, :])
            em.op("dve", f_rrc, reads=[(rnr, ALL)], writes=[(rnr, ALL)], tag=f"rrc_{m}")
            for tt in range(16):
                def f_rt(e, tt=tt):
                    return e.transpose(out=ps_c[0:128, 128:129], in_=rnr[0:1, tt * 128:(tt + 1) * 128],
                                       identity=ident_f[0:1, 0:1])
                em.op("pe", f_rt, reads=[(rnr, ALL), ident_f], writes=[(ps_c, "rn")], tag=f"rt_{m}_{tt}")
                def f_rte(e):
                    return e.tensor_copy(out=rncol[:, 0:1], in_=ps_c[0:128, 128:129])
                em.op("dve", f_rte, reads=[(ps_c, "rn")], writes=[(rncol, 0)], tag=f"rte_{m}_{tt}")
                def f_sco(e, tt=tt, qq=qq, half=half):
                    return e.matmul(out=ps_a[0:128, 512 - 32:512], lhsT=qq[:, tt * 128:(tt + 1) * 128],
                                    rhs=knT[half][:, :], start=True, stop=True)
                em.op("pe", f_sco, reads=[(qq, ALL), knT[half]], writes=[(ps_a, "sc")],
                      tag=f"sco_{m}_{tt}")
                def f_scs(e, half=half):
                    return e.tensor_scalar(out=s01[:, half * 32:(half + 1) * 32],
                                           in0=ps_a[0:128, 512 - 32:512],
                                           scalar1=rncol[:, 0:1], scalar2=None, op0=ALU.mult)
                em.op("dve", f_scs, reads=[(ps_a, "sc"), (rncol, 0)], writes=[(s01, half)],
                      tag=f"scs_{m}_{tt}")
                def f_m8(e, half=half):
                    return e.max(out=msc[:, 0:8], in_=s01[:, half * 32:(half + 1) * 32])
                em.op("dve", f_m8, reads=[(s01, half)], writes=[(msc, "m")], tag=f"m8_{m}_{tt}")
                def f_ng(e):
                    return e.tensor_scalar_mul(out=msc[:, 8:9], in0=msc[:, 0:1], scalar1=-1.0)
                em.op("dve", f_ng, reads=[(msc, "m")], writes=[(msc, "n")], tag=f"ng_{m}_{tt}")
                def f_kp(e, half=half):
                    return e.tensor_scalar(out=e01[:, half * 32:(half + 1) * 32],
                                           in0=s01[:, half * 32:(half + 1) * 32],
                                           scalar1=msc[:, 3:4], scalar2=None, op0=ALU.is_ge)
                em.op("dve", f_kp, reads=[(s01, half), (msc, "m")], writes=[(e01, ("k", half))],
                      tag=f"kp_{m}_{tt}")
                def f_ex2(e, half=half):
                    return e.activation(out=s01[:, half * 32:(half + 1) * 32],
                                        in_=s01[:, half * 32:(half + 1) * 32], func=AF.Exp,
                                        bias=msc[:, 8:9])
                em.op("act", f_ex2, reads=[(s01, half), (msc, "n")], writes=[(s01, half)],
                      tag=f"ex2_{m}_{tt}")
                def f_em(e, half=half):
                    return e.tensor_mul(out=e01[:, half * 32:(half + 1) * 32],
                                        in0=e01[:, half * 32:(half + 1) * 32],
                                        in1=s01[:, half * 32:(half + 1) * 32])
                em.op("dve", f_em, reads=[(e01, ("k", half)), (s01, half)],
                      writes=[(e01, ("k", half))], tag=f"em_{m}_{tt}")
                def f_sm2(e, half=half):
                    return e.tensor_reduce(out=msc[:, 4 + half:5 + half],
                                           in_=e01[:, half * 32:(half + 1) * 32], axis=AX.X,
                                           op=ALU.add)
                em.op("dve", f_sm2, reads=[(e01, ("k", half))], writes=[(msc, ("s", half))],
                      tag=f"sm2_{m}_{tt}")
                def f_rc2(e, half=half):
                    return e.reciprocal(out=msc[:, 6 + half:7 + half], in_=msc[:, 4 + half:5 + half])
                em.op("dve", f_rc2, reads=[(msc, ("s", half))], writes=[(msc, ("r", half))],
                      tag=f"rc2_{m}_{tt}")
                dstall = e0_all if half == 0 else e1_all
                def f_fd(e, h=h, tt=tt, half=half, dstall=dstall):
                    return e.tensor_scalar(out=dstall[:, tt * 256 + h * 32:tt * 256 + (h + 1) * 32],
                                           in0=e01[:, half * 32:(half + 1) * 32],
                                           scalar1=msc[:, 6 + half:7 + half], scalar2=None,
                                           op0=ALU.mult)
                em.op("dve", f_fd, reads=[(e01, ("k", half)), (msc, ("r", half))],
                      writes=[(dstall, (tt, h))], tag=f"fd_{m}_{tt}")

    for tt in range(16):
        em.dma("pool", ehb[:, :], eh_in[tt * 128:(tt + 1) * 128, :], reads=[eh_in],
               writes=[ehb], tag=f"ldeh_{tt}")
        for h in range(8):
            for i in range(32):
                co = tt * 256 + h * 32
                if h == 0:
                    def f_w(e, i=i, co=co):
                        return e.tensor_scalar(out=wsum[:, i * 32:(i + 1) * 32],
                                               in0=e1_all[:, co:co + 32],
                                               scalar1=e0_all[:, co + i:co + i + 1],
                                               scalar2=None, op0=ALU.mult)
                else:
                    def f_w(e, i=i, co=co):
                        return e.scalar_tensor_tensor(out=wsum[:, i * 32:(i + 1) * 32],
                                                      in0=e1_all[:, co:co + 32],
                                                      scalar=e0_all[:, co + i:co + i + 1],
                                                      in1=wsum[:, i * 32:(i + 1) * 32],
                                                      op0=ALU.mult, op1=ALU.add)
                em.op("dve", f_w, reads=[(e0_all, ALL), (e1_all, ALL), (wsum, ("w", i))],
                      writes=[(wsum, ("w", i))], tag=f"w_{tt}_{h}_{i}")
        def f_weh(e):
            return e.tensor_mul(out=weh[:, :], in0=wsum[:, :], in1=ehb[:, :])
        em.op("dve", f_weh, reads=[(wsum, ALL), ehb], writes=[weh], tag=f"weh_{tt}")
        for kc in range(8):
            def f_wt(e, kc=kc):
                return e.transpose(out=ps_t2[:, (kc % 4) * 128:(kc % 4) * 128 + 128],
                                   in_=weh[:, kc * 128:(kc + 1) * 128], identity=ident_b[:, :])
            em.op("pe", f_wt, reads=[weh, ident_b], writes=[(ps_t2, ALL)],
                  tag=f"wt_{tt}_{kc}")
            def f_we2(e, kc=kc):
                return e.tensor_copy(out=wfm[kc][:, :], in_=ps_t2[:, (kc % 4) * 128:(kc % 4) * 128 + 128])
            em.op("dve", f_we2, reads=[(ps_t2, ALL)], writes=[wfm[kc]], tag=f"we2_{tt}_{kc}")
        for mc in range(8):
            ps = ps_up[mc % 2]
            def f_up(e, mc=mc, ps=ps):
                last = None
                for kc in range(8):
                    last = e.matmul(out=ps[:, 0:128], lhsT=upw.sl(kc, mc * 128, (mc + 1) * 128),
                                    rhs=wfm[kc][:, :], start=(kc == 0), stop=(kc == 7))
                return last
            em.op("pe", f_up, reads=[(upw.buf, ALL)] + [(wfm[kc], ALL) for kc in range(8)],
                  writes=[(ps, ALL)], tag=f"up_{tt}_{mc}")
            def f_upe(e, mc=mc, ps=ps):
                return e.tensor_copy(out=fstg[:, mc * 128:(mc + 1) * 128], in_=ps[:, 0:128])
            em.op("dve", f_upe, reads=[(ps, ALL)], writes=[(fstg, mc)], tag=f"upe_{tt}_{mc}")
        em.dma("sp", f2_dram[:, tt * 128:(tt + 1) * 128].rearrange("(m p) t -> p m t", p=128),
               fstg[:, :].rearrange("p (m t) -> p m t", m=8),
               reads=[(fstg, ALL)], writes=[(f2_dram, tt)], tag=f"stf2_{tt}")

    f2b = em.sb("f2b", [128, TLOC], BF16)
    xb2 = em.sb("xb2", [128, TLOC])
    gb2 = em.sb("gb2", [128, TLOC], BF16)
    for mc in range(8):
        em.dma("pool", f2b[:, :], f2_dram[mc * 128:(mc + 1) * 128, :],
               reads=[(f2_dram, ALL)], writes=[f2b], tag=f"ldf2_{mc}")
        em.dma("sp", xb2[:, :], x_in[mc * 128:(mc + 1) * 128, :], reads=[x_in],
               writes=[xb2], tag=f"ldx2_{mc}")
        def f_si2(e):
            return e.activation(out=gb2[:, :], in_=f2b[:, :], func=AF.Silu)
        em.op("act", f_si2, reads=[f2b], writes=[gb2], tag=f"si2_{mc}")
        def f_sg2(e, mc=mc):
            return e.tensor_mul(out=gb2[:, :], in0=gb2[:, :], in1=s2t[mc][:, :])
        em.op("dve", f_sg2, reads=[(gb2, "x"), (s2t[mc], ALL)], writes=[gb2], tag=f"sg2_{mc}")
        def f_xo2(e):
            return e.tensor_add(out=xb2[:, :], in0=xb2[:, :], in1=gb2[:, :])
        em.op("dve", f_xo2, reads=[(xb2, "x"), gb2], writes=[xb2], tag=f"xo2_{mc}")
        em.dma("sp", xo_out[mc * 128:(mc + 1) * 128, :], xb2[:, :], reads=[xb2],
               writes=[(xo_out, mc)], tag=f"stxo_{mc}")
    em.barrier([xo_out])


def prep_l2b(c, H):
    b = c // 2
    gr = np.concatenate([np.asarray(H["gram2"][2 * b], np.float32),
                         np.asarray(H["gram2"][2 * b + 1], np.float32)], axis=1)
    bn = np.concatenate([np.asarray(H["bn2"][i], np.float32) for i in range(8)], axis=1)
    return {"qf_in": np.asarray(H["qf2"][c]), "eh_in": np.asarray(H["eh2"][c]),
            "vf_in": np.asarray(H["vf2"][c]), "gram_in": np.ascontiguousarray(gr),
            "bn_in": np.ascontiguousarray(bn),
            "xn_in": np.asarray(H["xn2"][c]),
            "x_in": np.ascontiguousarray(np.asarray(H["xL2"][c], np.float32)),
            "temp_in": np.ascontiguousarray(np.asarray(H["weights"]["L2_attn_temperature"],
                                                       np.float32).reshape(8, 1))}


def post_l2b(results, H):
    H["xL3"] = [r["xo_out"] for r in results]


STAGE_L2B = Stage("l2b", build_l2b, prep_l2b, post_l2b)



def build_l3a(em, sc, H):
    W = H["weights"]
    x_in = em.dram("x_in", [C, TLOC], F32, kind="ExternalInput")
    ropec_in = em.dram("ropec_in", [TLOC, 1024], BF16, kind="ExternalInput")
    ropes_in = em.dram("ropes_in", [TLOC, 1024], BF16, kind="ExternalInput")
    xn_out = em.dram("xn_out", [C, TLOC], BF16, kind="ExternalOutput")
    s3_out = em.dram("s3_out", [TLOC, 1024], BF16, kind="ExternalOutput")
    vscr = em.dram("v_scr", [TLOC, 1024], BF16, kind="Internal")

    ones1 = sc.ones1()
    x_bf = load_fm_tiles(em, x_in, 8, TLOC, BF16, tag="xbf")
    xn = [em.sb(f"xn_{i}", [128, TLOC], BF16) for i in range(8)]
    ps_p = [em.psum(f"psp{i}") for i in range(2)]
    ps_s = [em.psum(f"pss{i}") for i in range(2)]
    ns = NormScratch(em, TLOC, ps_pool=[ps_p[0], ps_p[1], ps_s[0]])
    emit_norm(em, sc, ns, x_bf, xn, 8, TLOC, rms=False, tag="ln3")
    for kc in range(8):
        em.dma("sp", xn_out[kc * 128:(kc + 1) * 128, :], xn[kc][:, :],
               reads=[(xn[kc], ALL)], writes=[(xn_out, kc)], tag=f"st_xn_{kc}")

    wq_d = sc.cin("Wq_r", np.ascontiguousarray(np.asarray(W["L3_attn_Wq"], np.float32)
                  .reshape(8, 128, 1024).transpose(1, 0, 2).reshape(128, 8 * 1024)))
    wk_d = sc.cin("Wk_r", np.ascontiguousarray(np.asarray(W["L3_attn_Wk"], np.float32)
                  .reshape(8, 128, 1024).transpose(1, 0, 2).reshape(128, 8 * 1024)))
    wv_d = sc.cin("Wv_r", np.ascontiguousarray(np.asarray(W["L3_attn_Wv"], np.float32)
                  .reshape(8, 128, 1024).transpose(1, 0, 2).reshape(128, 8 * 1024)))
    bq_t = sc.const_tile("bq_t", np.asarray(W["L3_attn_bq"], np.float32)[None, :])
    bk_t = sc.const_tile("bk_t", np.asarray(W["L3_attn_bk"], np.float32)[None, :])
    bv_t = sc.const_tile("bv_t", np.asarray(W["L3_attn_bv"], np.float32)[None, :])
    wqb = em.sb("wqb", [128, 8 * 1024], BF16)
    wkb = em.sb("wkb", [128, 8 * 1024], BF16)
    wvb = em.sb("wvb", [128, 8 * 1024], BF16)
    em.dma("pool", wqb[:, :], wq_d[0:128, :], reads=[wq_d], writes=[wqb], tag="ldwq")
    em.dma("pool", wkb[:, :], wk_d[0:128, :], reads=[wk_d], writes=[wkb], tag="ldwk")
    em.dma("pool", wvb[:, :], wv_d[0:128, :], reads=[wv_d], writes=[wvb], tag="ldwv")

    ps_avq = [em.psum(f"psavq{i}") for i in range(4)]
    rc = em.sb("rc3", [128, 1024], BF16)
    rs = em.sb("rs3", [128, 1024], BF16)
    qt = em.sb("qt3", [128, 1024], BF16)
    kt = em.sb("kt3", [128, 1024], BF16)
    qv = em.sb("qv", [128, 1024], BF16)
    kv = em.sb("kv", [128, 1024], BF16)
    qw = em.sb("qw", [128, 1024], BF16)
    kw = em.sb("kw", [128, 1024], BF16)
    rtmp = em.sb("rtmp", [128, 512])
    rtm2 = em.sb("rtm2", [128, 512])
    ssb = em.sb("ssb", [128, 2048], BF16)
    pt3 = em.sb("pt3", [128, 2048], BF16)
    vhm = x_bf[0]
    avsb = x_bf[1]
    sums3 = em.sb("sums3", [128, 64])
    bcol = em.sb("bcol", [128, 4])
    qx = em.sb("qx", [32, 1024], BF16)
    kx = em.sb("kx", [32, 1024], BF16)
    em.op("dve", lambda e: e.memset(ssb[:, :], 0.0), reads=[], writes=[(ssb, "init")], tag="ssbz")
    for _b in range(2):
        em.op("dve", lambda e, _b=_b: e.memset(ps_s[_b][:, 0:512], 0.0), reads=[],
              writes=[(ps_s[_b], "init")], tag=f"psz{_b}")

    for tt in range(16):
        t0 = tt * 128
        for which, wb, bt, dst in (("q", wqb, bq_t, qt), ("k", wkb, bk_t, kt),
                                   ("v", wvb, bv_t, qw)):
            for nb in range(2):
                ps = ps_p[nb]
                def f_mm(e, tt=tt, nb=nb, ps=ps, wb=wb, bt=bt):
                    for kc in range(8):
                        e.matmul(out=ps[:, 0:512], lhsT=xn[kc][:, tt * 128:(tt + 1) * 128],
                                 rhs=wb[:, kc * 1024 + nb * 512:kc * 1024 + (nb + 1) * 512],
                                 start=(kc == 0), stop=False)
                    return e.matmul(out=ps[:, 0:512], lhsT=ones1[0:1, :],
                                    rhs=bt[0:1, nb * 512:(nb + 1) * 512], start=False, stop=True)
                em.op("pe", f_mm, reads=[(wb, ALL), bt, ones1] + [(xn[k], ALL) for k in range(8)],
                      writes=[(ps, ALL)], tag=f"p{which}_{tt}_{nb}")
                def f_ev(e, nb=nb, ps=ps, dst=dst):
                    return e.tensor_copy(out=dst[:, nb * 512:(nb + 1) * 512], in_=ps[:, 0:512])
                em.op("dve", f_ev, reads=[(ps, ALL)], writes=[(dst, nb)], tag=f"e{which}_{tt}_{nb}")
        em.dma("sp", vscr[t0:t0 + 128, :], qw[:, :], reads=[(qw, 0), (qw, 1)],
               writes=[(vscr, tt)], tag=f"stv_{tt}")
        em.dma("pool", rc[:, :], ropec_in[t0:t0 + 128, :], reads=[ropec_in], writes=[rc],
               tag=f"ldrc_{tt}")
        em.dma("pool", rs[:, :], ropes_in[t0:t0 + 128, :], reads=[ropes_in], writes=[rs],
               tag=f"ldrs_{tt}")
        for src_t, dst_t, w1, w2 in ((qt, qv, "q", 0), (kt, kv, "k", 0)):
            def f_r1(e, src_t=src_t):
                return e.tensor_mul(out=rtmp[:, :], in0=src_t[:, 0:1024:2], in1=rc[:, 0:1024:2])
            em.op("dve", f_r1, reads=[(src_t, 0), (src_t, 1), rc], writes=[rtmp], tag=f"r1{w1}_{tt}")
            def f_r2(e, src_t=src_t):
                return e.tensor_mul(out=rtm2[:, :], in0=src_t[:, 1:1024:2], in1=rs[:, 1:1024:2])
            em.op("dve", f_r2, reads=[(src_t, 0), (src_t, 1), rs], writes=[rtm2], tag=f"r2{w1}_{tt}")
            def f_r3(e, dst_t=dst_t):
                return e.tensor_sub(out=dst_t[:, 0:1024:2], in0=rtmp[:, :], in1=rtm2[:, :])
            em.op("dve", f_r3, reads=[rtmp, rtm2], writes=[(dst_t, "e")], tag=f"r3{w1}_{tt}")
            def f_r4(e, src_t=src_t):
                return e.tensor_mul(out=rtmp[:, :], in0=src_t[:, 0:1024:2], in1=rs[:, 0:1024:2])
            em.op("dve", f_r4, reads=[(src_t, 0), (src_t, 1), rs], writes=[rtmp], tag=f"r4{w1}_{tt}")
            def f_r5(e, src_t=src_t):
                return e.tensor_mul(out=rtm2[:, :], in0=src_t[:, 1:1024:2], in1=rc[:, 1:1024:2])
            em.op("dve", f_r5, reads=[(src_t, 0), (src_t, 1), rc], writes=[rtm2], tag=f"r5{w1}_{tt}")
            def f_r6(e, dst_t=dst_t):
                return e.tensor_add(out=dst_t[:, 1:1024:2], in0=rtmp[:, :], in1=rtm2[:, :])
            em.op("dve", f_r6, reads=[rtmp, rtm2], writes=[(dst_t, "o")], tag=f"r6{w1}_{tt}")
        def f_tq(e):
            return e.transpose(out=qw[:, :], in_=qv[:, :])
        em.op("dve", f_tq, reads=[(qv, "e"), (qv, "o")], writes=[(qw, ALL)], tag=f"tq_{tt}")
        def f_tk(e):
            return e.transpose(out=kw[:, :], in_=kv[:, :])
        em.op("dve", f_tk, reads=[(kv, "e"), (kv, "o")], writes=[(kw, ALL)], tag=f"tk_{tt}")
        def f_cq(e):
            return e.tensor_copy(out=qx[0:32, :], in_=qw[96:128, :])
        em.op("dve", f_cq, reads=[(qw, ALL)], writes=[(qx, ALL)], tag=f"cq_{tt}")
        def f_ck(e):
            return e.tensor_copy(out=kx[0:32, :], in_=kw[96:128, :])
        em.op("dve", f_ck, reads=[(kw, ALL)], writes=[(kx, ALL)], tag=f"ck_{tt}")
        for tp0 in range(0, 128, 8):
            def f_s(e, tp0=tp0):
                last = None
                for tp in range(tp0, tp0 + 8):
                    g2, slot = tp % 2, tp // 2
                    p, t32 = tp // 32, tp % 32
                    psx = ps_s[slot // 32]
                    qb, kb, pb = (qw, kw, 32 * p) if p < 3 else (qx, kx, 0)
                    for dh in range(2):
                        lhs = qb[pb:pb + 32, :].rearrange(
                            "p (i r) -> p i r", r=64)[:, :, 32 * dh + t32].opt()
                        rhs = kb[pb:pb + 32, :].rearrange(
                            "p (i r) -> p i r", r=64)[:, :, 32 * dh + t32].opt()
                        last = e.matmul(out=psx[64 * g2:64 * g2 + 16,
                                                (slot % 32) * 16:(slot % 32) * 16 + 16],
                                        lhsT=lhs, rhs=rhs, start=(dh == 0), stop=(dh == 1))
                return last
            em.op("pe", f_s, reads=[(qw, ALL), (kw, ALL), (qx, ALL), (kx, ALL)],
                  writes=[(ps_s[tp0 // 128], ALL), (ps_s[(tp0 + 7) // 128], ALL)],
                  tag=f"s_{tt}_{tp0}")
        def f_m1(e):
            return e.tensor_reduce(out=bcol[:, 0:1], in_=ps_s[0][:, 0:512], axis=AX.X, op=ALU.max)
        em.op("dve", f_m1, reads=[(ps_s[0], ALL)], writes=[(bcol, 0)], tag=f"m1_{tt}")
        def f_m2(e):
            return e.tensor_reduce(out=bcol[:, 1:2], in_=ps_s[1][:, 0:512], axis=AX.X, op=ALU.max)
        em.op("dve", f_m2, reads=[(ps_s[1], ALL)], writes=[(bcol, 1)], tag=f"m2_{tt}")
        def f_n1(e):
            return e.tensor_scalar(out=bcol[:, 2:3], in0=bcol[:, 0:1], scalar1=-0.125,
                                   scalar2=None, op0=ALU.mult)
        em.op("dve", f_n1, reads=[(bcol, 0)], writes=[(bcol, 2)], tag=f"n1_{tt}")
        def f_n2(e):
            return e.tensor_scalar(out=bcol[:, 3:4], in0=bcol[:, 1:2], scalar1=-0.125,
                                   scalar2=None, op0=ALU.mult)
        em.op("dve", f_n2, reads=[(bcol, 1)], writes=[(bcol, 3)], tag=f"n2_{tt}")
        for bank in range(2):
            def f_ex(e, bank=bank):
                outap = ssb[:, bank * 1024:(bank + 1) * 1024].rearrange(
                    "p (s j) -> p s j", j=32)[:, :, 0:16]
                return e.activation(out=outap, in_=ps_s[bank][:, 0:512], func=AF.Exp,
                                    bias=bcol[:, 2 + bank:3 + bank], scale=0.125)
            em.op("act", f_ex, reads=[(ps_s[bank], ALL), (bcol, 2 + bank)], writes=[(ssb, bank)],
                  tag=f"ex_{tt}_{bank}")
            def f_sm(e, bank=bank):
                inap = ssb[:, bank * 1024:(bank + 1) * 1024].rearrange(
                    "p (s j) -> p s j", j=32)[:, :, 0:16]
                return e.tensor_reduce(out=sums3[:, bank * 32:(bank + 1) * 32], in_=inap,
                                       axis=AX.X, op=ALU.add)
            em.op("dve", f_sm, reads=[(ssb, bank)], writes=[(sums3, bank)], tag=f"sm_{tt}_{bank}")
        def f_rc2(e):
            return e.reciprocal(out=sums3[:, :], in_=sums3[:, :])
        em.op("dve", f_rc2, reads=[(sums3, 0), (sums3, 1)], writes=[(sums3, ALL)],
              tag=f"rc2_{tt}")
        def f_pt(e):
            return e.transpose(out=pt3[:, :], in_=ssb[:, :])
        em.op("dve", f_pt, reads=[(ssb, 0), (ssb, 1), (ssb, "init")], writes=[(pt3, ALL)],
              tag=f"pt_{tt}")
        for bank in range(2):
            for g2 in range(2):
                em.dma("pool", vhm[64 * g2:64 * g2 + 16, :].rearrange("j (s d) -> j s d", d=64),
                       vscr[t0 + 64 * bank + g2:t0 + 64 * (bank + 1):2, :].rearrange(
                           "s (j d) -> j s d", j=16),
                       reads=[(vscr, tt)], writes=[(vhm, g2)], tag=f"ldvh_{tt}_{bank}_{g2}")
            for u0 in range(0, 64, 8):
                def f_av(e, u0=u0, bank=bank):
                    last = None
                    for tp2 in range(u0, u0 + 8):
                        g2, sl = tp2 % 2, tp2 // 2
                        slot = bank * 32 + sl
                        psq = ps_avq[sl // 8]
                        last = e.matmul(out=psq[64 * g2:64 * g2 + 16,
                                                (sl % 8) * 64:(sl % 8) * 64 + 64],
                                        lhsT=pt3[64 * g2:64 * g2 + 16, 32 * slot:32 * slot + 16],
                                        rhs=vhm[64 * g2:64 * g2 + 16, sl * 64:(sl + 1) * 64],
                                        start=True, stop=True)
                    return last
                wl = []
                for tp2 in range(u0, u0 + 8):
                    g2, sl = tp2 % 2, tp2 // 2
                    wl.append((ps_avq[sl // 8], (g2, sl % 8)))
                em.op("pe", f_av, reads=[(pt3, ALL), (vhm, 0), (vhm, 1)],
                      writes=wl, tag=f"av_{tt}_{bank}_{u0}")
            for g2 in range(2):
                for q in range(4):
                    def f_ae(e, g2=g2, q=q, bank=bank):
                        return e.tensor_mul(
                            out=avsb[64 * g2:64 * g2 + 16, q * 512:(q + 1) * 512].rearrange(
                                "p (s d) -> p s d", d=64),
                            in0=ps_avq[q][64 * g2:64 * g2 + 16, 0:512].rearrange(
                                "p (s d) -> p s d", d=64),
                            in1=sums3[64 * g2:64 * g2 + 16,
                                      bank * 32 + q * 8:bank * 32 + q * 8 + 8].unsqueeze(
                                          2).broadcast_to([16, 8, 64]))
                    em.op("dve", f_ae, reads=[(ps_avq[q], (g2, s)) for s in range(8)] +
                          [(sums3, ALL)],
                          writes=[(avsb, (g2, s)) for s in range(q * 8, q * 8 + 8)],
                          tag=f"ae_{tt}_{bank}_{g2}_{q}")
            for g2 in range(2):
                em.dma("sp", s3_out[t0 + 64 * bank + g2:t0 + 64 * (bank + 1):2, :].rearrange(
                           "s (i d) -> i s d", i=16),
                       avsb[64 * g2:64 * g2 + 16, :].rearrange("i (s d) -> i s d", d=64),
                       reads=[(avsb, (g2, s)) for s in range(32)],
                       writes=[(s3_out, (tt, bank, g2))], tag=f"sts3_{tt}_{bank}_{g2}")
    em.barrier([xn_out, s3_out])


def l3_rope_tables(c):
    half = c % 2
    pos = np.arange(half * TLOC, (half + 1) * TLOC, dtype=np.float64)
    freqs = 10000.0 ** (-(np.arange(0, 64, 2, dtype=np.float64) / 1024.0))
    ang = pos[:, None] * freqs[None, :]
    cosv, sinv = np.cos(ang), np.sin(ang)
    Ct = np.zeros((TLOC, 1024), np.float64)
    St = np.zeros((TLOC, 1024), np.float64)
    for i in range(16):
        for f in range(32):
            Ct[:, i * 64 + 2 * f] = cosv[:, f]
            Ct[:, i * 64 + 2 * f + 1] = cosv[:, f]
            St[:, i * 64 + 2 * f] = sinv[:, f]
            St[:, i * 64 + 2 * f + 1] = sinv[:, f]
    return Ct.astype(BF16NP), St.astype(BF16NP)


def prep_l3a(c, H):
    Ct, St = l3_rope_tables(c)
    return {"x_in": np.ascontiguousarray(np.asarray(H["xL3"][c], np.float32)),
            "ropec_in": Ct, "ropes_in": St}


def post_l3a(results, H):
    H["xn3"] = [r["xn_out"] for r in results]
    H["s3"] = [r["s3_out"] for r in results]


STAGE_L3A = Stage("l3a", build_l3a, prep_l3a, post_l3a)



def build_l3b(em, sc, H):
    W = H["weights"]
    xn_in = em.dram("xn_in", [1024, TLOC], BF16, kind="ExternalInput")
    s_in = em.dram("s_in", [1024, TLOC], F32, kind="ExternalInput")
    x_in = em.dram("x_in", [1024, TLOC], F32, kind="ExternalInput")
    xo_out = em.dram("xo_out", [1024, TLOC], F32, kind="ExternalOutput")
    wbc_d = em.dram("wbc_d", [128, 8 * TLOC], F32, kind="Internal")

    ones1 = sc.ones1()
    ident_f = sc.const_tile("ident_f", np.eye(128, dtype=np.float32))
    ones1b = sc.const_tile("ones1b", np.ones((1, 128), BF16NP), BF16)

    xn = load_fm_tiles(em, xn_in, 8, TLOC, BF16, tag="xn")
    rww = sc.weight_bf16("W_rtr", np.asarray(W["L3_ffn_gate_w"], np.float32))
    w1np = np.asarray(W["L3_ffn_w1"], np.float32)
    w2np = np.asarray(W["L3_ffn_w2"], np.float32)
    w1h = [[[sc.cin(f"W1r_{e}_{hh}_{qq}", np.ascontiguousarray(
            w1np[e][:, hh * 2048 + qq * 1024:hh * 2048 + (qq + 1) * 1024]
            .reshape(8, 128, 1024).transpose(1, 0, 2).reshape(128, 8 * 1024)))
            for qq in range(2)] for hh in range(2)] for e in range(8)]
    w2h = [[sc.cin(f"W2r_{e}_{hh}", np.ascontiguousarray(
            w2np[e][hh * 2048:(hh + 1) * 2048, :].reshape(16, 128, 1024)
            .transpose(1, 0, 2).reshape(128, 16 * 1024))) for hh in range(2)] for e in range(8)]

    ps_r = em.psum("ps_r")
    ps_t = em.psum("ps_t")
    ps_h = [em.psum(f"ps_h{i}") for i in range(2)]
    ps_o = [em.psum(f"ps_o{i}") for i in range(2)]
    ps_b = em.psum("ps_b")

    sco = em.sb("sco", [128, 8])
    scc = em.sb("scc", [128, 3])
    wsp = em.sb("wsp", [128, 128])
    wr2 = [em.sb(f"wr2_{i}", [1, 128], BF16) for i in range(2)]
    wbcs = [em.sb(f"wbcs{i}", [128, 128]) for i in range(2)]
    em.op("dve", lambda e: e.memset(wsp[:, :], 0.0), writes=[wsp], tag="wsp0")
    for tt in range(16):
        def f_rt(e, tt=tt):
            last = None
            for kc in range(8):
                last = e.matmul(out=ps_r[0:128, 0:8], lhsT=xn[kc][:, tt * 128:(tt + 1) * 128],
                                rhs=rww.sl(kc, 0, 8), start=(kc == 0), stop=(kc == 7))
            return last
        em.op("pe", f_rt, reads=[(rww.buf, ALL)] + [(xn[k], ALL) for k in range(8)],
              writes=[ps_r], tag=f"rt_{tt}")
        def f_re(e):
            return e.tensor_copy(out=sco[:, :], in_=ps_r[0:128, 0:8])
        em.op("dve", f_re, reads=[ps_r], writes=[sco], tag=f"re_{tt}")
        def f_mx(e):
            return e.tensor_reduce(out=scc[:, 0:1], in_=sco[:, :], axis=AX.X, op=ALU.max,
                                   negate=True)
        em.op("dve", f_mx, reads=[(sco, ALL)], writes=[(scc, 0)], tag=f"mx_{tt}")
        def f_ex(e):
            return e.activation(out=sco[:, :], in_=sco[:, :], func=AF.Exp, bias=scc[:, 0:1])
        em.op("act", f_ex, reads=[(sco, ALL), (scc, 0)], writes=[(sco, ALL)], tag=f"exr_{tt}")
        def f_m8(e):
            return e.max(out=wsp[:, 120:128], in_=sco[:, :])
        em.op("dve", f_m8, reads=[(sco, ALL)], writes=[(wsp, "m8")], tag=f"m8_{tt}")
        def f_ke(e):
            return e.tensor_scalar(out=wsp[:, 112:120], in0=sco[:, :], scalar1=wsp[:, 121:122],
                                   scalar2=None, op0=ALU.is_ge)
        em.op("dve", f_ke, reads=[(sco, ALL), (wsp, "m8")], writes=[(wsp, "k")], tag=f"kg_{tt}")
        def f_wm(e):
            return e.tensor_mul(out=wsp[:, 112:120], in0=wsp[:, 112:120], in1=sco[:, :])
        em.op("dve", f_wm, reads=[(wsp, "k"), (sco, ALL)], writes=[(wsp, "w")], tag=f"wm_{tt}")
        def f_ws(e):
            return e.tensor_reduce(out=scc[:, 1:2], in_=wsp[:, 112:120], axis=AX.X, op=ALU.add)
        em.op("dve", f_ws, reads=[(wsp, "w")], writes=[(scc, 1)], tag=f"ws_{tt}")
        def f_rc(e):
            return e.reciprocal(out=scc[:, 2:3], in_=scc[:, 1:2])
        em.op("dve", f_rc, reads=[(scc, 1)], writes=[(scc, 2)], tag=f"rc_{tt}")
        def f_pb(e):
            return e.tensor_scalar(out=wsp[:, 112:120], in0=wsp[:, 112:120], scalar1=scc[:, 2:3],
                                   scalar2=None, op0=ALU.mult)
        em.op("dve", f_pb, reads=[(wsp, "w"), (scc, 2)], writes=[(wsp, "w")], tag=f"pb_{tt}")
        for g in range(2):
            for j in range(4):
                def f_cp(e, g=g, j=j):
                    return e.tensor_copy(out=wsp[:, 32 * j:32 * j + 1],
                                         in_=wsp[:, 112 + 4 * g + j:112 + 4 * g + j + 1])
                em.op("dve", f_cp, reads=[(wsp, "w")], writes=[(wsp, ALL)], tag=f"cp_{tt}_{g}_{j}")
            def f_tp(e):
                return e.transpose(out=ps_t[:, 0:128], in_=wsp[:, 0:128], identity=ident_f[:, :])
            em.op("pe", f_tp, reads=[(wsp, ALL), ident_f], writes=[ps_t], tag=f"tp_{tt}_{g}")
            for j in range(4):
                ee = 4 * g + j
                wrb = wr2[j % 2]
                wcb = wbcs[j % 2]
                def f_ev(e, j=j, wrb=wrb):
                    return e.tensor_copy(out=wrb[0:1, :], in_=ps_t[32 * j:32 * j + 1, 0:128])
                em.op("dve", f_ev, reads=[ps_t], writes=[(wrb, ALL)], tag=f"wev_{tt}_{ee}")
                def f_wb(e, wrb=wrb):
                    return e.matmul(out=ps_b[:, 0:128], lhsT=ones1b[0:1, :],
                                    rhs=wrb[0:1, :], start=True, stop=True)
                em.op("pe", f_wb, reads=[ones1b, (wrb, ALL)], writes=[ps_b], tag=f"wb_{tt}_{ee}")
                def f_we(e, wcb=wcb):
                    return e.tensor_copy(out=wcb[:, :], in_=ps_b[:, 0:128])
                em.op("dve", f_we, reads=[ps_b], writes=[(wcb, ALL)], tag=f"we_{tt}_{ee}")
                em.dma("sp", wbc_d[:, ee * TLOC + tt * 128:ee * TLOC + (tt + 1) * 128],
                       wcb[:, :], reads=[(wcb, ALL)], writes=[(wbc_d, (ee, tt // 4))],
                       tag=f"stwb_{tt}_{ee}")

    acc = [em.sb(f"acc_{m2}", [128, TLOC]) for m2 in range(8)]
    w1q = [em.sb(f"w1q{i}", [128, 8 * 1024], BF16) for i in range(2)]
    w2q = [em.sb(f"w2q{i}", [128, 8 * 1024], BF16) for i in range(2)]
    ht = [em.sb(f"ht_{i}", [128, 512], BF16) for i in range(8)]
    wbct = [em.sb(f"wbct{i}", [128, 512]) for i in range(2)]
    gbuf = em.sb("gbuf", [128, 512], BF16)
    sbuf2 = em.sb("sbuf2", [128, 512])
    xbuf2 = em.sb("xbuf2", [128, 512])
    for m2 in range(8):
        def f_z(e, m2=m2):
            return e.memset(acc[m2][:, :], 0.0)
        em.op("dve", f_z, reads=[], writes=[(acc[m2], ALL)], tag=f"accz_{m2}")
    qcnt = 0
    for ei in range(8):
        for hh in range(2):
            for qq in range(2):
                w1t = w1q[qcnt % 2]
                w2t = w2q[qcnt % 2]
                qcnt += 1
                em.dma("pool", w1t[:, :], w1h[ei][hh][qq][0:128, :], reads=[w1h[ei][hh][qq]],
                       writes=[(w1t, ALL)], tag=f"ldw1_{ei}_{hh}_{qq}")
                em.dma("pool", w2t[:, :], w2h[ei][hh][0:128, qq * 8192:(qq + 1) * 8192],
                       reads=[w2h[ei][hh]], writes=[(w2t, ALL)], tag=f"ldw2_{ei}_{hh}_{qq}")
                for tb in range(4):
                    t0, t1 = tb * 512, (tb + 1) * 512
                    wbb = wbct[tb % 2]
                    em.dma("sp", wbb[:, :], wbc_d[:, ei * TLOC + t0:ei * TLOC + t1],
                           reads=[(wbc_d, (ei, tb))], writes=[(wbb, ALL)],
                           tag=f"ldwb_{ei}_{hh}_{qq}_{tb}")
                    for hc in range(8):
                        ph = ps_h[hc % 2]
                        def f_h1(e, hc=hc, t0=t0, t1=t1, ph=ph, w1t=w1t):
                            last = None
                            for kc in range(8):
                                last = e.matmul(out=ph[:, 0:512],
                                                lhsT=w1t[:, kc * 1024 + hc * 128:kc * 1024 + (hc + 1) * 128],
                                                rhs=xn[kc][:, t0:t1], start=(kc == 0), stop=(kc == 7))
                            return last
                        em.op("pe", f_h1, reads=[(w1t, ALL)] + [(xn[k], ALL) for k in range(8)],
                              writes=[(ph, ALL)], tag=f"h1_{ei}_{hh}_{qq}_{tb}_{hc}")
                        def f_ge(e, hc=hc, ph=ph):
                            return e.activation(out=ht[hc][:, :], in_=ph[:, 0:512], func=AF.Gelu,
                                                bias=0.0)
                        em.op("act", f_ge, reads=[(ph, ALL)], writes=[(ht[hc], "g")],
                              tag=f"ge_{ei}_{hh}_{qq}_{tb}_{hc}")
                        def f_sc2(e, hc=hc, wbb=wbb):
                            return e.tensor_mul(out=ht[hc][:, :], in0=ht[hc][:, :], in1=wbb[:, :])
                        em.op("dve", f_sc2, reads=[(ht[hc], "g"), (wbb, ALL)],
                              writes=[(ht[hc], ALL)], tag=f"hs_{ei}_{hh}_{qq}_{tb}_{hc}")
                    for m2 in range(8):
                        po = ps_o[m2 % 2]
                        def f_o(e, m2=m2, po=po, w2t=w2t):
                            last = None
                            for hc in range(8):
                                last = e.matmul(out=po[:, 0:512],
                                                lhsT=w2t[:, hc * 1024 + m2 * 128:hc * 1024 + (m2 + 1) * 128],
                                                rhs=ht[hc][:, :], start=(hc == 0), stop=(hc == 7))
                            return last
                        em.op("pe", f_o, reads=[(w2t, ALL)] + [(ht[hc], ALL) for hc in range(8)],
                              writes=[(po, ALL)], tag=f"o_{ei}_{hh}_{qq}_{tb}_{m2}")
                        def f_acc(e, m2=m2, po=po, t0=t0, t1=t1):
                            return e.tensor_add(out=acc[m2][:, t0:t1], in0=acc[m2][:, t0:t1],
                                                in1=po[:, 0:512])
                        em.op("dve", f_acc, reads=[(acc[m2], ("a", tb)), (po, ALL)],
                              writes=[(acc[m2], ("a", tb))], tag=f"acc_{ei}_{hh}_{qq}_{tb}_{m2}")
    for m2 in range(8):
        for sb_ in range(4):
            c0, c1 = sb_ * 512, (sb_ + 1) * 512
            def f_g(e, m2=m2, c0=c0, c1=c1):
                return e.activation(out=gbuf[:, :], in_=acc[m2][:, c0:c1], func=AF.Silu,
                                    bias=0.0)
            em.op("act", f_g, reads=[(acc[m2], ALL)], writes=[gbuf], tag=f"g_{m2}_{sb_}")
            em.dma("sp", sbuf2[:, :], s_in[m2 * 128:(m2 + 1) * 128, c0:c1],
                   reads=[s_in], writes=[sbuf2], tag=f"lds_{m2}_{sb_}")
            em.dma("sp", xbuf2[:, :], x_in[m2 * 128:(m2 + 1) * 128, c0:c1],
                   reads=[x_in], writes=[xbuf2], tag=f"ldx_{m2}_{sb_}")
            def f_sg(e):
                return e.tensor_mul(out=sbuf2[:, :], in0=sbuf2[:, :], in1=gbuf[:, :])
            em.op("dve", f_sg, reads=[(sbuf2, "x"), gbuf], writes=[sbuf2], tag=f"sg_{m2}_{sb_}")
            def f_xn2(e):
                return e.tensor_add(out=xbuf2[:, :], in0=xbuf2[:, :], in1=sbuf2[:, :])
            em.op("dve", f_xn2, reads=[(xbuf2, "x"), sbuf2], writes=[xbuf2], tag=f"xo_{m2}_{sb_}")
            em.dma("sp", xo_out[m2 * 128:(m2 + 1) * 128, c0:c1], xbuf2[:, :],
                   reads=[xbuf2], writes=[(xo_out, (m2, sb_))], tag=f"stxo_{m2}_{sb_}")
    em.barrier([xo_out])


def prep_l3b(c, H):
    return {"xn_in": np.ascontiguousarray(np.asarray(H["xn3"][c])),
            "s_in": np.ascontiguousarray(np.asarray(H["s3"][c], np.float32).T),
            "x_in": np.ascontiguousarray(np.asarray(H["xL3"][c], np.float32))}


def post_l3b(results, H):
    H["xOUT"] = [r["xo_out"] for r in results]


STAGE_L3B = Stage("l3b", build_l3b, prep_l3b, post_l3b)



ORDER_STAGES = None


def _stages():
    global ORDER_STAGES
    if ORDER_STAGES is None:
        ORDER_STAGES = [STAGE_L0A, STAGE_L0B, STAGE_L0C, STAGE_L0D, STAGE_L0E,
                        STAGE_L0F, STAGE_L0G, STAGE_L1A, STAGE_L1B, STAGE_L1C,
                        STAGE_L1D, STAGE_L2A, STAGE_L2B, STAGE_L3A, STAGE_L3B]
    return ORDER_STAGES


def kernel(**inputs):
    H = {"x": np.asarray(inputs["L0_attn_x"]),
         "weights": {k: np.asarray(v) for k, v in inputs.items()}}
    for st in _stages():
        st.run(H)
    out = np.zeros((4, 4096, 1024), np.float32)
    for c in range(NCORES):
        b, half = c // 2, c % 2
        out[b, half * TLOC:(half + 1) * TLOC, :] = np.asarray(H["xOUT"][c], np.float32).T
    return out
